# version 21
# baseline (speedup 1.0000x reference)
import numpy as np
import ml_dtypes
from contextlib import ExitStack

import concourse.bass as bass
import concourse.mybir as mybir
from concourse.bass_utils import run_bass_kernel_spmd
from concourse.alu_op_type import AluOpType as OP

F32 = mybir.dt.float32
BF16 = mybir.dt.bfloat16
AF = mybir.ActivationFunctionType
AX = mybir.AxisListType.X
MAGIC = float(np.float32(1.5 * 2 ** 23))

NCORES = 8
L = 2048
D = 1024
T = 512
NTILE = L // T
C = 64
NCH = T // C
INNER = 4096
EPS = 1e-5

class Step:
    __slots__ = ("eng", "fn", "reads", "writes", "dma", "idx", "inc_val",
                 "waits", "needed", "ndma", "drain_after", "gate_id",
                 "lag", "parity", "gate_for")
    def __init__(self, eng, fn, reads, writes, dma, idx, ndma, lag=False):
        self.eng = eng; self.fn = fn; self.reads = tuple(reads)
        self.writes = tuple(writes); self.dma = dma; self.idx = idx
        self.inc_val = None; self.waits = []; self.needed = False
        self.ndma = ndma; self.drain_after = False; self.gate_id = None
        self.lag = lag; self.parity = 0; self.gate_for = None

class TK:
    def __init__(self):
        self.steps = []

    def step(self, eng, fn, reads=(), writes=(), dma=False, ndma=1, lag=False):
        s = Step(eng, fn, reads, writes, dma, len(self.steps), ndma, lag)
        self.steps.append(s)
        return s

    def t(self, fn, reads=(), writes=()):  return self.step("tensor", fn, reads, writes)
    def v(self, fn, reads=(), writes=()):  return self.step("vector", fn, reads, writes)
    def s(self, fn, reads=(), writes=()):  return self.step("scalar", fn, reads, writes)
    def g(self, fn, reads=(), writes=()):  return self.step("gpsimd", fn, reads, writes)
    def dsync(self, fn, reads=(), writes=(), ndma=1, lag=False):
        s = self.step("sync", fn, reads, writes, dma=True, ndma=ndma, lag=lag)
        g = self.step("gpsimd", None)
        g.gate_for = s
        return s
    def dg(self, fn, reads=(), writes=(), ndma=1):
        s = self.step("gpsimd", fn, reads, writes, dma=True, ndma=ndma)
        g = self.step("sync", None)
        g.gate_for = s
        return s

    def finalize(self, tiny=frozenset()):
        last_writer = {}
        readers = {}
        deps_of = []
        for s in self.steps:
            deps = set()
            for r in s.reads:
                lw = last_writer.get(r)
                if lw is not None:
                    deps.add(lw)
            for w in s.writes:
                lw = last_writer.get(w)
                if lw is not None:
                    deps.add(lw)
                deps.update(readers.get(w, ()))
            for r in s.reads:
                if r.startswith("pb"):
                    for pr_ in readers.get(r, ()):
                        if self.steps[pr_].eng != s.eng:
                            deps.add(pr_)
            pruned = set()
            for d in deps:
                ds = self.steps[d]
                if not ds.dma and any(w in tiny and w in s.reads for w in ds.writes):
                    ds.drain_after = True
                if ds.eng == s.eng and not ds.dma and not s.dma:
                    continue
                pruned.add(d)
            deps_of.append(pruned)
            for d in pruned:
                self.steps[d].needed = True
            for r in s.reads:
                readers.setdefault(r, set()).add(s.idx)
            for w in s.writes:
                last_writer[w] = s.idx
                readers[w] = {s.idx}
        cnt = {}
        qord = {}
        for s in self.steps:
            if s.dma:
                s.parity = qord.get(s.eng, 0) % 2
                qord[s.eng] = qord.get(s.eng, 0) + 1
                key = f"q_{s.eng}{s.parity}"
                cnt[key] = cnt.get(key, 0) + 16 * s.ndma
                s.inc_val = cnt[key]
                s.needed = True
                gk = "g_gate" if s.eng == "gpsimd" else "s_gate"
                cnt[gk] = cnt.get(gk, 0) + 1
                s.gate_id = cnt[gk]
            elif s.needed:
                key = s.eng
                cnt[key] = cnt.get(key, 0) + 1
                s.inc_val = cnt[key]
        waited = {}
        for s, deps in zip(self.steps, deps_of):
            for d in sorted(deps):
                ds = self.steps[d]
                if ds.inc_val is None:
                    continue
                if ds.gate_id is not None:
                    key = "g_gate" if ds.eng == "gpsimd" else "s_gate"
                    val = ds.gate_id
                else:
                    key, val = (("q_" + ds.eng) if ds.dma else ds.eng), ds.inc_val
                wkey = (s.eng, key)
                if waited.get(wkey, -1) >= val:
                    continue
                waited[wkey] = val
                s.waits.append((key, val))
        self.final_counts = cnt

    def emit(self, nc, sems):
        by_eng = {}
        for s in self.steps:
            by_eng.setdefault(s.eng, []).append(s)
        with nc.Block() as block:
            def section(eng_name):
                def body(eng):
                    inflight = []
                    def flush_gate(ps):
                        gk = "g_gate" if ps.eng == "gpsimd" else "s_gate"
                        eng.wait_ge(sems[f"q_{ps.eng}{ps.parity}"], ps.inc_val)
                        eng.nop().then_inc(sems[gk], 1)
                    for s in by_eng.get(eng_name, ()):
                        if s.gate_for is not None:
                            flush_gate(s.gate_for)
                            continue
                        for key, val in s.waits:
                            eng.wait_ge(sems[key], val)
                        if s.dma:
                            gk = "g_gate" if s.eng == "gpsimd" else "s_gate"
                            while len(inflight) >= 2:
                                eng.wait_ge(sems[gk], inflight.pop(0).gate_id)
                            s.fn(eng, sems[f"q_{s.eng}{s.parity}"])
                            inflight.append(s)
                        else:
                            inst = s.fn(eng)
                            if s.drain_after:
                                eng.drain()
                                if s.inc_val is not None:
                                    eng.nop().then_inc(sems[s.eng], 1)
                            elif s.inc_val is not None:
                                inst.then_inc(sems[s.eng], 1)
                return body
            block.sync(section("sync"))
            block.tensor(section("tensor"))
            block.vector(section("vector"))
            block.scalar(section("scalar"))
            block.gpsimd(section("gpsimd"))
            @block.sync
            def _(sync):
                for q in ("q_sync0", "q_sync1", "q_gpsimd0", "q_gpsimd1"):
                    if self.final_counts.get(q):
                        sync.wait_ge(sems[q], self.final_counts[q])


def build_nc(n_seq=2, n_tiles=NTILE, do_f1=True, do_attn=True, do_f2=True, dbg=False, delta_mode="all"):
    nc = bass.Bass("TRN2", target_bir_lowering=False)
    tk = TK()

    dx = nc.dram_tensor("x", [n_seq, D, L], F32, kind="ExternalInput").ap()
    dwg = nc.dram_tensor("wgT", [D, INNER], F32, kind="ExternalInput").ap()
    dwu = nc.dram_tensor("wuT", [D, INNER], F32, kind="ExternalInput").ap()
    dwd = nc.dram_tensor("wdT", [INNER, D], F32, kind="ExternalInput").ap()
    dW = nc.dram_tensor("WT", [D, D], F32, kind="ExternalInput").ap()
    dgdl = nc.dram_tensor("gdl", [128, 512], F32, kind="ExternalInput").ap()
    dbdl = nc.dram_tensor("bdl", [128, 512], F32, kind="ExternalInput").ap()
    deta = nc.dram_tensor("eta1", [1, 1], F32, kind="ExternalInput").ap()
    dlamz = nc.dram_tensor("lamz", [1, 1], F32, kind="ExternalInput").ap()
    dI128 = nc.dram_tensor("I128", [128, 128], F32, kind="ExternalInput").ap()
    dONESROW = nc.dram_tensor("ONESROW", [1, 128], F32, kind="ExternalInput").ap()
    dONE1 = nc.dram_tensor("ONE1", [1, 1], F32, kind="ExternalInput").ap()
    dONESCOL = nc.dram_tensor("ONESCOL", [128, 1], F32, kind="ExternalInput").ap()
    dSELDUP = nc.dram_tensor("SELDUP", [128, 128], F32, kind="ExternalInput").ap()
    dDL = nc.dram_tensor("DLm", [128, 512], F32, kind="ExternalInput").ap()
    dDU = nc.dram_tensor("DUm", [128, 512], F32, kind="ExternalInput").ap()
    dDQ = nc.dram_tensor("DQm", [128, 512], F32, kind="ExternalInput").ap()
    dML = nc.dram_tensor("MLm", [128, 512], F32, kind="ExternalInput").ap()
    dMU = nc.dram_tensor("MUm", [128, 512], F32, kind="ExternalInput").ap()
    dMQ = nc.dram_tensor("MQm", [128, 512], F32, kind="ExternalInput").ap()
    dIOT = nc.dram_tensor("IOT", [128, 1], F32, kind="ExternalInput").ap()
    dREV = nc.dram_tensor("REV", [128, 1], F32, kind="ExternalInput").ap()
    dCC = nc.dram_tensor("CC", [128, 1], F32, kind="ExternalInput").ap()
    dout = nc.dram_tensor("out", [n_seq, D, L], F32, kind="ExternalOutput").ap()
    sg = nc.dram_tensor("scr_g", [8, 128, 4096], BF16, kind="Internal").ap()
    su = nc.dram_tensor("scr_u", [8, 128, 4096], BF16, kind="Internal").ap()
    sd = nc.dram_tensor("scr_d", [8, 128, 4096], BF16, kind="Internal").ap()
    sW = nc.dram_tensor("scr_W", [8, 128, 1024], BF16, kind="Internal").ap()
    dIDP = nc.dram_tensor("IDP", [128, 512], F32, kind="ExternalInput").ap()
    ddbg = nc.dram_tensor("dbg", [40, 128, 512], F32, kind="ExternalOutput").ap() if dbg else None

    st = ExitStack()
    e = st.enter_context
    with st:
        xr = e(nc.sbuf_tensor("xr", [128, 4096], F32))
        gu = e(nc.sbuf_tensor("gu", [128, 16384], F32))
        qx = e(nc.sbuf_tensor("qxa", [128, 4096], BF16))
        xa = qx
        stat = e(nc.sbuf_tensor("stat", [128, 4096], F32))
        wg0 = e(nc.sbuf_tensor("wg0", [128, 4096], BF16))
        wg1 = e(nc.sbuf_tensor("wg1", [128, 4096], BF16))
        wu0 = e(nc.sbuf_tensor("wu0", [128, 4096], BF16))
        wu1 = e(nc.sbuf_tensor("wu1", [128, 4096], BF16))
        gsil = e(nc.sbuf_tensor("gsil", [128, 512], F32))
        qgu = [e(nc.sbuf_tensor(f"qgu{i}", [128, 512], BF16)) for i in range(2)]
        m2acc = e(nc.sbuf_tensor("m2acc", [128, 512], F32))
        sqb = e(nc.sbuf_tensor("sqb", [128, 512], F32))
        tA = e(nc.sbuf_tensor("tA", [128, 512], F32))
        tB = e(nc.sbuf_tensor("tB", [128, 512], F32))
        mxacc = tA
        mnacc = tB
        s2bc = e(nc.sbuf_tensor("s2bc", [128, 512], F32))
        is2bc = e(nc.sbuf_tensor("is2bc", [128, 512], F32))
        P_LOW = e(nc.sbuf_tensor("P_LOW", [128, 512], BF16))
        P_UP = e(nc.sbuf_tensor("P_UP", [128, 512], BF16))
        P_UQI = e(nc.sbuf_tensor("P_UQI", [128, 512], BF16))
        Nb = [e(nc.sbuf_tensor(f"Nb{i}", [128, 512], BF16)) for i in range(2)]
        NTw = [e(nc.sbuf_tensor(f"NTw{i}", [128, 512], BF16)) for i in range(2)]
        NT1 = e(nc.sbuf_tensor("NT1", [128, 512], BF16))
        FTS = [e(nc.sbuf_tensor(f"FTS{i}", [128, 512], BF16)) for i in range(5)]
        FT1n = e(nc.sbuf_tensor("FT1n", [128, 512], BF16))
        identp = e(nc.sbuf_tensor("identp", [128, 512], BF16))
        Mqb = [e(nc.sbuf_tensor(f"Mqb{i}", [128, 512], BF16)) for i in range(2)]
        X0 = [e(nc.sbuf_tensor(f"X0_{i}", [128, 1024], BF16)) for i in range(2)]
        Xw = [e(nc.sbuf_tensor(f"Xw{i}", [128, 1024], BF16)) for i in range(2)]
        Bcb = [e(nc.sbuf_tensor(f"Bcb{i}", [128, 512], BF16)) for i in range(2)]
        ULc = [e(nc.sbuf_tensor(f"ULc{i}", [128, 512], BF16)) for i in range(2)]
        BBTn = [e(nc.sbuf_tensor(f"BBTn{i}", [128, 512], BF16)) for i in range(2)]
        KTb = [e(nc.sbuf_tensor(f"KTb{i}", [128, 512], BF16)) for i in range(2)]
        Ub = e(nc.sbuf_tensor("Ub", [128, 512], BF16))
        Sbf = e(nc.sbuf_tensor("Sbf", [128, 512], BF16))
        ypart = e(nc.sbuf_tensor("ypart", [128, 512], F32))
        yv = e(nc.sbuf_tensor("yv", [128, 512], F32))
        ao = yv
        I128f = e(nc.sbuf_tensor("I128f", [128, 128], F32))
        I128b = e(nc.sbuf_tensor("I128b", [128, 128], BF16))
        lCI = e(nc.sbuf_tensor("lCI", [128, 128], BF16))
        onesrow = e(nc.sbuf_tensor("onesrow", [1, 128], F32))
        one1 = e(nc.sbuf_tensor("one1", [1, 1], F32))
        onescol = e(nc.sbuf_tensor("onescol", [128, 1], F32))
        seldup = e(nc.sbuf_tensor("seldup", [128, 128], F32))
        gdl = e(nc.sbuf_tensor("gdl_s", [128, 512], BF16))
        bdl = e(nc.sbuf_tensor("bdl_s", [128, 512], BF16))
        def col(name):
            return e(nc.sbuf_tensor(name, [128, 1], F32))
        zcol = col("zcol"); etacol = col("etacol"); loglam = col("loglam")
        betacol = col("betacol")
        lamcol = col("lamcol"); nlamcol = col("nlamcol"); lrevb = col("lrevb")
        lCcol = col("lCcol"); tc0 = col("tc0"); tc1 = col("tc1")
        swcol = col("swcol"); dqcol = col("dqcol")
        muc = col("muc"); invc = col("invc"); magcol = col("magcol")
        mxc = e(nc.sbuf_tensor("mxc", [128, 4], F32))
        mnc = e(nc.sbuf_tensor("mnc", [128, 4], F32))
        cA = e(nc.sbuf_tensor("cA", [128, 4], F32))
        cB = e(nc.sbuf_tensor("cB", [128, 4], F32))
        cC = e(nc.sbuf_tensor("cC", [128, 4], F32))
        cD = e(nc.sbuf_tensor("cD", [128, 4], F32))
        cE = e(nc.sbuf_tensor("cE", [128, 4], F32))
        muc4 = e(nc.sbuf_tensor("muc4", [128, 4], F32))
        ivc4 = e(nc.sbuf_tensor("ivc4", [128, 4], F32))
        def row(name):
            return e(nc.sbuf_tensor(name, [1, 512], F32))
        murow = row("murow")
        invrow = row("invrow"); r0 = row("r0"); r1 = row("r1"); r2 = row("r2")
        varrow = r1
        msrow = r2
        escal = e(nc.sbuf_tensor("escal", [1, 1], F32))
        zscal = e(nc.sbuf_tensor("zscal", [1, 1], F32))
        acc_col = col("acc_col"); red_col = col("red_col")
        pb = [e(nc.psum_tensor(f"pb{i}", [128, 512], F32)) for i in range(8)]
        PB = [f"pb{i}" for i in range(8)]

        sems = {}
        for k in ("tensor", "vector", "scalar", "gpsimd", "sync",
                  "q_sync0", "q_sync1", "q_gpsimd0", "q_gpsimd1",
                  "g_gate", "s_gate"):
            sems[k] = e(nc.semaphore(k + "_sem"))

        class _WS:
            def __getitem__(self, idx):
                if idx == slice(None):
                    return gu[:, 0:4096]
                return gu.__getitem__(idx)
        wstage = _WS()
        wq_stage = qx
        xr3 = xr[:].rearrange("p (a b) -> p a b", b=512)
        gu3 = gu[:].rearrange("p (a b) -> p a b", b=512)
        qx3 = qx[:].rearrange("p (a b) -> p a b", b=512)
        xa3 = xa[:].rearrange("p (a b) -> p a b", b=512)
        st3 = stat[:].rearrange("p (a b) -> p a b", b=512)

        def vrecip(v, out, in_):
            v.drain()
            i = v.reciprocal(out, in_)
            v.drain()
            return i

        def vred(v, out, in_, op, absv=None):
            i = v.tensor_reduce(out, in_, AX, op, apply_absolute_value=absv)
            v.drain()
            return i

        dbg_i = [0]
        def dump(ap, res):
            if not dbg or dbg_i[0] >= 40:
                return
            slot = dbg_i[0]; dbg_i[0] += 1
            sh = ap.shape if hasattr(ap, "shape") else None
            tk.dg(lambda g, sem, ap=ap, slot=slot: g.dma_start(
                ddbg[slot][0:ap.shape[0], 0:ap.shape[1]], ap,
                allow_slow_non_contiguous=True).then_inc(sem, 16),
                reads=[res])

        def ld(dst, src, rs=(), ws=()):
            tk.dsync(lambda g, sem, d=dst, s=src: g.dma_start(d, s).then_inc(sem, 16),
                     reads=rs, writes=ws)

        def ldg(dst, src, rs=(), ws=()):
            tk.dg(lambda g, sem, d=dst, s=src: g.dma_start(d, s).then_inc(sem, 16),
                  reads=rs, writes=ws)

        ld(I128f[:], dI128, ws=["I128f"])
        ld(onesrow[:], dONESROW, ws=["onesrow"])
        ld(one1[:], dONE1, ws=["one1"])
        ld(onescol[:], dONESCOL, ws=["onescol"])
        ld(seldup[:], dSELDUP, ws=["seldup"])
        ldg(gdl[:], dgdl, ws=["gdl"])
        ldg(bdl[:], dbdl, ws=["bdl"])
        ld(escal[:], deta, ws=["escal"])
        ld(zscal[:], dlamz, ws=["zscal"])
        tk.v(lambda v: v.tensor_copy(I128b[:], I128f[:]), reads=["I128f"], writes=["I128b"])
        tk.v(lambda v: v.memset(magcol[:], MAGIC), reads=["magcol"], writes=["magcol"])

        tk.t(lambda t: t.matmul(pb[0][:, 0:1], onesrow[:], escal[:], start=True, stop=True),
             reads=["onesrow", "escal"], writes=[PB[0]])
        tk.v(lambda v: v.tensor_copy(etacol[:], pb[0][:, 0:1]), reads=[PB[0]], writes=["etacol"])
        tk.t(lambda t: t.matmul(pb[0][:, 0:1], onesrow[:], zscal[:], start=True, stop=True),
             reads=["onesrow", "zscal", "etacol"], writes=[PB[0]])
        tk.v(lambda v: v.tensor_copy(zcol[:], pb[0][:, 0:1]), reads=[PB[0]], writes=["zcol"])
        tk.s(lambda s: s.activation(tc0[:], zcol[:], AF.Sigmoid), reads=["zcol"], writes=["tc0"])
        tk.s(lambda s: s.activation(loglam[:], tc0[:], AF.Ln), reads=["tc0"], writes=["loglam"])
        tk.v(lambda v: v.tensor_scalar(tc1[:], tc0[:], -1.0, 1.0, OP.mult, OP.add),
             reads=["tc0"], writes=["tc1"])
        tk.v(lambda v: v.tensor_tensor(tc1[:], tc1[:], etacol[:], OP.mult),
             reads=["tc1", "etacol"], writes=["tc1"])
        tk.v(lambda v: v.tensor_scalar(betacol[:], tc1[:], 1.0 / 64, None, OP.mult),
             reads=["tc1"], writes=["betacol"])
        ld(tc0[:], dIOT, rs=["tc0"], ws=["tc0"])
        tk.v(lambda v: v.tensor_tensor(tc0[:], tc0[:], loglam[:], OP.mult),
             reads=["tc0", "loglam"], writes=["tc0"])
        tk.s(lambda s: s.activation(lamcol[:], tc0[:], AF.Exp), reads=["tc0"], writes=["lamcol"])
        tk.v(lambda v: v.tensor_scalar(nlamcol[:], lamcol[:], -1.0, None, OP.mult),
             reads=["lamcol"], writes=["nlamcol"])
        ld(tc0[:], dREV, rs=["tc0"], ws=["tc0"])
        tk.v(lambda v: v.tensor_tensor(tc0[:], tc0[:], loglam[:], OP.mult),
             reads=["tc0", "loglam"], writes=["tc0"])
        tk.s(lambda s: s.activation(tc1[:], tc0[:], AF.Exp), reads=["tc0"], writes=["tc1"])
        tk.v(lambda v: v.tensor_tensor(lrevb[:], tc1[:], betacol[:], OP.mult),
             reads=["tc1", "betacol"], writes=["lrevb"])
        ld(tc0[:], dCC, rs=["tc0"], ws=["tc0"])
        tk.v(lambda v: v.tensor_tensor(tc0[:], tc0[:], loglam[:], OP.mult),
             reads=["tc0", "loglam"], writes=["tc0"])
        tk.s(lambda s: s.activation(lCcol[:], tc0[:], AF.Exp), reads=["tc0"], writes=["lCcol"])
        tk.v(lambda v: v.tensor_scalar(lCI[:], I128f[:], lCcol[:], None, OP.mult),
             reads=["I128f", "lCcol"], writes=["lCI"])
        for Dm, Mm, dst, dstn in [(dDL, dML, P_LOW, "P_LOW"), (dDU, dMU, P_UP, "P_UP"),
                                  (dDQ, dMQ, P_UQI, "P_UQI")]:
            ld(tA[:], Dm, rs=["tA"], ws=["tA"])
            ld(tB[:], Mm, rs=["tB"], ws=["tB"])
            tk.v(lambda v: v.tensor_scalar(tA[:], tA[:], loglam[:], None, OP.mult),
                 reads=["tA", "loglam"], writes=["tA"])
            tk.s(lambda s: s.activation(tA[:], tA[:], AF.Exp), reads=["tA"], writes=["tA"])
            tk.v(lambda v, d=dst: v.scalar_tensor_tensor(d[:], tA[:], betacol[:], tB[:],
                                                         OP.mult, OP.mult),
                 reads=["tA", "betacol", "tB", dstn], writes=[dstn])

        ldg(identp[:], dIDP, ws=["identp"])
        for kd in range(8):
            tk.dg(lambda g, sem, kd=kd: g.dma_start(
                wg0[:, 0:1024], dW[128 * kd:128 * kd + 128, :]).then_inc(sem, 16),
                  reads=["wg0"], writes=["wg0"])
            tk.dg(lambda g, sem, kd=kd: g.dma_start(
                sW[kd], wg0[:, 0:1024]).then_inc(sem, 16),
                  reads=["wg0"], writes=["scr_W"])

        def prep_scale(extra):
            tk.v(lambda v: v.tensor_scalar(escal[:], acc_col[0:1, 0:1], 1.0 / (D * INNER), 1e-5,
                                           OP.mult, OP.max), reads=["acc_col", "escal"], writes=["escal"])
            tk.v(lambda v: vrecip(v, zscal[:], escal[:]), reads=["zscal", "escal"], writes=["zscal"])
            tk.t(lambda t: t.matmul(pb[0][:, 0:1], onesrow[:], zscal[:], start=True, stop=True),
                 reads=["onesrow", "zscal"], writes=[PB[0]])
            tk.v(lambda v: v.tensor_copy(swcol[:], pb[0][:, 0:1]), reads=[PB[0]], writes=["swcol"])
            tk.t(lambda t: t.matmul(pb[0][:, 0:1], onesrow[:], escal[:], start=True, stop=True),
                 reads=["onesrow", "escal", "swcol"], writes=[PB[0]])
            tk.v(lambda v, x=extra: v.tensor_scalar(dqcol[:], pb[0][:, 0:1], x, None, OP.mult),
                 reads=[PB[0]], writes=["dqcol"])

        wsl = [gu[:, 4096 * i:4096 * i + 4096] for i in range(4)]
        wsn = ["guA", "guB", "guC", "guD"]
        wctr = [0]

        def prep_reduce(dsrc, rowblock):
            for pg in range(8):
                pp = wctr[0] % 4; wctr[0] += 1
                if rowblock:
                    tk.dsync(lambda g, sem, pg=pg, pp=pp: g.dma_start(
                        wsl[pp].rearrange("p (a b) -> p a b", b=1024),
                        dsrc[512 * pg:512 * pg + 512, :].rearrange("(a p) b -> p a b", p=128)
                        ).then_inc(sem, 16),
                          reads=[wsn[pp]], writes=[wsn[pp]])
                else:
                    tk.dsync(lambda g, sem, pg=pg, pp=pp: g.dma_start(
                        wsl[pp], dsrc[128 * pg:128 * pg + 128, :]).then_inc(sem, 16),
                          reads=[wsn[pp]], writes=[wsn[pp]])
                tk.s(lambda s, pp=pp: s.activation(wsl[pp], wsl[pp], AF.Abs),
                     reads=[wsn[pp]], writes=[wsn[pp]])
                def f(t, pg=pg, pp=pp):
                    last = None
                    w3 = wsl[pp].rearrange("p (a b) -> p a b", b=512)
                    for q8 in range(8):
                        last = t.matmul(pb[1][0:1, :], onescol[:], w3[:, q8, :],
                                        start=(pg == 0 and q8 == 0),
                                        stop=(pg == 7 and q8 == 7))
                    return last
                tk.t(f, reads=[wsn[pp], "onescol"] + ([] if pg == 0 else [PB[1]]),
                     writes=[PB[1]])
            tk.v(lambda v: vred(v, acc_col[0:1, :], pb[1][0:1, :], OP.add),
                 reads=[PB[1], "acc_col"], writes=["acc_col"])

        def prep_quant(dsrc, dscr, scrn, rowblock):
            for pg in range(8):
                pp = wctr[0] % 4; wctr[0] += 1
                if rowblock:
                    tk.dsync(lambda g, sem, pg=pg, pp=pp: g.dma_start(
                        wsl[pp].rearrange("p (a b) -> p a b", b=1024),
                        dsrc[512 * pg:512 * pg + 512, :].rearrange("(a p) b -> p a b", p=128)
                        ).then_inc(sem, 16),
                          reads=[wsn[pp]], writes=[wsn[pp]])
                else:
                    tk.dsync(lambda g, sem, pg=pg, pp=pp: g.dma_start(
                        wsl[pp], dsrc[128 * pg:128 * pg + 128, :]).then_inc(sem, 16),
                          reads=[wsn[pp]], writes=[wsn[pp]])
                tk.s(lambda s, pp=pp: s.activation(wsl[pp], wsl[pp], AF.Identity,
                                                   scale=swcol[:], bias=magcol[:]),
                     reads=[wsn[pp], "swcol", "magcol"], writes=[wsn[pp]])
                tk.v(lambda v, pp=pp: v.tensor_scalar(wsl[pp], wsl[pp], MAGIC, 1.0,
                                               OP.subtract, OP.min),
                     reads=[wsn[pp]], writes=[wsn[pp]])
                tk.v(lambda v, pp=pp: v.tensor_scalar(wsl[pp], wsl[pp], -1.0, dqcol[:],
                                               OP.max, OP.mult),
                     reads=[wsn[pp], "dqcol"], writes=[wsn[pp]])
                tk.dg(lambda g, sem, pg=pg, pp=pp: g.dma_start(
                    dscr[pg], wsl[pp]).then_inc(sem, 16),
                    reads=[wsn[pp]], writes=[scrn])

        def prep_gu(dsrc, dscr, scrn, dbg_this=False):
            prep_reduce(dsrc, False)
            prep_scale(1.0)
            prep_quant(dsrc, dscr, scrn, False)

        def prep_all():
            prep_gu(dwg, sg, "scr_g")
            prep_gu(dwu, su, "scr_u")
            prep_reduce(dwd, True)
            prep_scale(0.5)
            prep_quant(dwd, sd, "scr_d", True)
            tk.v(lambda v: v.tensor_copy(gu[:, 0:1], gu[:, 0:1]),
                 reads=["guA", "guB", "guC", "guD", "gu"],
                 writes=["guA", "guB", "guC", "guD", "gu"])

        def ln_stats():
            for j in range(8):
                tk.t(lambda t, j=j: t.matmul(pb[6][0:1, :], onescol[:], xr3[:, j, :],
                                             start=(j == 0), stop=(j == 7)),
                     reads=[f"xr{j}", "onescol"] + ([] if j else [PB[6]]), writes=[PB[6]])
            for j in range(8):
                tk.s(lambda s, j=j: s.activation(sqb[:], xr3[:, j, :], AF.Square),
                     reads=[f"xr{j}", "sqb"], writes=["sqb"])
                tk.t(lambda t, j=j: t.matmul(pb[7][0:1, :], onescol[:], sqb[:],
                                             start=(j == 0), stop=(j == 7)),
                     reads=["sqb", "onescol"], writes=[PB[7]])
            tk.v(lambda v: v.tensor_scalar(murow[:], pb[6][0:1, :], 1.0 / D, None, OP.mult),
                 reads=[PB[6]], writes=["murow"])
            tk.v(lambda v: v.tensor_scalar(msrow[:], pb[7][0:1, :], 1.0 / D, None, OP.mult),
                 reads=[PB[7], "r2"], writes=["r2"])
            tk.v(lambda v: v.tensor_tensor(varrow[:], murow[:], murow[:], OP.mult),
                 reads=["murow", "r1"], writes=["r1"])
            tk.v(lambda v: v.tensor_tensor(varrow[:], msrow[:], varrow[:], OP.subtract),
                 reads=["r2", "r1"], writes=["r1"])
            tk.v(lambda v: v.tensor_scalar(varrow[:], varrow[:], EPS, None, OP.add),
                 reads=["r1"], writes=["r1"])
            tk.v(lambda v: vrecip(v, r0[:], varrow[:]), reads=["r1", "r0"], writes=["r0"])
            tk.s(lambda s: s.activation(invrow[:], r0[:], AF.Sqrt), reads=["r0"], writes=["invrow"])

        def bcast_row(src_row, srcn, bank):
            def f(t, src_row=src_row, bank=bank):
                last = None
                for bq in range(4):
                    last = t.matmul(pb[bank][:, 128 * bq:128 * bq + 128], onesrow[:],
                                    src_row[0:1, 128 * bq:128 * bq + 128],
                                    start=True, stop=True)
                return last
            tk.t(f, reads=[srcn, "onesrow"], writes=[PB[bank]])

        def quant_rows():
            tk.v(lambda v: v.tensor_tensor(mxacc[:], xr3[:, 0, :], xr3[:, 1, :], OP.max),
                 reads=["xr0", "xr1", "tA"], writes=["tA"])
            tk.v(lambda v: v.tensor_tensor(mnacc[:], xr3[:, 0, :], xr3[:, 1, :], OP.min),
                 reads=["xr0", "xr1", "tB"], writes=["tB"])
            tk.v(lambda v: v.tensor_tensor(sqb[:], xr3[:, 2, :], xr3[:, 3, :], OP.max),
                 reads=["xr2", "xr3", "sqb"], writes=["sqb"])
            tk.v(lambda v: v.tensor_tensor(mxacc[:], mxacc[:], sqb[:], OP.max),
                 reads=["tA", "sqb"], writes=["tA"])
            tk.v(lambda v: v.tensor_tensor(sqb[:], xr3[:, 2, :], xr3[:, 3, :], OP.min),
                 reads=["xr2", "xr3", "sqb"], writes=["sqb"])
            tk.v(lambda v: v.tensor_tensor(mnacc[:], mnacc[:], sqb[:], OP.min),
                 reads=["tB", "sqb"], writes=["tB"])
            tk.v(lambda v: v.tensor_tensor(sqb[:], xr3[:, 4, :], xr3[:, 5, :], OP.max),
                 reads=["xr4", "xr5", "sqb"], writes=["sqb"])
            tk.v(lambda v: v.tensor_tensor(mxacc[:], mxacc[:], sqb[:], OP.max),
                 reads=["tA", "sqb"], writes=["tA"])
            tk.v(lambda v: v.tensor_tensor(sqb[:], xr3[:, 4, :], xr3[:, 5, :], OP.min),
                 reads=["xr4", "xr5", "sqb"], writes=["sqb"])
            tk.v(lambda v: v.tensor_tensor(mnacc[:], mnacc[:], sqb[:], OP.min),
                 reads=["tB", "sqb"], writes=["tB"])
            tk.v(lambda v: v.tensor_tensor(sqb[:], xr3[:, 6, :], xr3[:, 7, :], OP.max),
                 reads=["xr6", "xr7", "sqb"], writes=["sqb"])
            tk.v(lambda v: v.tensor_tensor(mxacc[:], mxacc[:], sqb[:], OP.max),
                 reads=["tA", "sqb"], writes=["tA"])
            tk.v(lambda v: v.tensor_tensor(sqb[:], xr3[:, 6, :], xr3[:, 7, :], OP.min),
                 reads=["xr6", "xr7", "sqb"], writes=["sqb"])
            tk.v(lambda v: v.tensor_tensor(mnacc[:], mnacc[:], sqb[:], OP.min),
                 reads=["tB", "sqb"], writes=["tB"])
            def trx(t):
                last = None
                for bq in range(4):
                    last = t.matmul(pb[4][:, 128 * bq:128 * bq + 128],
                                    mxacc[:, 128 * bq:128 * bq + 128], I128f[:],
                                    start=True, stop=True)
                for bq in range(4):
                    last = t.matmul(pb[5][:, 128 * bq:128 * bq + 128],
                                    mnacc[:, 128 * bq:128 * bq + 128], I128f[:],
                                    start=True, stop=True)
                return last
            tk.t(trx, reads=["tA", "tB", "I128f"], writes=[PB[4], PB[5]])
            for bq in range(4):
                tk.v(lambda v, bq=bq: vred(v, mxc[:, bq:bq + 1],
                                                      pb[4][:, 128 * bq:128 * bq + 128], OP.max),
                     reads=[PB[4], "mxc"], writes=["mxc"])
                tk.v(lambda v, bq=bq: vred(v, mnc[:, bq:bq + 1],
                                                      pb[5][:, 128 * bq:128 * bq + 128], OP.min),
                     reads=[PB[5], "mnc"], writes=["mnc"])
            def trmu(t):
                last = None
                for bq in range(4):
                    last = t.matmul(pb[6][:, 4 * bq:4 * bq + 1],
                                    murow[0:1, 128 * bq:128 * bq + 128],
                                    one1[:], start=True, stop=True)
                for bq in range(4):
                    last = t.matmul(pb[7][:, 4 * bq:4 * bq + 1],
                                    invrow[0:1, 128 * bq:128 * bq + 128],
                                    one1[:], start=True, stop=True)
                return last
            tk.t(trmu, reads=["murow", "invrow", "one1"], writes=[PB[6], PB[7]])
            if dbg:
                tk.v(lambda v: v.tensor_copy(yv[:, 0:16], pb[6][:, 0:16]),
                     reads=[PB[6], "yv"], writes=["yv"])
                dump(yv[:, 0:16], "yv")
                tk.v(lambda v: v.tensor_copy(yv[:, 0:16], pb[7][:, 0:16]),
                     reads=[PB[7], "yv"], writes=["yv"])
                dump(yv[:, 0:16], "yv")
            for bq in range(4):
                tk.v(lambda v, bq=bq: v.tensor_copy(muc4[:, bq:bq + 1], pb[6][:, 4 * bq:4 * bq + 1]),
                     reads=[PB[6], "muc4"], writes=["muc4"])
                tk.v(lambda v, bq=bq: v.tensor_copy(ivc4[:, bq:bq + 1], pb[7][:, 4 * bq:4 * bq + 1]),
                     reads=[PB[7], "ivc4"], writes=["ivc4"])
            mu4 = muc4[:]
            iv4 = ivc4[:]
            tk.v(lambda v: v.tensor_tensor(cA[:], mxc[:], mu4, OP.subtract),
                 reads=["mxc", "muc4", "cA"], writes=["cA"])
            tk.v(lambda v: v.tensor_tensor(cB[:], mu4, mnc[:], OP.subtract),
                 reads=["mnc", "muc4", "cB"], writes=["cB"])
            tk.v(lambda v: v.tensor_tensor(cA[:], cA[:], cB[:], OP.max),
                 reads=["cA", "cB"], writes=["cA"])
            tk.v(lambda v: v.tensor_tensor(cA[:], cA[:], iv4, OP.mult),
                 reads=["cA", "ivc4"], writes=["cA"])
            tk.v(lambda v: v.tensor_scalar(cA[:], cA[:], 1e-5, None, OP.max),
                 reads=["cA"], writes=["cA"])
            tk.v(lambda v: vrecip(v, cB[:], cA[:]), reads=["cA", "cB"], writes=["cB"])
            tk.v(lambda v: v.tensor_scalar(cB[:], cB[:], 127.0, None, OP.mult),
                 reads=["cB"], writes=["cB"])
            tk.v(lambda v: v.tensor_scalar(cC[:], cA[:], 1.0 / 127, None, OP.mult),
                 reads=["cA", "cC"], writes=["cC"])
            tk.v(lambda v: v.tensor_tensor(cD[:], iv4, cB[:], OP.mult),
                 reads=["ivc4", "cB", "cD"], writes=["cD"])
            tk.v(lambda v: v.tensor_tensor(cE[:], mu4, cD[:], OP.mult),
                 reads=["muc4", "cD", "cE"], writes=["cE"])
            def c2r(t):
                last = None
                for bq in range(4):
                    last = t.matmul(pb[4][0:1, 128 * bq:128 * bq + 128], cD[:, bq:bq + 1],
                                    I128f[:], start=True, stop=True)
                for bq in range(4):
                    last = t.matmul(pb[5][0:1, 128 * bq:128 * bq + 128], cE[:, bq:bq + 1],
                                    I128f[:], start=True, stop=True)
                for bq in range(4):
                    last = t.matmul(pb[6][0:1, 128 * bq:128 * bq + 128], cC[:, bq:bq + 1],
                                    I128f[:], start=True, stop=True)
                return last
            tk.t(c2r, reads=["cC", "cD", "cE", "I128f"], writes=[PB[4], PB[5], PB[6]])
            tk.v(lambda v: v.tensor_copy(r0[:], pb[4][0:1, :]), reads=[PB[4], "r0"], writes=["r0"])
            tk.v(lambda v: v.tensor_copy(r1[:], pb[5][0:1, :]), reads=[PB[5], "r1"], writes=["r1"])
            tk.v(lambda v: v.tensor_copy(r2[:], pb[6][0:1, :]), reads=[PB[6], "r2"], writes=["r2"])
            dump(mxc[:], "mxc"); dump(mnc[:], "mnc")
            dump(cA[:], "cA"); dump(cB[:], "cB"); dump(cC[:], "cC")
            dump(cD[:], "cD"); dump(cE[:], "cE")

        def wdma(buf, bufn, scr, grp):
            scrn = "scr_g" if scr is sg else "scr_u"
            def f(g, sem, buf=buf, scr=scr, grp=grp):
                b3 = buf[:].rearrange("p (a b) -> p a b", b=512)
                for kf in range(8):
                    g.dma_start(b3[:, kf, :], scr[kf][:, 512 * grp:512 * grp + 512]).then_inc(sem, 16)
            tk.dsync(f, reads=[bufn, scrn], writes=[bufn], ndma=8, lag=True)

        def ddma(buf, bufn, jg):
            def f(g, sem, buf=buf, jg=jg):
                g.dma_start(buf[:], sd[jg]).then_inc(sem, 16)
            tk.dsync(f, reads=[bufn, "scr_d"], writes=[bufn], ndma=1, lag=True)

        def ffn_front():
            ln_stats()
            quant_rows()
            bcast_row(r0, "r0", 2)
            bcast_row(r1, "r1", 3)
            bcast_row(r2, "r2", 4)
            for j in range(8):
                tk.v(lambda v, j=j: v.tensor_tensor(tA[:], xr3[:, j, :], pb[2][:, :], OP.mult),
                     reads=[f"xr{j}", PB[2], "tA"], writes=["tA"])
                tk.v(lambda v: v.tensor_tensor(tA[:], tA[:], pb[3][:, :], OP.subtract),
                     reads=["tA", PB[3]], writes=["tA"])
                tk.v(lambda v: v.tensor_scalar(tA[:], tA[:], MAGIC, MAGIC, OP.add, OP.subtract),
                     reads=["tA"], writes=["tA"])
                tk.v(lambda v, j=j: v.tensor_tensor(qx3[:, j, :], tA[:], pb[4][:, :], OP.mult),
                     reads=["tA", PB[4], f"qx{j}"], writes=[f"qx{j}"])

        def ffn(skip_front=False):
            if not skip_front:
                ffn_front()
            wgb = [wg0, wg1]; wub = [wu0, wu1]
            wgn = ["wg0", "wg1"]; wun = ["wu0", "wu1"]
            wdma(wgb[0], wgn[0], sg, 0)
            wdma(wub[0], wun[0], su, 0)
            for grp in range(8):
                pa = grp % 2
                if grp + 1 < 8:
                    wdma(wgb[1 - pa], wgn[1 - pa], sg, grp + 1)
                    wdma(wub[1 - pa], wun[1 - pa], su, grp + 1)
                if grp == 0:
                    dump(wgb[0][:, 0:512], wgn[0])
                qall = ["qx%d" % _q for _q in range(8)]
                for ob in range(4):
                    gb = (ob % 2) * 2
                    ub = (ob % 2) * 2 + 1
                    if grp == 0 and ob == 0:
                        for kf in range(8):
                            tk.t(lambda t, pa=pa, kf=kf, gb=gb: t.matmul(
                                pb[gb][:, :],
                                wgb[pa][:].rearrange("p (a b) -> p a b", b=512)[:, kf, 0:128],
                                qx3[:, kf, :], start=(kf == 0), stop=(kf == 7)),
                                reads=[wgn[pa], f"qx{kf}"] + ([] if kf == 0 else [PB[gb]]),
                                writes=[PB[gb]])
                        for kf in range(8):
                            tk.t(lambda t, pa=pa, kf=kf, ub=ub: t.matmul(
                                pb[ub][:, :],
                                wub[pa][:].rearrange("p (a b) -> p a b", b=512)[:, kf, 0:128],
                                qx3[:, kf, :], start=(kf == 0), stop=(kf == 7)),
                                reads=[wun[pa], f"qx{kf}"] + ([] if kf == 0 else [PB[ub]]),
                                writes=[PB[ub]])
                        idx = 0
                        tk.s(lambda s, gb=gb: s.activation(gsil[:], pb[gb][:, :], AF.Silu),
                             reads=[PB[gb], "gsil"], writes=["gsil"])
                        tk.v(lambda v, ub=ub: v.tensor_tensor(gu3[:, 0, :], gsil[:],
                                                              pb[ub][:, :], OP.mult),
                             reads=["gsil", PB[ub], "gu"], writes=["gu"])
                        ab = tB
                        tk.s(lambda s: s.activation(tB[:], gu3[:, 0, :], AF.Abs),
                             reads=["gu", "tB"], writes=["tB"])
                        tk.v(lambda v: v.tensor_copy(m2acc[:], tB[:]),
                             reads=["tB", "m2acc"], writes=["m2acc"])
                        continue
                    def mmg(t, pa=pa, ob=ob, gb=gb):
                        w3 = wgb[pa][:].rearrange("p (a b) -> p a b", b=512)
                        last = None
                        for kf in range(8):
                            last = t.matmul(pb[gb][:, :], w3[:, kf, 128 * ob:128 * ob + 128],
                                            qx3[:, kf, :], start=(kf == 0), stop=(kf == 7))
                        return last
                    tk.t(mmg, reads=[wgn[pa]] + qall, writes=[PB[gb]])
                    def mmu(t, pa=pa, ob=ob, ub=ub):
                        w3 = wub[pa][:].rearrange("p (a b) -> p a b", b=512)
                        last = None
                        for kf in range(8):
                            last = t.matmul(pb[ub][:, :], w3[:, kf, 128 * ob:128 * ob + 128],
                                            qx3[:, kf, :], start=(kf == 0), stop=(kf == 7))
                        return last
                    tk.t(mmu, reads=[wun[pa]] + qall, writes=[PB[ub]])
                    idx = grp * 4 + ob
                    tk.s(lambda s, gb=gb: s.activation(gsil[:], pb[gb][:, :], AF.Silu),
                         reads=[PB[gb], "gsil"], writes=["gsil"])
                    tk.v(lambda v, idx=idx, ub=ub: v.tensor_tensor(gu3[:, idx, :], gsil[:],
                                                                   pb[ub][:, :], OP.mult),
                         reads=["gsil", PB[ub], "gu"], writes=["gu"])
                    ab = sqb if idx % 2 else tB
                    abn = "sqb" if idx % 2 else "tB"
                    tk.s(lambda s, idx=idx, ab=ab: s.activation(ab[:], gu3[:, idx, :], AF.Abs),
                         reads=["gu", abn], writes=[abn])
                    if idx == 0:
                        tk.v(lambda v, ab=ab: v.tensor_copy(m2acc[:], ab[:]),
                             reads=[abn, "m2acc"], writes=["m2acc"])
                    else:
                        tk.v(lambda v, ab=ab: v.tensor_tensor(m2acc[:], m2acc[:], ab[:], OP.max),
                             reads=["m2acc", abn], writes=["m2acc"])
            dump(gu3[:, 0, :], "gu"); dump(gu3[:, 31, :], "gu"); dump(m2acc[:], "m2acc")
            def trm2(t):
                last = None
                for bq in range(4):
                    last = t.matmul(pb[4][:, 128 * bq:128 * bq + 128],
                                    m2acc[:, 128 * bq:128 * bq + 128], I128f[:],
                                    start=True, stop=True)
                return last
            tk.t(trm2, reads=["m2acc", "I128f"], writes=[PB[4]])
            for bq in range(4):
                tk.v(lambda v, bq=bq: vred(v, mxc[:, bq:bq + 1],
                                                      pb[4][:, 128 * bq:128 * bq + 128], OP.max),
                     reads=[PB[4], "mxc"], writes=["mxc"])
            tk.v(lambda v: v.tensor_scalar(cA[:], mxc[:], 1e-5, None, OP.max),
                 reads=["mxc", "cA"], writes=["cA"])
            tk.v(lambda v: vrecip(v, cB[:], cA[:]), reads=["cA", "cB"], writes=["cB"])
            tk.v(lambda v: v.tensor_scalar(cB[:], cB[:], 127.0, None, OP.mult),
                 reads=["cB"], writes=["cB"])
            tk.v(lambda v: v.tensor_scalar(cC[:], cA[:], 1.0 / 127, None, OP.mult),
                 reads=["cA", "cC"], writes=["cC"])
            def c2r2(t):
                last = None
                for bq in range(4):
                    last = t.matmul(pb[4][0:1, 128 * bq:128 * bq + 128], cB[:, bq:bq + 1],
                                    I128f[:], start=True, stop=True)
                for bq in range(4):
                    last = t.matmul(pb[5][0:1, 128 * bq:128 * bq + 128], cC[:, bq:bq + 1],
                                    I128f[:], start=True, stop=True)
                return last
            tk.t(c2r2, reads=["cB", "cC", "I128f"], writes=[PB[4], PB[5]])
            tk.v(lambda v: v.tensor_copy(r0[:], pb[4][0:1, :]), reads=[PB[4], "r0"], writes=["r0"])
            tk.v(lambda v: v.tensor_copy(r1[:], pb[5][0:1, :]), reads=[PB[5], "r1"], writes=["r1"])
            bcast_row(r0, "r0", 4)
            bcast_row(r1, "r1", 5)
            tk.v(lambda v: v.tensor_copy(s2bc[:], pb[4][:, :]), reads=[PB[4], "s2bc"], writes=["s2bc"])
            tk.v(lambda v: v.tensor_copy(is2bc[:], pb[5][:, :]), reads=[PB[5], "is2bc"], writes=["is2bc"])
            dump(s2bc[:], "s2bc"); dump(is2bc[:], "is2bc")
            ddma(wg0, "wg0", 0)
            for jg in range(8):
                pa = jg % 2
                if jg + 1 < 8:
                    ddma(wgb[1 - pa], wgn[1 - pa], jg + 1)
                for jj in range(4):
                    j = 4 * jg + jj
                    qb = j % 2
                    tk.v(lambda v, j=j: v.tensor_tensor(tB[:], gu3[:, j, :], s2bc[:], OP.mult),
                         reads=["gu", "s2bc", "tB"], writes=["tB"])
                    tk.v(lambda v, qb=qb: v.tensor_scalar(qgu[qb][:], tB[:], MAGIC, MAGIC,
                                                          OP.add, OP.subtract),
                         reads=["tB", f"qgu{qb}"], writes=[f"qgu{qb}"])
                    def mmd(t, pa=pa, jj=jj, j=j, qb=qb):
                        w3 = wgb[pa][:].rearrange("p (a b) -> p a b", b=1024)
                        last = None
                        for o in range(8):
                            last = t.matmul(pb[o][:, :], w3[:, jj, 128 * o:128 * o + 128],
                                            qgu[qb][:], start=(j == 0), stop=(j == 31))
                        return last
                    tk.t(mmd, reads=[wgn[pa], f"qgu{qb}"], writes=list(PB))
            for o in range(8):
                tk.v(lambda v, o=o: v.tensor_tensor(tA[:], pb[o][:, :], is2bc[:], OP.mult),
                     reads=[PB[o], "is2bc", "tA"], writes=["tA"])
                tk.v(lambda v, o=o: v.tensor_tensor(xr3[:, o, :], xr3[:, o, :], tA[:], OP.add),
                     reads=["tA", f"xr{o}"], writes=[f"xr{o}"])

        def wWdma(buf, bufn, kd):
            def f(g, sem, buf=buf, kd=kd):
                g.dma_start(buf[:, 0:1024], sW[kd]).then_inc(sem, 16)
            tk.dsync(f, reads=[bufn, "scr_W"], writes=[bufn], ndma=1, lag=True)

        def attn():
            ln_stats()
            tk.v(lambda v: v.tensor_tensor(r0[:], murow[:], invrow[:], OP.mult),
                 reads=["murow", "invrow", "r0"], writes=["r0"])
            bcast_row(invrow, "invrow", 2)
            bcast_row(r0, "r0", 3)
            for j in range(8):
                tk.v(lambda v, j=j: v.tensor_tensor(tA[:], xr3[:, j, :], pb[2][:, :], OP.mult),
                     reads=[f"xr{j}", PB[2], "tA"], writes=["tA"])
                tk.v(lambda v: v.tensor_tensor(tA[:], tA[:], pb[3][:, :], OP.subtract),
                     reads=["tA", PB[3]], writes=["tA"])
                tk.s(lambda s, j=j: s.activation(xa3[:, j, :], tA[:], AF.Tanh),
                     reads=["tA", f"qx{j}"], writes=[f"qx{j}"])
            wgb = [wg0, wg1]; wgn = ["wg0", "wg1"]
            wWdma(wgb[0], wgn[0], 0)
            for kd in range(8):
                pa = kd % 2
                if kd + 1 < 8:
                    wWdma(wgb[1 - pa], wgn[1 - pa], kd + 1)
                def mstk(t, pa=pa, kd=kd):
                    last = None
                    for eb in range(8):
                        last = t.matmul(pb[eb][:, :],
                                        wgb[pa][:, 128 * eb:128 * eb + 128],
                                        xa3[:, kd, :], start=(kd == 0), stop=(kd == 7))
                    return last
                tk.t(mstk, reads=[wgn[pa], f"qx{kd}"], writes=list(PB))
            for eb in range(8):
                tk.s(lambda s, eb=eb: s.copy(st3[:, eb, :], pb[eb][:, :]),
                     reads=[PB[eb], "stat"], writes=["stat"])
            if delta_mode == "none":
                return
            for f in d_front_pieces(0):
                f()
            for c in range(NCH):
                ss = d_solve_stages(c)
                tp = d_tail_pieces(c - 1) if c >= 1 else [None] * 6
                fp = d_front_pieces(c + 1) if c + 1 < NCH else [None] * 3
                order = [ss[0], tp[0], ss[1], tp[1], ss[2], tp[2], ss[3],
                         tp[3], ss[4], fp[0], tp[4], ss[5], fp[1], tp[5],
                         ss[6], fp[2]]
                for f in order:
                    if f is not None:
                        f()
            for f in d_tail_pieces(NCH - 1):
                f()

        def _acc(c):
            cc = slice(64 * c, 64 * c + 64)
            def quad16(t, lhs_of, rhs_of, outp, start=True, stop=True):
                last = None
                for p in range(8):
                    for hh in range(2):
                        last = t.matmul(outp(p, hh), lhs_of(p, hh), rhs_of(p, hh),
                                        start=start, stop=stop,
                                        tile_position=(64 * hh, 64 * hh))
                return last
            def pairs(buf, w=64):
                return lambda p, hh, buf=buf, w=w: buf[64 * hh:64 * hh + 64, w * p:w * p + w]
            def ppsum(bank, w=64):
                return lambda p, hh, bank=bank, w=w: pb[bank][64 * hh:64 * hh + 64, w * p:w * p + w]
            kslice = lambda p, hh, cc=cc: xa3[:, p, cc][64 * hh:64 * hh + 64, :]
            sslice = lambda p, hh, cc=cc: st3[:, p, cc][64 * hh:64 * hh + 64, :]
            i32f = lambda p, hh: I128f[64 * hh:64 * hh + 64, 64 * hh:64 * hh + 64]
            i32b = lambda p, hh: I128b[64 * hh:64 * hh + 64, 64 * hh:64 * hh + 64]
            return cc, quad16, pairs, ppsum, kslice, sslice, i32f, i32b

        def d_front_pieces(c):
            pc = c % 2
            cc, quad16, pairs, ppsum, kslice, sslice, i32f, i32b = _acc(c)
            def F1():
                _front_f1(c, pc, quad16, ppsum, kslice)
            def F2():
                _front_f2(c, pc, quad16, ppsum, kslice, i32b)
            def F3():
                _front_f3(c, pc, quad16, ppsum, sslice, i32f)
            return [F1, F2, F3]

        def _front_f1(c, pc, quad16, ppsum, kslice):
            tk.t(lambda t, k=kslice, q=quad16, o=ppsum(0): q(t, k, k, o),
                 reads=["qx%d" % _q for _q in range(8)], writes=[PB[0]])
            tk.v(lambda v: v.tensor_tensor(Nb[0][:], pb[0][:, :], P_LOW[:], OP.mult),
                 reads=[PB[0], "P_LOW", "Nb0"], writes=["Nb0"])
            tk.v(lambda v: v.tensor_tensor(NT1[:], pb[0][:, :], P_UP[:], OP.mult),
                 reads=[PB[0], "P_UP", "NT1"], writes=["NT1"])
            tk.v(lambda v, pc=pc: v.tensor_tensor(Mqb[pc][:], pb[0][:, :], P_UQI[:], OP.mult),
                 reads=[PB[0], "P_UQI", f"Mqb{pc}"], writes=[f"Mqb{pc}"])
            tk.v(lambda v: v.tensor_tensor(FT1n[:], identp[:], NT1[:], OP.subtract),
                 reads=["identp", "NT1", "FT1n"], writes=["FT1n"])

        def _front_f2(c, pc, quad16, ppsum, kslice, i32b):
            tk.t(lambda t, k=kslice, i=i32b, q=quad16, o=ppsum(0): q(t, k, i, o),
                 reads=["qx%d" % _q for _q in range(8)] + ["I128b", PB[0]], writes=[PB[0]])
            tk.v(lambda v, pc=pc: v.tensor_scalar(KTb[pc][:], pb[0][:, :], lrevb[:], None, OP.mult),
                 reads=[PB[0], "lrevb", f"KTb{pc}"], writes=[f"KTb{pc}"])
            x0k = X0[pc][:].rearrange("p (a b) -> p a b", b=128)[:, :, 0:64]
            tk.v(lambda v, pc=pc, x0k=x0k: v.tensor_scalar(x0k, pb[0][:, :], lamcol[:], None, OP.mult),
                 reads=[PB[0], "lamcol", f"X0_{pc}"], writes=[f"X0_{pc}"])

        def _front_f3(c, pc, quad16, ppsum, sslice, i32f):
            tk.t(lambda t, s=sslice, i=i32f, q=quad16, o=ppsum(0): q(t, s, i, o),
                 reads=["stat", "I128f", PB[0]], writes=[PB[0]])
            x0v = X0[pc][:].rearrange("p (a b) -> p a b", b=128)[:, :, 64:128]
            tk.s(lambda s, pc=pc, x0v=x0v: s.copy(x0v, pb[0][:, :]),
                 reads=[PB[0], f"X0_{pc}"], writes=[f"X0_{pc}"])

        def d_solve_stages(c):
            pc = c % 2
            cc, quad16, pairs, ppsum, kslice, sslice, i32f, i32b = _acc(c)
            stages = []
            xseq = [(X0[pc], f"X0_{pc}"), (Xw[0], "Xw0"), (Xw[1], "Xw1"),
                    (Xw[0], "Xw0"), (Xw[1], "Xw1"), (Xw[0], "Xw0")]
            def app(t, ft, ftn, xi):
                xcur, xcn = xseq[xi]
                q = quad16; pr = pairs
                return q(t, pr(ft[:]), pr(xcur[:], 128),
                         lambda p, hh: pb[5 + p // 4][64 * hh:64 * hh + 64,
                                                      128 * (p % 4):128 * (p % 4) + 128])
            def stage_k(k):
                if k < 5:
                    ntp = NT1 if k == 0 else NTw[(k - 1) % 2]
                    ntpn = "NT1" if k == 0 else f"NTw{(k - 1) % 2}"
                    ncur = k % 2
                    tk.t(lambda t, ntp=ntp, ncur=ncur, q=quad16, pr=pairs, o=ppsum(3):
                         q(t, pr(ntp[:]), pr(Nb[ncur][:]), o),
                         reads=[ntpn, f"Nb{ncur}", PB[3]], writes=[PB[3]])
                    tk.t(lambda t, ntp=ntp, ncur=ncur, q=quad16, pr=pairs, o=ppsum(4):
                         q(t, pr(Nb[ncur][:]), pr(ntp[:]), o),
                         reads=[ntpn, f"Nb{ncur}", PB[4]], writes=[PB[4]])
                ft, ftn = (FT1n, "FT1n") if k == 0 else (FTS[k - 1], f"FTS{k - 1}")
                xcur, xcn = xseq[k]
                tk.t(lambda t, ft=ft, ftn=ftn, k=k: app(t, ft, ftn, k),
                     reads=[ftn, xcn, PB[5], PB[6]], writes=[PB[5], PB[6]])
                if k < 5:
                    if k < 4:
                        tk.v(lambda v, ncur=ncur: v.tensor_copy(Nb[1 - ncur][:], pb[3][:, :]),
                             reads=[PB[3], f"Nb{1 - ncur}"], writes=[f"Nb{1 - ncur}"])
                        tk.s(lambda s, k=k: s.copy(NTw[k % 2][:], pb[4][:, :]),
                             reads=[PB[4], f"NTw{k % 2}"], writes=[f"NTw{k % 2}"])
                    tk.v(lambda v, k=k: v.tensor_tensor(FTS[k][:], pb[4][:, :], identp[:], OP.add),
                         reads=[PB[4], "identp", f"FTS{k}"], writes=[f"FTS{k}"])
                    xnext, xnn = xseq[k + 1]
                    tk.v(lambda v, xnext=xnext: v.tensor_copy(xnext[:, 0:512], pb[5][:, :]),
                         reads=[PB[5], xnn], writes=[xnn])
                    tk.s(lambda s, xnext=xnext: s.copy(xnext[:, 512:1024], pb[6][:, :]),
                         reads=[PB[6], xnn], writes=[xnn])
                else:
                    for bi in range(2):
                        kview = pb[5 + bi][:].rearrange("p (a b) -> p a b", b=128)[:, :, 0:64]
                        vview = pb[5 + bi][:].rearrange("p (a b) -> p a b", b=128)[:, :, 64:128]
                        tk.v(lambda v, pc=pc, kview=kview, bi=bi: v.tensor_copy(
                            Bcb[pc][:, 256 * bi:256 * bi + 256], kview),
                             reads=[PB[5 + bi], f"Bcb{pc}"], writes=[f"Bcb{pc}"])
                        tk.s(lambda s, pc=pc, vview=vview, bi=bi: s.copy(
                            ULc[pc][:, 256 * bi:256 * bi + 256], vview),
                             reads=[PB[5 + bi], f"ULc{pc}"], writes=[f"ULc{pc}"])
            def stage_bbt():
                tk.t(lambda t, pc=pc, q=quad16, pr=pairs, i=i32b, o=ppsum(3):
                     q(t, pr(Bcb[pc][:]), i, o),
                     reads=[f"Bcb{pc}", "I128b", PB[3]], writes=[PB[3]])
                tk.v(lambda v, pc=pc: v.tensor_scalar(BBTn[pc][:], pb[3][:, :], -1.0, None, OP.mult),
                     reads=[PB[3], f"BBTn{pc}"], writes=[f"BBTn{pc}"])
            for k in range(6):
                stages.append(lambda k=k: stage_k(k))
            stages.append(stage_bbt)
            return stages

        def d_tail_pieces(c):
            pc = c % 2
            cc, quad16, pairs, ppsum, kslice, sslice, i32f, i32b = _acc(c)
            def T1():
                tk.t(lambda t, k=kslice, q=quad16, pr=pairs, o=ppsum(1):
                     q(t, k, pr(Sbf[:]), o),
                     reads=["qx%d" % _q for _q in range(8)] + ["Sbf", PB[1]], writes=[PB[1]])
                x0v = X0[pc][:].rearrange("p (a b) -> p a b", b=128)[:, :, 64:128]
                tk.v(lambda v, x0v=x0v: v.scalar_tensor_tensor(ypart[:], pb[1][:, :], lamcol[:],
                                                               x0v, OP.mult, OP.add),
                     reads=[PB[1], "lamcol", f"X0_{pc}", "ypart"], writes=["ypart"])
            def T2():
                def f_U(t, pc=pc):
                    t.matmul(pb[2][:, :], I128b[:], ULc[pc][:], start=True, stop=False)
                    last = None
                    for p in range(8):
                        for hh in range(2):
                            stop = (p == 7 and hh == 1)
                            last = t.matmul(pb[2][64 * hh:64 * hh + 64, 64 * p:64 * p + 64],
                                            BBTn[pc][64 * hh:64 * hh + 64, 64 * p:64 * p + 64],
                                            Sbf[64 * hh:64 * hh + 64, 64 * p:64 * p + 64],
                                            start=False, stop=stop,
                                            tile_position=(64 * hh, 64 * hh))
                    return last
                tk.t(f_U, reads=[f"ULc{pc}", f"BBTn{pc}", "Sbf", "I128b", PB[2]], writes=[PB[2]])
                tk.v(lambda v: v.tensor_copy(Ub[:], pb[2][:, :]), reads=[PB[2], "Ub"], writes=["Ub"])
            def T3():
                def f_dS(t, pc=pc):
                    last = t.matmul(pb[7][:, :], lCI[:], Sbf[:], start=True, stop=False)
                    for p in range(8):
                        for hh in range(2):
                            stop = (p == 7 and hh == 1)
                            last = t.matmul(pb[7][64 * hh:64 * hh + 64, 64 * p:64 * p + 64],
                                            KTb[pc][64 * hh:64 * hh + 64, 64 * p:64 * p + 64],
                                            Ub[64 * hh:64 * hh + 64, 64 * p:64 * p + 64],
                                            start=False, stop=stop,
                                            tile_position=(64 * hh, 64 * hh))
                    return last
                tk.t(f_dS, reads=["lCI", "Sbf", f"KTb{pc}", "Ub", PB[7]], writes=[PB[7]])
                tk.s(lambda s: s.copy(Sbf[:], pb[7][:, :]), reads=[PB[7], "Sbf"], writes=["Sbf"])
            def T4():
                tk.t(lambda t, pc=pc, q=quad16, pr=pairs, o=ppsum(1):
                     q(t, pr(Mqb[pc][:]), pr(Ub[:]), o),
                     reads=[f"Mqb{pc}", "Ub", PB[1], "ypart"], writes=[PB[1]])
                tk.v(lambda v: v.tensor_tensor(yv[:], ypart[:], pb[1][:, :], OP.add),
                     reads=["ypart", PB[1], "yv"], writes=["yv"])
            def T5():
                tk.v(lambda v: vred(v, tc0[:], yv[:], OP.add),
                     reads=["yv", "tc0"], writes=["tc0"])
                tk.s(lambda s: s.activation(sqb[:], yv[:], AF.Square), reads=["yv", "sqb"], writes=["sqb"])
                tk.v(lambda v: vred(v, tc1[:], sqb[:], OP.add),
                     reads=["sqb", "tc1"], writes=["tc1"])
                def f_sel(t):
                    t.matmul(pb[7][:, 0:1], seldup[:], tc0[:], start=True, stop=True)
                    return t.matmul(pb[7][:, 4:5], seldup[:], tc1[:], start=True, stop=True)
                tk.t(f_sel, reads=["seldup", "tc0", "tc1", PB[7]], writes=[PB[7]])
                tk.v(lambda v: v.tensor_scalar(muc[:], pb[7][:, 0:1], 1.0 / D, None, OP.mult),
                     reads=[PB[7], "muc"], writes=["muc"])
                tk.v(lambda v: v.tensor_scalar(tc1[:], pb[7][:, 4:5], 1.0 / D, None, OP.mult),
                     reads=[PB[7], "tc1"], writes=["tc1"])
                tk.v(lambda v: v.tensor_tensor(tc0[:], muc[:], muc[:], OP.mult),
                     reads=["muc", "tc0"], writes=["tc0"])
                tk.v(lambda v: v.tensor_tensor(tc1[:], tc1[:], tc0[:], OP.subtract),
                     reads=["tc0", "tc1"], writes=["tc1"])
                tk.v(lambda v: v.tensor_scalar(tc1[:], tc1[:], EPS, None, OP.add),
                     reads=["tc1"], writes=["tc1"])
                tk.v(lambda v: vrecip(v, tc0[:], tc1[:]), reads=["tc0", "tc1"], writes=["tc0"])
                tk.s(lambda s: s.activation(invc[:], tc0[:], AF.Sqrt), reads=["tc0", "invc"], writes=["invc"])
                tk.v(lambda v: v.tensor_scalar(muc[:], muc[:], -1.0, None, OP.mult),
                     reads=["muc"], writes=["muc"])
                tk.v(lambda v: v.tensor_scalar(yv[:], yv[:], muc[:], invc[:], OP.add, OP.mult),
                     reads=["yv", "muc", "invc"], writes=["yv"])
                tk.v(lambda v: v.tensor_tensor(yv[:], yv[:], gdl[:], OP.mult),
                     reads=["yv", "gdl"], writes=["yv"])
                tk.v(lambda v: v.tensor_tensor(ao[:], yv[:], bdl[:], OP.add),
                     reads=["yv", "bdl"], writes=["yv"])
            def T6():
                tk.t(lambda t, i=i32f, q=quad16, pr=pairs, o=ppsum(7): q(t, pr(ao[:]), i, o),
                     reads=["yv", "I128f", PB[7]], writes=[PB[7]])
                xrv = xr3[:, :, cc]
                pbv = pb[7][:].rearrange("p (a b) -> p a b", b=64)
                xrall = ["xr%d" % _j for _j in range(8)]
                tk.v(lambda v, xrv=xrv, pbv=pbv: v.tensor_tensor(xrv, xrv, pbv, OP.add),
                     reads=[PB[7]] + xrall, writes=xrall)
            return [T1, T2, T3, T4, T5, T6]

        def xload(seq, t0):
            for h in range(2):
                def f(g, sem, seq=seq, t0=t0, h=h):
                    for j in range(4 * h, 4 * h + 4):
                        g.dma_start(xr3[:, j, :],
                                    dx[seq][128 * j:128 * j + 128, t0:t0 + T]
                                    ).then_inc(sem, 16)
                tk.dg(f, reads=[f"xr{j}" for j in range(4 * h, 4 * h + 4)],
                      writes=[f"xr{j}" for j in range(4 * h, 4 * h + 4)], ndma=4)

        def xstore(seq, t0):
            for h in range(2):
                def f(g, sem, seq=seq, t0=t0, h=h):
                    for j in range(4 * h, 4 * h + 4):
                        g.dma_start(dout[seq][128 * j:128 * j + 128, t0:t0 + T],
                                    xr3[:, j, :]).then_inc(sem, 16)
                tk.dg(f, reads=[f"xr{j}" for j in range(4 * h, 4 * h + 4)], ndma=4)

        xload(0, 0)
        if do_f1:
            ffn_front()
        prep_all()

        for seq in range(n_seq):
            tk.v(lambda v: v.memset(Sbf[:], 0.0), reads=["Sbf"], writes=["Sbf"])
            for ti in range(n_tiles):
                t0 = T * ti
                if not (seq == 0 and ti == 0):
                    xload(seq, t0)
                if do_f1:
                    ffn(skip_front=(seq == 0 and ti == 0))
                if do_attn:
                    attn()
                if do_f2:
                    ffn()
                xstore(seq, t0)

        TINY = frozenset(["muc", "invc", "tc0", "tc1", "cA", "cB", "cC", "cD",
                          "cE", "mxc", "mnc", "muc4", "ivc4", "acc_col",
                          "red_col", "swcol", "dqcol", "etacol", "zcol",
                          "betacol", "nlamcol", "lamcol", "lrevb", "lCcol",
                          "escal", "zscal", "loglam"])
        tk.finalize(tiny=TINY)
        tk.emit(nc, sems)
    return nc


_NC_CACHE = {}

def _consts():
    I128 = np.eye(128, dtype=np.float32)
    ONESROW = np.ones((1, 128), np.float32)
    ONE1 = np.ones((1, 1), np.float32)
    ONESCOL = np.ones((128, 1), np.float32)
    q = np.arange(128)
    SELDUP = ((q[:, None] % 64) == (q[None, :] % 64)).astype(np.float32)
    pm = np.arange(128) % 64
    fm = np.arange(512) % 64
    PH, FH = np.meshgrid(pm, fm, indexing="ij")
    DL = np.where(PH > FH, (PH - FH), 0).astype(np.float32)
    ML = (PH > FH).astype(np.float32)
    DU = np.where(FH > PH, (FH - PH), 0).astype(np.float32)
    MU = (FH > PH).astype(np.float32)
    DQ = np.where(FH >= PH, (FH - PH), 0).astype(np.float32)
    MQ = (FH >= PH).astype(np.float32)
    IOT = (np.arange(128) % 64 + 1).astype(np.float32).reshape(128, 1)
    REV = (63 - np.arange(128) % 64).astype(np.float32).reshape(128, 1)
    CC = np.full((128, 1), 64.0, np.float32)
    IDP = ((np.arange(128) % 64)[:, None] == (np.arange(512) % 64)[None, :]).astype(np.float32)
    return dict(I128=I128, ONESROW=ONESROW, ONE1=ONE1, ONESCOL=ONESCOL,
                SELDUP=SELDUP, DLm=DL, MLm=ML, DUm=DU, MUm=MU, DQm=DQ, MQm=MQ,
                IOT=IOT, REV=REV, CC=CC, IDP=IDP)


def make_inputs(L0_attn_x, L0_attn_W, L0_attn_ln_gamma, L0_attn_ln_beta,
                L0_attn_eta, L0_attn_lam_logit, L0_ffn_w_gate, L0_ffn_w_up,
                L0_ffn_w_down):
    x = np.ascontiguousarray(np.asarray(L0_attn_x, np.float32).transpose(0, 2, 1))
    wgT = np.ascontiguousarray(np.asarray(L0_ffn_w_gate, np.float32).T)
    wuT = np.ascontiguousarray(np.asarray(L0_ffn_w_up, np.float32).T)
    wdT = np.ascontiguousarray(np.asarray(L0_ffn_w_down, np.float32).T)
    WT = np.ascontiguousarray(np.asarray(L0_attn_W, np.float32).T)
    gamma = np.asarray(L0_attn_ln_gamma, np.float32)
    betal = np.asarray(L0_attn_ln_beta, np.float32)
    g3 = gamma.reshape(8, 2, 64)
    b3 = betal.reshape(8, 2, 64)
    gdl = np.concatenate([
        np.broadcast_to(g3[:, 0, :].reshape(1, 512), (64, 512)),
        np.broadcast_to(g3[:, 1, :].reshape(1, 512), (64, 512))], 0).astype(np.float32)
    bdl = np.concatenate([
        np.broadcast_to(b3[:, 0, :].reshape(1, 512), (64, 512)),
        np.broadcast_to(b3[:, 1, :].reshape(1, 512), (64, 512))], 0).astype(np.float32)
    eta1 = np.asarray(L0_attn_eta, np.float32).reshape(1, 1)
    lamz = np.asarray(L0_attn_lam_logit, np.float32).reshape(1, 1)
    base = dict(wgT=wgT, wuT=wuT, wdT=wdT, WT=WT, gdl=gdl, bdl=bdl,
                eta1=eta1, lamz=lamz, **_consts())
    return x, base


def kernel(**inputs):
    x, base = make_inputs(**inputs)
    if "nc" not in _NC_CACHE:
        _NC_CACHE["nc"] = build_nc()
    nc = _NC_CACHE["nc"]
    in_maps = [dict(base, x=np.ascontiguousarray(x[2 * c:2 * c + 2]))
               for c in range(NCORES)]
    res = run_bass_kernel_spmd(nc, in_maps, core_ids=list(range(NCORES)))
    out = np.concatenate([r["out"] for r in res.results], 0)
    return np.ascontiguousarray(out.transpose(0, 2, 1))


# version 22 (final)
# speedup vs baseline: 1.0204x; 1.0204x over previous
import numpy as np
import ml_dtypes
from contextlib import ExitStack

import concourse.bass as bass
import concourse.mybir as mybir
from concourse.bass_utils import run_bass_kernel_spmd
from concourse.alu_op_type import AluOpType as OP

F32 = mybir.dt.float32
BF16 = mybir.dt.bfloat16
AF = mybir.ActivationFunctionType
AX = mybir.AxisListType.X
MAGIC = float(np.float32(1.5 * 2 ** 23))

NCORES = 8
L = 2048
D = 1024
T = 512
NTILE = L // T
C = 64
NCH = T // C
INNER = 4096
EPS = 1e-5

class Step:
    __slots__ = ("eng", "fn", "reads", "writes", "dma", "idx", "inc_val",
                 "waits", "needed", "ndma", "drain_after", "gate_id",
                 "lag", "parity", "gate_for")
    def __init__(self, eng, fn, reads, writes, dma, idx, ndma, lag=False):
        self.eng = eng; self.fn = fn; self.reads = tuple(reads)
        self.writes = tuple(writes); self.dma = dma; self.idx = idx
        self.inc_val = None; self.waits = []; self.needed = False
        self.ndma = ndma; self.drain_after = False; self.gate_id = None
        self.lag = lag; self.parity = 0; self.gate_for = None

class TK:
    def __init__(self):
        self.steps = []

    def step(self, eng, fn, reads=(), writes=(), dma=False, ndma=1, lag=False):
        s = Step(eng, fn, reads, writes, dma, len(self.steps), ndma, lag)
        self.steps.append(s)
        return s

    def t(self, fn, reads=(), writes=()):  return self.step("tensor", fn, reads, writes)
    def v(self, fn, reads=(), writes=()):  return self.step("vector", fn, reads, writes)
    def s(self, fn, reads=(), writes=()):  return self.step("scalar", fn, reads, writes)
    def g(self, fn, reads=(), writes=()):  return self.step("gpsimd", fn, reads, writes)
    def dsync(self, fn, reads=(), writes=(), ndma=1, lag=False):
        s = self.step("sync", fn, reads, writes, dma=True, ndma=ndma, lag=lag)
        g = self.step("gpsimd", None)
        g.gate_for = s
        return s
    def dg(self, fn, reads=(), writes=(), ndma=1):
        return self.step("gpsimd", fn, reads, writes, dma=True, ndma=ndma)

    def finalize(self, tiny=frozenset()):
        last_writer = {}
        readers = {}
        deps_of = []
        for s in self.steps:
            deps = set()
            for r in s.reads:
                lw = last_writer.get(r)
                if lw is not None:
                    deps.add(lw)
            for w in s.writes:
                lw = last_writer.get(w)
                if lw is not None:
                    deps.add(lw)
                deps.update(readers.get(w, ()))
            for r in s.reads:
                if r.startswith("pb"):
                    for pr_ in readers.get(r, ()):
                        if self.steps[pr_].eng != s.eng:
                            deps.add(pr_)
            pruned = set()
            for d in deps:
                ds = self.steps[d]
                if not ds.dma and any(w in tiny and w in s.reads for w in ds.writes):
                    ds.drain_after = True
                if ds.eng == s.eng and not ds.dma and not s.dma:
                    continue
                pruned.add(d)
            deps_of.append(pruned)
            for d in pruned:
                self.steps[d].needed = True
            for r in s.reads:
                readers.setdefault(r, set()).add(s.idx)
            for w in s.writes:
                last_writer[w] = s.idx
                readers[w] = {s.idx}
        cnt = {}
        qord = {}
        for s in self.steps:
            if s.dma:
                s.parity = qord.get(s.eng, 0) % 3
                qord[s.eng] = qord.get(s.eng, 0) + 1
                key = f"q_{s.eng}{s.parity}"
                cnt[key] = cnt.get(key, 0) + 16 * s.ndma
                s.inc_val = cnt[key]
                s.needed = True
                gk = "g_gate" if s.eng == "gpsimd" else "s_gate"
                cnt[gk] = cnt.get(gk, 0) + 1
                s.gate_id = cnt[gk]
            elif s.needed:
                key = s.eng
                cnt[key] = cnt.get(key, 0) + 1
                s.inc_val = cnt[key]
        waited = {}
        for s, deps in zip(self.steps, deps_of):
            for d in sorted(deps):
                ds = self.steps[d]
                if ds.inc_val is None:
                    continue
                if ds.gate_id is not None:
                    key = "g_gate" if ds.eng == "gpsimd" else "s_gate"
                    val = ds.gate_id
                else:
                    key, val = (("q_" + ds.eng) if ds.dma else ds.eng), ds.inc_val
                wkey = (s.eng, key)
                if waited.get(wkey, -1) >= val:
                    continue
                waited[wkey] = val
                s.waits.append((key, val))
        self.final_counts = cnt

    def emit(self, nc, sems):
        by_eng = {}
        for s in self.steps:
            by_eng.setdefault(s.eng, []).append(s)
        with nc.Block() as block:
            def section(eng_name):
                def body(eng):
                    inflight = []
                    def flush_gate(ps):
                        gk = "g_gate" if ps.eng == "gpsimd" else "s_gate"
                        eng.wait_ge(sems[f"q_{ps.eng}{ps.parity}"], ps.inc_val)
                        eng.nop().then_inc(sems[gk], 1)
                    for s in by_eng.get(eng_name, ()):
                        if s.gate_for is not None:
                            flush_gate(s.gate_for)
                            continue
                        for key, val in s.waits:
                            eng.wait_ge(sems[key], val)
                        if s.dma:
                            if s.eng == "sync":
                                while len(inflight) >= 3:
                                    eng.wait_ge(sems["s_gate"],
                                                inflight.pop(0).gate_id)
                                s.fn(eng, sems[f"q_{s.eng}{s.parity}"])
                                inflight.append(s)
                            else:
                                s.fn(eng, sems[f"q_{s.eng}{s.parity}"])
                                flush_gate(s)
                        else:
                            inst = s.fn(eng)
                            if s.drain_after:
                                eng.drain()
                                if s.inc_val is not None:
                                    eng.nop().then_inc(sems[s.eng], 1)
                            elif s.inc_val is not None:
                                inst.then_inc(sems[s.eng], 1)
                return body
            block.sync(section("sync"))
            block.tensor(section("tensor"))
            block.vector(section("vector"))
            block.scalar(section("scalar"))
            block.gpsimd(section("gpsimd"))
            @block.sync
            def _(sync):
                for q in ("q_sync0", "q_sync1", "q_sync2",
                          "q_gpsimd0", "q_gpsimd1", "q_gpsimd2"):
                    if self.final_counts.get(q):
                        sync.wait_ge(sems[q], self.final_counts[q])


def build_nc(n_seq=2, n_tiles=NTILE, do_f1=True, do_attn=True, do_f2=True, dbg=False, delta_mode="all"):
    nc = bass.Bass("TRN2", target_bir_lowering=False)
    tk = TK()

    dx = nc.dram_tensor("x", [n_seq, D, L], F32, kind="ExternalInput").ap()
    dwg = nc.dram_tensor("wgT", [D, INNER], F32, kind="ExternalInput").ap()
    dwu = nc.dram_tensor("wuT", [D, INNER], F32, kind="ExternalInput").ap()
    dwd = nc.dram_tensor("wdT", [INNER, D], F32, kind="ExternalInput").ap()
    dW = nc.dram_tensor("WT", [D, D], F32, kind="ExternalInput").ap()
    dgdl = nc.dram_tensor("gdl", [128, 512], F32, kind="ExternalInput").ap()
    dbdl = nc.dram_tensor("bdl", [128, 512], F32, kind="ExternalInput").ap()
    deta = nc.dram_tensor("eta1", [1, 1], F32, kind="ExternalInput").ap()
    dlamz = nc.dram_tensor("lamz", [1, 1], F32, kind="ExternalInput").ap()
    dI128 = nc.dram_tensor("I128", [128, 128], F32, kind="ExternalInput").ap()
    dONESROW = nc.dram_tensor("ONESROW", [1, 128], F32, kind="ExternalInput").ap()
    dONE1 = nc.dram_tensor("ONE1", [1, 1], F32, kind="ExternalInput").ap()
    dONESCOL = nc.dram_tensor("ONESCOL", [128, 1], F32, kind="ExternalInput").ap()
    dSELDUP = nc.dram_tensor("SELDUP", [128, 128], F32, kind="ExternalInput").ap()
    dDL = nc.dram_tensor("DLm", [128, 512], F32, kind="ExternalInput").ap()
    dDU = nc.dram_tensor("DUm", [128, 512], F32, kind="ExternalInput").ap()
    dDQ = nc.dram_tensor("DQm", [128, 512], F32, kind="ExternalInput").ap()
    dML = nc.dram_tensor("MLm", [128, 512], F32, kind="ExternalInput").ap()
    dMU = nc.dram_tensor("MUm", [128, 512], F32, kind="ExternalInput").ap()
    dMQ = nc.dram_tensor("MQm", [128, 512], F32, kind="ExternalInput").ap()
    dIOT = nc.dram_tensor("IOT", [128, 1], F32, kind="ExternalInput").ap()
    dREV = nc.dram_tensor("REV", [128, 1], F32, kind="ExternalInput").ap()
    dCC = nc.dram_tensor("CC", [128, 1], F32, kind="ExternalInput").ap()
    dout = nc.dram_tensor("out", [n_seq, D, L], F32, kind="ExternalOutput").ap()
    sg = nc.dram_tensor("scr_g", [8, 128, 4096], BF16, kind="Internal").ap()
    su = nc.dram_tensor("scr_u", [8, 128, 4096], BF16, kind="Internal").ap()
    sd = nc.dram_tensor("scr_d", [8, 128, 4096], BF16, kind="Internal").ap()
    sW = nc.dram_tensor("scr_W", [8, 128, 1024], BF16, kind="Internal").ap()
    dIDP = nc.dram_tensor("IDP", [128, 512], F32, kind="ExternalInput").ap()
    ddbg = nc.dram_tensor("dbg", [40, 128, 512], F32, kind="ExternalOutput").ap() if dbg else None

    st = ExitStack()
    e = st.enter_context
    with st:
        xr = e(nc.sbuf_tensor("xr", [128, 4096], F32))
        gu = e(nc.sbuf_tensor("gu", [128, 16384], F32))
        qx = e(nc.sbuf_tensor("qxa", [128, 4096], BF16))
        xa = qx
        stat = e(nc.sbuf_tensor("stat", [128, 4096], F32))
        wg0 = e(nc.sbuf_tensor("wg0", [128, 4096], BF16))
        wg1 = e(nc.sbuf_tensor("wg1", [128, 4096], BF16))
        wu0 = e(nc.sbuf_tensor("wu0", [128, 4096], BF16))
        wu1 = e(nc.sbuf_tensor("wu1", [128, 4096], BF16))
        gsil = e(nc.sbuf_tensor("gsil", [128, 512], F32))
        qgu = [e(nc.sbuf_tensor(f"qgu{i}", [128, 512], BF16)) for i in range(2)]
        m2acc = e(nc.sbuf_tensor("m2acc", [128, 512], F32))
        sqb = e(nc.sbuf_tensor("sqb", [128, 512], F32))
        tA = e(nc.sbuf_tensor("tA", [128, 512], F32))
        tB = e(nc.sbuf_tensor("tB", [128, 512], F32))
        mxacc = tA
        mnacc = tB
        s2bc = e(nc.sbuf_tensor("s2bc", [128, 512], F32))
        is2bc = e(nc.sbuf_tensor("is2bc", [128, 512], F32))
        P_LOW = e(nc.sbuf_tensor("P_LOW", [128, 512], BF16))
        P_UP = e(nc.sbuf_tensor("P_UP", [128, 512], BF16))
        P_UQI = e(nc.sbuf_tensor("P_UQI", [128, 512], BF16))
        Nb = [e(nc.sbuf_tensor(f"Nb{i}", [128, 512], BF16)) for i in range(2)]
        NTw = [e(nc.sbuf_tensor(f"NTw{i}", [128, 512], BF16)) for i in range(2)]
        NT1 = e(nc.sbuf_tensor("NT1", [128, 512], BF16))
        FTS = [e(nc.sbuf_tensor(f"FTS{i}", [128, 512], BF16)) for i in range(5)]
        FT1n = e(nc.sbuf_tensor("FT1n", [128, 512], BF16))
        identp = e(nc.sbuf_tensor("identp", [128, 512], BF16))
        Mqb = [e(nc.sbuf_tensor(f"Mqb{i}", [128, 512], BF16)) for i in range(2)]
        X0 = [e(nc.sbuf_tensor(f"X0_{i}", [128, 1024], BF16)) for i in range(2)]
        Xw = [e(nc.sbuf_tensor(f"Xw{i}", [128, 1024], BF16)) for i in range(2)]
        Bcb = [e(nc.sbuf_tensor(f"Bcb{i}", [128, 512], BF16)) for i in range(2)]
        ULc = [e(nc.sbuf_tensor(f"ULc{i}", [128, 512], BF16)) for i in range(2)]
        BBTn = [e(nc.sbuf_tensor(f"BBTn{i}", [128, 512], BF16)) for i in range(2)]
        KTb = [e(nc.sbuf_tensor(f"KTb{i}", [128, 512], BF16)) for i in range(2)]
        Ub = e(nc.sbuf_tensor("Ub", [128, 512], BF16))
        Sbf = e(nc.sbuf_tensor("Sbf", [128, 512], BF16))
        ypart = e(nc.sbuf_tensor("ypart", [128, 512], F32))
        yv = e(nc.sbuf_tensor("yv", [128, 512], F32))
        ao = yv
        I128f = e(nc.sbuf_tensor("I128f", [128, 128], F32))
        I128b = e(nc.sbuf_tensor("I128b", [128, 128], BF16))
        lCI = e(nc.sbuf_tensor("lCI", [128, 128], BF16))
        onesrow = e(nc.sbuf_tensor("onesrow", [1, 128], F32))
        one1 = e(nc.sbuf_tensor("one1", [1, 1], F32))
        onescol = e(nc.sbuf_tensor("onescol", [128, 1], F32))
        seldup = e(nc.sbuf_tensor("seldup", [128, 128], F32))
        gdl = e(nc.sbuf_tensor("gdl_s", [128, 512], BF16))
        bdl = e(nc.sbuf_tensor("bdl_s", [128, 512], BF16))
        def col(name):
            return e(nc.sbuf_tensor(name, [128, 1], F32))
        zcol = col("zcol"); etacol = col("etacol"); loglam = col("loglam")
        betacol = col("betacol")
        lamcol = col("lamcol"); nlamcol = col("nlamcol"); lrevb = col("lrevb")
        lCcol = col("lCcol"); tc0 = col("tc0"); tc1 = col("tc1")
        swcol = col("swcol"); dqcol = col("dqcol")
        muc = col("muc"); invc = col("invc"); magcol = col("magcol")
        mxc = e(nc.sbuf_tensor("mxc", [128, 4], F32))
        mnc = e(nc.sbuf_tensor("mnc", [128, 4], F32))
        cA = e(nc.sbuf_tensor("cA", [128, 4], F32))
        cB = e(nc.sbuf_tensor("cB", [128, 4], F32))
        cC = e(nc.sbuf_tensor("cC", [128, 4], F32))
        cD = e(nc.sbuf_tensor("cD", [128, 4], F32))
        cE = e(nc.sbuf_tensor("cE", [128, 4], F32))
        muc4 = e(nc.sbuf_tensor("muc4", [128, 4], F32))
        ivc4 = e(nc.sbuf_tensor("ivc4", [128, 4], F32))
        def row(name):
            return e(nc.sbuf_tensor(name, [1, 512], F32))
        murow = row("murow")
        invrow = row("invrow"); r0 = row("r0"); r1 = row("r1"); r2 = row("r2")
        varrow = r1
        msrow = r2
        escal = e(nc.sbuf_tensor("escal", [1, 1], F32))
        zscal = e(nc.sbuf_tensor("zscal", [1, 1], F32))
        acc_col = col("acc_col"); red_col = col("red_col")
        pb = [e(nc.psum_tensor(f"pb{i}", [128, 512], F32)) for i in range(8)]
        PB = [f"pb{i}" for i in range(8)]

        sems = {}
        for k in ("tensor", "vector", "scalar", "gpsimd", "sync",
                  "q_sync0", "q_sync1", "q_sync2",
                  "q_gpsimd0", "q_gpsimd1", "q_gpsimd2",
                  "g_gate", "s_gate"):
            sems[k] = e(nc.semaphore(k + "_sem"))

        class _WS:
            def __getitem__(self, idx):
                if idx == slice(None):
                    return gu[:, 0:4096]
                return gu.__getitem__(idx)
        wstage = _WS()
        wq_stage = qx
        xr3 = xr[:].rearrange("p (a b) -> p a b", b=512)
        gu3 = gu[:].rearrange("p (a b) -> p a b", b=512)
        qx3 = qx[:].rearrange("p (a b) -> p a b", b=512)
        xa3 = xa[:].rearrange("p (a b) -> p a b", b=512)
        st3 = stat[:].rearrange("p (a b) -> p a b", b=512)

        def vrecip(v, out, in_):
            v.drain()
            i = v.reciprocal(out, in_)
            v.drain()
            return i

        def vred(v, out, in_, op, absv=None):
            i = v.tensor_reduce(out, in_, AX, op, apply_absolute_value=absv)
            v.drain()
            return i

        dbg_i = [0]
        def dump(ap, res):
            if not dbg or dbg_i[0] >= 40:
                return
            slot = dbg_i[0]; dbg_i[0] += 1
            sh = ap.shape if hasattr(ap, "shape") else None
            tk.dg(lambda g, sem, ap=ap, slot=slot: g.dma_start(
                ddbg[slot][0:ap.shape[0], 0:ap.shape[1]], ap,
                allow_slow_non_contiguous=True).then_inc(sem, 16),
                reads=[res])

        def ld(dst, src, rs=(), ws=()):
            tk.dsync(lambda g, sem, d=dst, s=src: g.dma_start(d, s).then_inc(sem, 16),
                     reads=rs, writes=ws)

        def ldg(dst, src, rs=(), ws=()):
            tk.dg(lambda g, sem, d=dst, s=src: g.dma_start(d, s).then_inc(sem, 16),
                  reads=rs, writes=ws)

        ld(I128f[:], dI128, ws=["I128f"])
        ld(onesrow[:], dONESROW, ws=["onesrow"])
        ld(one1[:], dONE1, ws=["one1"])
        ld(onescol[:], dONESCOL, ws=["onescol"])
        ld(seldup[:], dSELDUP, ws=["seldup"])
        ldg(gdl[:], dgdl, ws=["gdl"])
        ldg(bdl[:], dbdl, ws=["bdl"])
        ld(escal[:], deta, ws=["escal"])
        ld(zscal[:], dlamz, ws=["zscal"])
        tk.v(lambda v: v.tensor_copy(I128b[:], I128f[:]), reads=["I128f"], writes=["I128b"])
        tk.v(lambda v: v.memset(magcol[:], MAGIC), reads=["magcol"], writes=["magcol"])

        tk.t(lambda t: t.matmul(pb[0][:, 0:1], onesrow[:], escal[:], start=True, stop=True),
             reads=["onesrow", "escal"], writes=[PB[0]])
        tk.v(lambda v: v.tensor_copy(etacol[:], pb[0][:, 0:1]), reads=[PB[0]], writes=["etacol"])
        tk.t(lambda t: t.matmul(pb[0][:, 0:1], onesrow[:], zscal[:], start=True, stop=True),
             reads=["onesrow", "zscal", "etacol"], writes=[PB[0]])
        tk.v(lambda v: v.tensor_copy(zcol[:], pb[0][:, 0:1]), reads=[PB[0]], writes=["zcol"])
        tk.s(lambda s: s.activation(tc0[:], zcol[:], AF.Sigmoid), reads=["zcol"], writes=["tc0"])
        tk.s(lambda s: s.activation(loglam[:], tc0[:], AF.Ln), reads=["tc0"], writes=["loglam"])
        tk.v(lambda v: v.tensor_scalar(tc1[:], tc0[:], -1.0, 1.0, OP.mult, OP.add),
             reads=["tc0"], writes=["tc1"])
        tk.v(lambda v: v.tensor_tensor(tc1[:], tc1[:], etacol[:], OP.mult),
             reads=["tc1", "etacol"], writes=["tc1"])
        tk.v(lambda v: v.tensor_scalar(betacol[:], tc1[:], 1.0 / 64, None, OP.mult),
             reads=["tc1"], writes=["betacol"])
        ld(tc0[:], dIOT, rs=["tc0"], ws=["tc0"])
        tk.v(lambda v: v.tensor_tensor(tc0[:], tc0[:], loglam[:], OP.mult),
             reads=["tc0", "loglam"], writes=["tc0"])
        tk.s(lambda s: s.activation(lamcol[:], tc0[:], AF.Exp), reads=["tc0"], writes=["lamcol"])
        tk.v(lambda v: v.tensor_scalar(nlamcol[:], lamcol[:], -1.0, None, OP.mult),
             reads=["lamcol"], writes=["nlamcol"])
        ld(tc0[:], dREV, rs=["tc0"], ws=["tc0"])
        tk.v(lambda v: v.tensor_tensor(tc0[:], tc0[:], loglam[:], OP.mult),
             reads=["tc0", "loglam"], writes=["tc0"])
        tk.s(lambda s: s.activation(tc1[:], tc0[:], AF.Exp), reads=["tc0"], writes=["tc1"])
        tk.v(lambda v: v.tensor_tensor(lrevb[:], tc1[:], betacol[:], OP.mult),
             reads=["tc1", "betacol"], writes=["lrevb"])
        ld(tc0[:], dCC, rs=["tc0"], ws=["tc0"])
        tk.v(lambda v: v.tensor_tensor(tc0[:], tc0[:], loglam[:], OP.mult),
             reads=["tc0", "loglam"], writes=["tc0"])
        tk.s(lambda s: s.activation(lCcol[:], tc0[:], AF.Exp), reads=["tc0"], writes=["lCcol"])
        tk.v(lambda v: v.tensor_scalar(lCI[:], I128f[:], lCcol[:], None, OP.mult),
             reads=["I128f", "lCcol"], writes=["lCI"])
        for Dm, Mm, dst, dstn in [(dDL, dML, P_LOW, "P_LOW"), (dDU, dMU, P_UP, "P_UP"),
                                  (dDQ, dMQ, P_UQI, "P_UQI")]:
            ld(tA[:], Dm, rs=["tA"], ws=["tA"])
            ld(tB[:], Mm, rs=["tB"], ws=["tB"])
            tk.v(lambda v: v.tensor_scalar(tA[:], tA[:], loglam[:], None, OP.mult),
                 reads=["tA", "loglam"], writes=["tA"])
            tk.s(lambda s: s.activation(tA[:], tA[:], AF.Exp), reads=["tA"], writes=["tA"])
            tk.v(lambda v, d=dst: v.scalar_tensor_tensor(d[:], tA[:], betacol[:], tB[:],
                                                         OP.mult, OP.mult),
                 reads=["tA", "betacol", "tB", dstn], writes=[dstn])

        ldg(identp[:], dIDP, ws=["identp"])
        for kd in range(8):
            tk.dg(lambda g, sem, kd=kd: g.dma_start(
                wg0[:, 0:1024], dW[128 * kd:128 * kd + 128, :]).then_inc(sem, 16),
                  reads=["wg0"], writes=["wg0"])
            tk.dg(lambda g, sem, kd=kd: g.dma_start(
                sW[kd], wg0[:, 0:1024]).then_inc(sem, 16),
                  reads=["wg0"], writes=["scr_W"])

        def prep_scale(extra):
            tk.v(lambda v: v.tensor_scalar(escal[:], acc_col[0:1, 0:1], 1.0 / (D * INNER), 1e-5,
                                           OP.mult, OP.max), reads=["acc_col", "escal"], writes=["escal"])
            tk.v(lambda v: vrecip(v, zscal[:], escal[:]), reads=["zscal", "escal"], writes=["zscal"])
            tk.t(lambda t: t.matmul(pb[0][:, 0:1], onesrow[:], zscal[:], start=True, stop=True),
                 reads=["onesrow", "zscal"], writes=[PB[0]])
            tk.v(lambda v: v.tensor_copy(swcol[:], pb[0][:, 0:1]), reads=[PB[0]], writes=["swcol"])
            tk.t(lambda t: t.matmul(pb[0][:, 0:1], onesrow[:], escal[:], start=True, stop=True),
                 reads=["onesrow", "escal", "swcol"], writes=[PB[0]])
            tk.v(lambda v, x=extra: v.tensor_scalar(dqcol[:], pb[0][:, 0:1], x, None, OP.mult),
                 reads=[PB[0]], writes=["dqcol"])

        wsl = [gu[:, 4096 * i:4096 * i + 4096] for i in range(4)]
        wsn = ["guA", "guB", "guC", "guD"]
        wctr = [0]

        def prep_reduce(dsrc, rowblock):
            for pg in range(8):
                pp = wctr[0] % 4; wctr[0] += 1
                if rowblock:
                    tk.dsync(lambda g, sem, pg=pg, pp=pp: g.dma_start(
                        wsl[pp].rearrange("p (a b) -> p a b", b=1024),
                        dsrc[512 * pg:512 * pg + 512, :].rearrange("(a p) b -> p a b", p=128)
                        ).then_inc(sem, 16),
                          reads=[wsn[pp]], writes=[wsn[pp]])
                else:
                    tk.dsync(lambda g, sem, pg=pg, pp=pp: g.dma_start(
                        wsl[pp], dsrc[128 * pg:128 * pg + 128, :]).then_inc(sem, 16),
                          reads=[wsn[pp]], writes=[wsn[pp]])
                tk.s(lambda s, pp=pp: s.activation(wsl[pp], wsl[pp], AF.Abs),
                     reads=[wsn[pp]], writes=[wsn[pp]])
                def f(t, pg=pg, pp=pp):
                    last = None
                    w3 = wsl[pp].rearrange("p (a b) -> p a b", b=512)
                    for q8 in range(8):
                        last = t.matmul(pb[1][0:1, :], onescol[:], w3[:, q8, :],
                                        start=(pg == 0 and q8 == 0),
                                        stop=(pg == 7 and q8 == 7))
                    return last
                tk.t(f, reads=[wsn[pp], "onescol"] + ([] if pg == 0 else [PB[1]]),
                     writes=[PB[1]])
            tk.v(lambda v: vred(v, acc_col[0:1, :], pb[1][0:1, :], OP.add),
                 reads=[PB[1], "acc_col"], writes=["acc_col"])

        def prep_quant(dsrc, dscr, scrn, rowblock):
            for pg in range(8):
                pp = wctr[0] % 4; wctr[0] += 1
                if rowblock:
                    tk.dsync(lambda g, sem, pg=pg, pp=pp: g.dma_start(
                        wsl[pp].rearrange("p (a b) -> p a b", b=1024),
                        dsrc[512 * pg:512 * pg + 512, :].rearrange("(a p) b -> p a b", p=128)
                        ).then_inc(sem, 16),
                          reads=[wsn[pp]], writes=[wsn[pp]])
                else:
                    tk.dsync(lambda g, sem, pg=pg, pp=pp: g.dma_start(
                        wsl[pp], dsrc[128 * pg:128 * pg + 128, :]).then_inc(sem, 16),
                          reads=[wsn[pp]], writes=[wsn[pp]])
                tk.s(lambda s, pp=pp: s.activation(wsl[pp], wsl[pp], AF.Identity,
                                                   scale=swcol[:], bias=magcol[:]),
                     reads=[wsn[pp], "swcol", "magcol"], writes=[wsn[pp]])
                tk.v(lambda v, pp=pp: v.tensor_scalar(wsl[pp], wsl[pp], MAGIC, 1.0,
                                               OP.subtract, OP.min),
                     reads=[wsn[pp]], writes=[wsn[pp]])
                tk.v(lambda v, pp=pp: v.tensor_scalar(wsl[pp], wsl[pp], -1.0, dqcol[:],
                                               OP.max, OP.mult),
                     reads=[wsn[pp], "dqcol"], writes=[wsn[pp]])
                tk.dg(lambda g, sem, pg=pg, pp=pp: g.dma_start(
                    dscr[pg], wsl[pp]).then_inc(sem, 16),
                    reads=[wsn[pp]], writes=[scrn])

        def prep_gu(dsrc, dscr, scrn, dbg_this=False):
            prep_reduce(dsrc, False)
            prep_scale(1.0)
            prep_quant(dsrc, dscr, scrn, False)

        def prep_all():
            prep_gu(dwg, sg, "scr_g")
            prep_gu(dwu, su, "scr_u")
            prep_reduce(dwd, True)
            prep_scale(0.5)
            prep_quant(dwd, sd, "scr_d", True)
            tk.v(lambda v: v.tensor_copy(gu[:, 0:1], gu[:, 0:1]),
                 reads=["guA", "guB", "guC", "guD", "gu"],
                 writes=["guA", "guB", "guC", "guD", "gu"])

        def ln_stats():
            for j in range(8):
                tk.t(lambda t, j=j: t.matmul(pb[6][0:1, :], onescol[:], xr3[:, j, :],
                                             start=(j == 0), stop=(j == 7)),
                     reads=[f"xr{j}", "onescol"] + ([] if j else [PB[6]]), writes=[PB[6]])
            for j in range(8):
                tk.s(lambda s, j=j: s.activation(sqb[:], xr3[:, j, :], AF.Square),
                     reads=[f"xr{j}", "sqb"], writes=["sqb"])
                tk.t(lambda t, j=j: t.matmul(pb[7][0:1, :], onescol[:], sqb[:],
                                             start=(j == 0), stop=(j == 7)),
                     reads=["sqb", "onescol"], writes=[PB[7]])
            tk.v(lambda v: v.tensor_scalar(murow[:], pb[6][0:1, :], 1.0 / D, None, OP.mult),
                 reads=[PB[6]], writes=["murow"])
            tk.v(lambda v: v.tensor_scalar(msrow[:], pb[7][0:1, :], 1.0 / D, None, OP.mult),
                 reads=[PB[7], "r2"], writes=["r2"])
            tk.v(lambda v: v.tensor_tensor(varrow[:], murow[:], murow[:], OP.mult),
                 reads=["murow", "r1"], writes=["r1"])
            tk.v(lambda v: v.tensor_tensor(varrow[:], msrow[:], varrow[:], OP.subtract),
                 reads=["r2", "r1"], writes=["r1"])
            tk.v(lambda v: v.tensor_scalar(varrow[:], varrow[:], EPS, None, OP.add),
                 reads=["r1"], writes=["r1"])
            tk.v(lambda v: vrecip(v, r0[:], varrow[:]), reads=["r1", "r0"], writes=["r0"])
            tk.s(lambda s: s.activation(invrow[:], r0[:], AF.Sqrt), reads=["r0"], writes=["invrow"])

        def bcast_row(src_row, srcn, bank):
            def f(t, src_row=src_row, bank=bank):
                last = None
                for bq in range(4):
                    last = t.matmul(pb[bank][:, 128 * bq:128 * bq + 128], onesrow[:],
                                    src_row[0:1, 128 * bq:128 * bq + 128],
                                    start=True, stop=True)
                return last
            tk.t(f, reads=[srcn, "onesrow"], writes=[PB[bank]])

        def quant_rows():
            tk.v(lambda v: v.tensor_tensor(mxacc[:], xr3[:, 0, :], xr3[:, 1, :], OP.max),
                 reads=["xr0", "xr1", "tA"], writes=["tA"])
            tk.v(lambda v: v.tensor_tensor(mnacc[:], xr3[:, 0, :], xr3[:, 1, :], OP.min),
                 reads=["xr0", "xr1", "tB"], writes=["tB"])
            tk.v(lambda v: v.tensor_tensor(sqb[:], xr3[:, 2, :], xr3[:, 3, :], OP.max),
                 reads=["xr2", "xr3", "sqb"], writes=["sqb"])
            tk.v(lambda v: v.tensor_tensor(mxacc[:], mxacc[:], sqb[:], OP.max),
                 reads=["tA", "sqb"], writes=["tA"])
            tk.v(lambda v: v.tensor_tensor(sqb[:], xr3[:, 2, :], xr3[:, 3, :], OP.min),
                 reads=["xr2", "xr3", "sqb"], writes=["sqb"])
            tk.v(lambda v: v.tensor_tensor(mnacc[:], mnacc[:], sqb[:], OP.min),
                 reads=["tB", "sqb"], writes=["tB"])
            tk.v(lambda v: v.tensor_tensor(sqb[:], xr3[:, 4, :], xr3[:, 5, :], OP.max),
                 reads=["xr4", "xr5", "sqb"], writes=["sqb"])
            tk.v(lambda v: v.tensor_tensor(mxacc[:], mxacc[:], sqb[:], OP.max),
                 reads=["tA", "sqb"], writes=["tA"])
            tk.v(lambda v: v.tensor_tensor(sqb[:], xr3[:, 4, :], xr3[:, 5, :], OP.min),
                 reads=["xr4", "xr5", "sqb"], writes=["sqb"])
            tk.v(lambda v: v.tensor_tensor(mnacc[:], mnacc[:], sqb[:], OP.min),
                 reads=["tB", "sqb"], writes=["tB"])
            tk.v(lambda v: v.tensor_tensor(sqb[:], xr3[:, 6, :], xr3[:, 7, :], OP.max),
                 reads=["xr6", "xr7", "sqb"], writes=["sqb"])
            tk.v(lambda v: v.tensor_tensor(mxacc[:], mxacc[:], sqb[:], OP.max),
                 reads=["tA", "sqb"], writes=["tA"])
            tk.v(lambda v: v.tensor_tensor(sqb[:], xr3[:, 6, :], xr3[:, 7, :], OP.min),
                 reads=["xr6", "xr7", "sqb"], writes=["sqb"])
            tk.v(lambda v: v.tensor_tensor(mnacc[:], mnacc[:], sqb[:], OP.min),
                 reads=["tB", "sqb"], writes=["tB"])
            def trx(t):
                last = None
                for bq in range(4):
                    last = t.matmul(pb[4][:, 128 * bq:128 * bq + 128],
                                    mxacc[:, 128 * bq:128 * bq + 128], I128f[:],
                                    start=True, stop=True)
                for bq in range(4):
                    last = t.matmul(pb[5][:, 128 * bq:128 * bq + 128],
                                    mnacc[:, 128 * bq:128 * bq + 128], I128f[:],
                                    start=True, stop=True)
                return last
            tk.t(trx, reads=["tA", "tB", "I128f"], writes=[PB[4], PB[5]])
            for bq in range(4):
                tk.v(lambda v, bq=bq: vred(v, mxc[:, bq:bq + 1],
                                                      pb[4][:, 128 * bq:128 * bq + 128], OP.max),
                     reads=[PB[4], "mxc"], writes=["mxc"])
                tk.v(lambda v, bq=bq: vred(v, mnc[:, bq:bq + 1],
                                                      pb[5][:, 128 * bq:128 * bq + 128], OP.min),
                     reads=[PB[5], "mnc"], writes=["mnc"])
            def trmu(t):
                last = None
                for bq in range(4):
                    last = t.matmul(pb[6][:, 4 * bq:4 * bq + 1],
                                    murow[0:1, 128 * bq:128 * bq + 128],
                                    one1[:], start=True, stop=True)
                for bq in range(4):
                    last = t.matmul(pb[7][:, 4 * bq:4 * bq + 1],
                                    invrow[0:1, 128 * bq:128 * bq + 128],
                                    one1[:], start=True, stop=True)
                return last
            tk.t(trmu, reads=["murow", "invrow", "one1"], writes=[PB[6], PB[7]])
            if dbg:
                tk.v(lambda v: v.tensor_copy(yv[:, 0:16], pb[6][:, 0:16]),
                     reads=[PB[6], "yv"], writes=["yv"])
                dump(yv[:, 0:16], "yv")
                tk.v(lambda v: v.tensor_copy(yv[:, 0:16], pb[7][:, 0:16]),
                     reads=[PB[7], "yv"], writes=["yv"])
                dump(yv[:, 0:16], "yv")
            for bq in range(4):
                tk.v(lambda v, bq=bq: v.tensor_copy(muc4[:, bq:bq + 1], pb[6][:, 4 * bq:4 * bq + 1]),
                     reads=[PB[6], "muc4"], writes=["muc4"])
                tk.v(lambda v, bq=bq: v.tensor_copy(ivc4[:, bq:bq + 1], pb[7][:, 4 * bq:4 * bq + 1]),
                     reads=[PB[7], "ivc4"], writes=["ivc4"])
            mu4 = muc4[:]
            iv4 = ivc4[:]
            tk.v(lambda v: v.tensor_tensor(cA[:], mxc[:], mu4, OP.subtract),
                 reads=["mxc", "muc4", "cA"], writes=["cA"])
            tk.v(lambda v: v.tensor_tensor(cB[:], mu4, mnc[:], OP.subtract),
                 reads=["mnc", "muc4", "cB"], writes=["cB"])
            tk.v(lambda v: v.tensor_tensor(cA[:], cA[:], cB[:], OP.max),
                 reads=["cA", "cB"], writes=["cA"])
            tk.v(lambda v: v.tensor_tensor(cA[:], cA[:], iv4, OP.mult),
                 reads=["cA", "ivc4"], writes=["cA"])
            tk.v(lambda v: v.tensor_scalar(cA[:], cA[:], 1e-5, None, OP.max),
                 reads=["cA"], writes=["cA"])
            tk.v(lambda v: vrecip(v, cB[:], cA[:]), reads=["cA", "cB"], writes=["cB"])
            tk.v(lambda v: v.tensor_scalar(cB[:], cB[:], 127.0, None, OP.mult),
                 reads=["cB"], writes=["cB"])
            tk.v(lambda v: v.tensor_scalar(cC[:], cA[:], 1.0 / 127, None, OP.mult),
                 reads=["cA", "cC"], writes=["cC"])
            tk.v(lambda v: v.tensor_tensor(cD[:], iv4, cB[:], OP.mult),
                 reads=["ivc4", "cB", "cD"], writes=["cD"])
            tk.v(lambda v: v.tensor_tensor(cE[:], mu4, cD[:], OP.mult),
                 reads=["muc4", "cD", "cE"], writes=["cE"])
            def c2r(t):
                last = None
                for bq in range(4):
                    last = t.matmul(pb[4][0:1, 128 * bq:128 * bq + 128], cD[:, bq:bq + 1],
                                    I128f[:], start=True, stop=True)
                for bq in range(4):
                    last = t.matmul(pb[5][0:1, 128 * bq:128 * bq + 128], cE[:, bq:bq + 1],
                                    I128f[:], start=True, stop=True)
                for bq in range(4):
                    last = t.matmul(pb[6][0:1, 128 * bq:128 * bq + 128], cC[:, bq:bq + 1],
                                    I128f[:], start=True, stop=True)
                return last
            tk.t(c2r, reads=["cC", "cD", "cE", "I128f"], writes=[PB[4], PB[5], PB[6]])
            tk.v(lambda v: v.tensor_copy(r0[:], pb[4][0:1, :]), reads=[PB[4], "r0"], writes=["r0"])
            tk.v(lambda v: v.tensor_copy(r1[:], pb[5][0:1, :]), reads=[PB[5], "r1"], writes=["r1"])
            tk.v(lambda v: v.tensor_copy(r2[:], pb[6][0:1, :]), reads=[PB[6], "r2"], writes=["r2"])
            dump(mxc[:], "mxc"); dump(mnc[:], "mnc")
            dump(cA[:], "cA"); dump(cB[:], "cB"); dump(cC[:], "cC")
            dump(cD[:], "cD"); dump(cE[:], "cE")

        def wdma(buf, bufn, scr, grp):
            scrn = "scr_g" if scr is sg else "scr_u"
            def f(g, sem, buf=buf, scr=scr, grp=grp):
                b3 = buf[:].rearrange("p (a b) -> p a b", b=512)
                for kf in range(8):
                    g.dma_start(b3[:, kf, :], scr[kf][:, 512 * grp:512 * grp + 512]).then_inc(sem, 16)
            tk.dsync(f, reads=[bufn, scrn], writes=[bufn], ndma=8, lag=True)

        def ddma(buf, bufn, jg):
            def f(g, sem, buf=buf, jg=jg):
                g.dma_start(buf[:], sd[jg]).then_inc(sem, 16)
            tk.dsync(f, reads=[bufn, "scr_d"], writes=[bufn], ndma=1, lag=True)

        def ffn_front():
            ln_stats()
            quant_rows()
            bcast_row(r0, "r0", 2)
            bcast_row(r1, "r1", 3)
            bcast_row(r2, "r2", 4)
            for j in range(8):
                tk.v(lambda v, j=j: v.tensor_tensor(tA[:], xr3[:, j, :], pb[2][:, :], OP.mult),
                     reads=[f"xr{j}", PB[2], "tA"], writes=["tA"])
                tk.v(lambda v: v.tensor_tensor(tA[:], tA[:], pb[3][:, :], OP.subtract),
                     reads=["tA", PB[3]], writes=["tA"])
                tk.v(lambda v: v.tensor_scalar(tA[:], tA[:], MAGIC, MAGIC, OP.add, OP.subtract),
                     reads=["tA"], writes=["tA"])
                tk.v(lambda v, j=j: v.tensor_tensor(qx3[:, j, :], tA[:], pb[4][:, :], OP.mult),
                     reads=["tA", PB[4], f"qx{j}"], writes=[f"qx{j}"])

        def ffn(skip_front=False):
            if not skip_front:
                ffn_front()
            wgb = [wg0, wg1]; wub = [wu0, wu1]
            wgn = ["wg0", "wg1"]; wun = ["wu0", "wu1"]
            wdma(wgb[0], wgn[0], sg, 0)
            wdma(wub[0], wun[0], su, 0)
            for grp in range(8):
                pa = grp % 2
                if grp + 1 < 8:
                    wdma(wgb[1 - pa], wgn[1 - pa], sg, grp + 1)
                    wdma(wub[1 - pa], wun[1 - pa], su, grp + 1)
                if grp == 0:
                    dump(wgb[0][:, 0:512], wgn[0])
                qall = ["qx%d" % _q for _q in range(8)]
                for ob in range(4):
                    gb = (ob % 2) * 2
                    ub = (ob % 2) * 2 + 1
                    if grp == 0 and ob == 0:
                        for kf in range(8):
                            tk.t(lambda t, pa=pa, kf=kf, gb=gb: t.matmul(
                                pb[gb][:, :],
                                wgb[pa][:].rearrange("p (a b) -> p a b", b=512)[:, kf, 0:128],
                                qx3[:, kf, :], start=(kf == 0), stop=(kf == 7)),
                                reads=[wgn[pa], f"qx{kf}"] + ([] if kf == 0 else [PB[gb]]),
                                writes=[PB[gb]])
                        for kf in range(8):
                            tk.t(lambda t, pa=pa, kf=kf, ub=ub: t.matmul(
                                pb[ub][:, :],
                                wub[pa][:].rearrange("p (a b) -> p a b", b=512)[:, kf, 0:128],
                                qx3[:, kf, :], start=(kf == 0), stop=(kf == 7)),
                                reads=[wun[pa], f"qx{kf}"] + ([] if kf == 0 else [PB[ub]]),
                                writes=[PB[ub]])
                        idx = 0
                        tk.s(lambda s, gb=gb: s.activation(gsil[:], pb[gb][:, :], AF.Silu),
                             reads=[PB[gb], "gsil"], writes=["gsil"])
                        tk.v(lambda v, ub=ub: v.tensor_tensor(gu3[:, 0, :], gsil[:],
                                                              pb[ub][:, :], OP.mult),
                             reads=["gsil", PB[ub], "gu"], writes=["gu"])
                        ab = tB
                        tk.s(lambda s: s.activation(tB[:], gu3[:, 0, :], AF.Abs),
                             reads=["gu", "tB"], writes=["tB"])
                        tk.v(lambda v: v.tensor_copy(m2acc[:], tB[:]),
                             reads=["tB", "m2acc"], writes=["m2acc"])
                        continue
                    def mmg(t, pa=pa, ob=ob, gb=gb):
                        w3 = wgb[pa][:].rearrange("p (a b) -> p a b", b=512)
                        last = None
                        for kf in range(8):
                            last = t.matmul(pb[gb][:, :], w3[:, kf, 128 * ob:128 * ob + 128],
                                            qx3[:, kf, :], start=(kf == 0), stop=(kf == 7))
                        return last
                    tk.t(mmg, reads=[wgn[pa]] + qall, writes=[PB[gb]])
                    def mmu(t, pa=pa, ob=ob, ub=ub):
                        w3 = wub[pa][:].rearrange("p (a b) -> p a b", b=512)
                        last = None
                        for kf in range(8):
                            last = t.matmul(pb[ub][:, :], w3[:, kf, 128 * ob:128 * ob + 128],
                                            qx3[:, kf, :], start=(kf == 0), stop=(kf == 7))
                        return last
                    tk.t(mmu, reads=[wun[pa]] + qall, writes=[PB[ub]])
                    idx = grp * 4 + ob
                    tk.s(lambda s, gb=gb: s.activation(gsil[:], pb[gb][:, :], AF.Silu),
                         reads=[PB[gb], "gsil"], writes=["gsil"])
                    tk.v(lambda v, idx=idx, ub=ub: v.tensor_tensor(gu3[:, idx, :], gsil[:],
                                                                   pb[ub][:, :], OP.mult),
                         reads=["gsil", PB[ub], "gu"], writes=["gu"])
                    ab = sqb if idx % 2 else tB
                    abn = "sqb" if idx % 2 else "tB"
                    tk.s(lambda s, idx=idx, ab=ab: s.activation(ab[:], gu3[:, idx, :], AF.Abs),
                         reads=["gu", abn], writes=[abn])
                    if idx == 0:
                        tk.v(lambda v, ab=ab: v.tensor_copy(m2acc[:], ab[:]),
                             reads=[abn, "m2acc"], writes=["m2acc"])
                    else:
                        tk.v(lambda v, ab=ab: v.tensor_tensor(m2acc[:], m2acc[:], ab[:], OP.max),
                             reads=["m2acc", abn], writes=["m2acc"])
            dump(gu3[:, 0, :], "gu"); dump(gu3[:, 31, :], "gu"); dump(m2acc[:], "m2acc")
            def trm2(t):
                last = None
                for bq in range(4):
                    last = t.matmul(pb[4][:, 128 * bq:128 * bq + 128],
                                    m2acc[:, 128 * bq:128 * bq + 128], I128f[:],
                                    start=True, stop=True)
                return last
            tk.t(trm2, reads=["m2acc", "I128f"], writes=[PB[4]])
            for bq in range(4):
                tk.v(lambda v, bq=bq: vred(v, mxc[:, bq:bq + 1],
                                                      pb[4][:, 128 * bq:128 * bq + 128], OP.max),
                     reads=[PB[4], "mxc"], writes=["mxc"])
            tk.v(lambda v: v.tensor_scalar(cA[:], mxc[:], 1e-5, None, OP.max),
                 reads=["mxc", "cA"], writes=["cA"])
            tk.v(lambda v: vrecip(v, cB[:], cA[:]), reads=["cA", "cB"], writes=["cB"])
            tk.v(lambda v: v.tensor_scalar(cB[:], cB[:], 127.0, None, OP.mult),
                 reads=["cB"], writes=["cB"])
            tk.v(lambda v: v.tensor_scalar(cC[:], cA[:], 1.0 / 127, None, OP.mult),
                 reads=["cA", "cC"], writes=["cC"])
            def c2r2(t):
                last = None
                for bq in range(4):
                    last = t.matmul(pb[4][0:1, 128 * bq:128 * bq + 128], cB[:, bq:bq + 1],
                                    I128f[:], start=True, stop=True)
                for bq in range(4):
                    last = t.matmul(pb[5][0:1, 128 * bq:128 * bq + 128], cC[:, bq:bq + 1],
                                    I128f[:], start=True, stop=True)
                return last
            tk.t(c2r2, reads=["cB", "cC", "I128f"], writes=[PB[4], PB[5]])
            tk.v(lambda v: v.tensor_copy(r0[:], pb[4][0:1, :]), reads=[PB[4], "r0"], writes=["r0"])
            tk.v(lambda v: v.tensor_copy(r1[:], pb[5][0:1, :]), reads=[PB[5], "r1"], writes=["r1"])
            bcast_row(r0, "r0", 4)
            bcast_row(r1, "r1", 5)
            tk.v(lambda v: v.tensor_copy(s2bc[:], pb[4][:, :]), reads=[PB[4], "s2bc"], writes=["s2bc"])
            tk.v(lambda v: v.tensor_copy(is2bc[:], pb[5][:, :]), reads=[PB[5], "is2bc"], writes=["is2bc"])
            dump(s2bc[:], "s2bc"); dump(is2bc[:], "is2bc")
            ddma(wg0, "wg0", 0)
            for jg in range(8):
                pa = jg % 2
                if jg + 1 < 8:
                    ddma(wgb[1 - pa], wgn[1 - pa], jg + 1)
                for jj in range(4):
                    j = 4 * jg + jj
                    qb = j % 2
                    tk.v(lambda v, j=j: v.tensor_tensor(tB[:], gu3[:, j, :], s2bc[:], OP.mult),
                         reads=["gu", "s2bc", "tB"], writes=["tB"])
                    tk.v(lambda v, qb=qb: v.tensor_scalar(qgu[qb][:], tB[:], MAGIC, MAGIC,
                                                          OP.add, OP.subtract),
                         reads=["tB", f"qgu{qb}"], writes=[f"qgu{qb}"])
                    def mmd(t, pa=pa, jj=jj, j=j, qb=qb):
                        w3 = wgb[pa][:].rearrange("p (a b) -> p a b", b=1024)
                        last = None
                        for o in range(8):
                            last = t.matmul(pb[o][:, :], w3[:, jj, 128 * o:128 * o + 128],
                                            qgu[qb][:], start=(j == 0), stop=(j == 31))
                        return last
                    tk.t(mmd, reads=[wgn[pa], f"qgu{qb}"], writes=list(PB))
            for o in range(8):
                tk.v(lambda v, o=o: v.tensor_tensor(tA[:], pb[o][:, :], is2bc[:], OP.mult),
                     reads=[PB[o], "is2bc", "tA"], writes=["tA"])
                tk.v(lambda v, o=o: v.tensor_tensor(xr3[:, o, :], xr3[:, o, :], tA[:], OP.add),
                     reads=["tA", f"xr{o}"], writes=[f"xr{o}"])

        def wWdma(buf, bufn, kd):
            def f(g, sem, buf=buf, kd=kd):
                g.dma_start(buf[:, 0:1024], sW[kd]).then_inc(sem, 16)
            tk.dsync(f, reads=[bufn, "scr_W"], writes=[bufn], ndma=1, lag=True)

        def attn():
            ln_stats()
            tk.v(lambda v: v.tensor_tensor(r0[:], murow[:], invrow[:], OP.mult),
                 reads=["murow", "invrow", "r0"], writes=["r0"])
            bcast_row(invrow, "invrow", 2)
            bcast_row(r0, "r0", 3)
            for j in range(8):
                tk.v(lambda v, j=j: v.tensor_tensor(tA[:], xr3[:, j, :], pb[2][:, :], OP.mult),
                     reads=[f"xr{j}", PB[2], "tA"], writes=["tA"])
                tk.v(lambda v: v.tensor_tensor(tA[:], tA[:], pb[3][:, :], OP.subtract),
                     reads=["tA", PB[3]], writes=["tA"])
                tk.s(lambda s, j=j: s.activation(xa3[:, j, :], tA[:], AF.Tanh),
                     reads=["tA", f"qx{j}"], writes=[f"qx{j}"])
            wgb = [wg0, wg1]; wgn = ["wg0", "wg1"]
            wWdma(wgb[0], wgn[0], 0)
            for kd in range(8):
                pa = kd % 2
                if kd + 1 < 8:
                    wWdma(wgb[1 - pa], wgn[1 - pa], kd + 1)
                def mstk(t, pa=pa, kd=kd):
                    last = None
                    for eb in range(8):
                        last = t.matmul(pb[eb][:, :],
                                        wgb[pa][:, 128 * eb:128 * eb + 128],
                                        xa3[:, kd, :], start=(kd == 0), stop=(kd == 7))
                    return last
                tk.t(mstk, reads=[wgn[pa], f"qx{kd}"], writes=list(PB))
            for eb in range(8):
                tk.s(lambda s, eb=eb: s.copy(st3[:, eb, :], pb[eb][:, :]),
                     reads=[PB[eb], "stat"], writes=["stat"])
            if delta_mode == "none":
                return
            for f in d_front_pieces(0):
                f()
            for c in range(NCH):
                ss = d_solve_stages(c)
                tp = d_tail_pieces(c - 1) if c >= 1 else [None] * 6
                fp = d_front_pieces(c + 1) if c + 1 < NCH else [None] * 3
                order = [ss[0], tp[0], ss[1], tp[1], ss[2], tp[2], ss[3],
                         tp[3], ss[4], fp[0], tp[4], ss[5], fp[1], tp[5],
                         ss[6], fp[2]]
                for f in order:
                    if f is not None:
                        f()
            for f in d_tail_pieces(NCH - 1):
                f()

        def _acc(c):
            cc = slice(64 * c, 64 * c + 64)
            def quad16(t, lhs_of, rhs_of, outp, start=True, stop=True):
                last = None
                for p in range(8):
                    for hh in range(2):
                        last = t.matmul(outp(p, hh), lhs_of(p, hh), rhs_of(p, hh),
                                        start=start, stop=stop,
                                        tile_position=(64 * hh, 64 * hh))
                return last
            def pairs(buf, w=64):
                return lambda p, hh, buf=buf, w=w: buf[64 * hh:64 * hh + 64, w * p:w * p + w]
            def ppsum(bank, w=64):
                return lambda p, hh, bank=bank, w=w: pb[bank][64 * hh:64 * hh + 64, w * p:w * p + w]
            kslice = lambda p, hh, cc=cc: xa3[:, p, cc][64 * hh:64 * hh + 64, :]
            sslice = lambda p, hh, cc=cc: st3[:, p, cc][64 * hh:64 * hh + 64, :]
            i32f = lambda p, hh: I128f[64 * hh:64 * hh + 64, 64 * hh:64 * hh + 64]
            i32b = lambda p, hh: I128b[64 * hh:64 * hh + 64, 64 * hh:64 * hh + 64]
            return cc, quad16, pairs, ppsum, kslice, sslice, i32f, i32b

        def d_front_pieces(c):
            pc = c % 2
            cc, quad16, pairs, ppsum, kslice, sslice, i32f, i32b = _acc(c)
            def F1():
                _front_f1(c, pc, quad16, ppsum, kslice)
            def F2():
                _front_f2(c, pc, quad16, ppsum, kslice, i32b)
            def F3():
                _front_f3(c, pc, quad16, ppsum, sslice, i32f)
            return [F1, F2, F3]

        def _front_f1(c, pc, quad16, ppsum, kslice):
            tk.t(lambda t, k=kslice, q=quad16, o=ppsum(0): q(t, k, k, o),
                 reads=["qx%d" % _q for _q in range(8)], writes=[PB[0]])
            tk.v(lambda v: v.tensor_tensor(Nb[0][:], pb[0][:, :], P_LOW[:], OP.mult),
                 reads=[PB[0], "P_LOW", "Nb0"], writes=["Nb0"])
            tk.v(lambda v: v.tensor_tensor(NT1[:], pb[0][:, :], P_UP[:], OP.mult),
                 reads=[PB[0], "P_UP", "NT1"], writes=["NT1"])
            tk.v(lambda v, pc=pc: v.tensor_tensor(Mqb[pc][:], pb[0][:, :], P_UQI[:], OP.mult),
                 reads=[PB[0], "P_UQI", f"Mqb{pc}"], writes=[f"Mqb{pc}"])
            tk.v(lambda v: v.tensor_tensor(FT1n[:], identp[:], NT1[:], OP.subtract),
                 reads=["identp", "NT1", "FT1n"], writes=["FT1n"])

        def _front_f2(c, pc, quad16, ppsum, kslice, i32b):
            tk.t(lambda t, k=kslice, i=i32b, q=quad16, o=ppsum(0): q(t, k, i, o),
                 reads=["qx%d" % _q for _q in range(8)] + ["I128b", PB[0]], writes=[PB[0]])
            tk.v(lambda v, pc=pc: v.tensor_scalar(KTb[pc][:], pb[0][:, :], lrevb[:], None, OP.mult),
                 reads=[PB[0], "lrevb", f"KTb{pc}"], writes=[f"KTb{pc}"])
            x0k = X0[pc][:].rearrange("p (a b) -> p a b", b=128)[:, :, 0:64]
            tk.v(lambda v, pc=pc, x0k=x0k: v.tensor_scalar(x0k, pb[0][:, :], lamcol[:], None, OP.mult),
                 reads=[PB[0], "lamcol", f"X0_{pc}"], writes=[f"X0_{pc}"])

        def _front_f3(c, pc, quad16, ppsum, sslice, i32f):
            tk.t(lambda t, s=sslice, i=i32f, q=quad16, o=ppsum(0): q(t, s, i, o),
                 reads=["stat", "I128f", PB[0]], writes=[PB[0]])
            x0v = X0[pc][:].rearrange("p (a b) -> p a b", b=128)[:, :, 64:128]
            tk.s(lambda s, pc=pc, x0v=x0v: s.copy(x0v, pb[0][:, :]),
                 reads=[PB[0], f"X0_{pc}"], writes=[f"X0_{pc}"])

        def d_solve_stages(c):
            pc = c % 2
            cc, quad16, pairs, ppsum, kslice, sslice, i32f, i32b = _acc(c)
            stages = []
            xseq = [(X0[pc], f"X0_{pc}"), (Xw[0], "Xw0"), (Xw[1], "Xw1"),
                    (Xw[0], "Xw0"), (Xw[1], "Xw1"), (Xw[0], "Xw0")]
            def app(t, ft, ftn, xi):
                xcur, xcn = xseq[xi]
                q = quad16; pr = pairs
                return q(t, pr(ft[:]), pr(xcur[:], 128),
                         lambda p, hh: pb[5 + p // 4][64 * hh:64 * hh + 64,
                                                      128 * (p % 4):128 * (p % 4) + 128])
            def stage_k(k):
                if k < 5:
                    ntp = NT1 if k == 0 else NTw[(k - 1) % 2]
                    ntpn = "NT1" if k == 0 else f"NTw{(k - 1) % 2}"
                    ncur = k % 2
                    tk.t(lambda t, ntp=ntp, ncur=ncur, q=quad16, pr=pairs, o=ppsum(3):
                         q(t, pr(ntp[:]), pr(Nb[ncur][:]), o),
                         reads=[ntpn, f"Nb{ncur}", PB[3]], writes=[PB[3]])
                    tk.t(lambda t, ntp=ntp, ncur=ncur, q=quad16, pr=pairs, o=ppsum(4):
                         q(t, pr(Nb[ncur][:]), pr(ntp[:]), o),
                         reads=[ntpn, f"Nb{ncur}", PB[4]], writes=[PB[4]])
                ft, ftn = (FT1n, "FT1n") if k == 0 else (FTS[k - 1], f"FTS{k - 1}")
                xcur, xcn = xseq[k]
                tk.t(lambda t, ft=ft, ftn=ftn, k=k: app(t, ft, ftn, k),
                     reads=[ftn, xcn, PB[5], PB[6]], writes=[PB[5], PB[6]])
                if k < 5:
                    if k < 4:
                        tk.v(lambda v, ncur=ncur: v.tensor_copy(Nb[1 - ncur][:], pb[3][:, :]),
                             reads=[PB[3], f"Nb{1 - ncur}"], writes=[f"Nb{1 - ncur}"])
                        tk.s(lambda s, k=k: s.copy(NTw[k % 2][:], pb[4][:, :]),
                             reads=[PB[4], f"NTw{k % 2}"], writes=[f"NTw{k % 2}"])
                    tk.v(lambda v, k=k: v.tensor_tensor(FTS[k][:], pb[4][:, :], identp[:], OP.add),
                         reads=[PB[4], "identp", f"FTS{k}"], writes=[f"FTS{k}"])
                    xnext, xnn = xseq[k + 1]
                    tk.v(lambda v, xnext=xnext: v.tensor_copy(xnext[:, 0:512], pb[5][:, :]),
                         reads=[PB[5], xnn], writes=[xnn])
                    tk.s(lambda s, xnext=xnext: s.copy(xnext[:, 512:1024], pb[6][:, :]),
                         reads=[PB[6], xnn], writes=[xnn])
                else:
                    for bi in range(2):
                        kview = pb[5 + bi][:].rearrange("p (a b) -> p a b", b=128)[:, :, 0:64]
                        vview = pb[5 + bi][:].rearrange("p (a b) -> p a b", b=128)[:, :, 64:128]
                        tk.v(lambda v, pc=pc, kview=kview, bi=bi: v.tensor_copy(
                            Bcb[pc][:, 256 * bi:256 * bi + 256], kview),
                             reads=[PB[5 + bi], f"Bcb{pc}"], writes=[f"Bcb{pc}"])
                        tk.s(lambda s, pc=pc, vview=vview, bi=bi: s.copy(
                            ULc[pc][:, 256 * bi:256 * bi + 256], vview),
                             reads=[PB[5 + bi], f"ULc{pc}"], writes=[f"ULc{pc}"])
            def stage_bbt():
                tk.t(lambda t, pc=pc, q=quad16, pr=pairs, i=i32b, o=ppsum(3):
                     q(t, pr(Bcb[pc][:]), i, o),
                     reads=[f"Bcb{pc}", "I128b", PB[3]], writes=[PB[3]])
                tk.v(lambda v, pc=pc: v.tensor_scalar(BBTn[pc][:], pb[3][:, :], -1.0, None, OP.mult),
                     reads=[PB[3], f"BBTn{pc}"], writes=[f"BBTn{pc}"])
            for k in range(6):
                stages.append(lambda k=k: stage_k(k))
            stages.append(stage_bbt)
            return stages

        def d_tail_pieces(c):
            pc = c % 2
            cc, quad16, pairs, ppsum, kslice, sslice, i32f, i32b = _acc(c)
            def T1():
                tk.t(lambda t, k=kslice, q=quad16, pr=pairs, o=ppsum(1):
                     q(t, k, pr(Sbf[:]), o),
                     reads=["qx%d" % _q for _q in range(8)] + ["Sbf", PB[1]], writes=[PB[1]])
                x0v = X0[pc][:].rearrange("p (a b) -> p a b", b=128)[:, :, 64:128]
                tk.v(lambda v, x0v=x0v: v.scalar_tensor_tensor(ypart[:], pb[1][:, :], lamcol[:],
                                                               x0v, OP.mult, OP.add),
                     reads=[PB[1], "lamcol", f"X0_{pc}", "ypart"], writes=["ypart"])
            def T2():
                def f_U(t, pc=pc):
                    t.matmul(pb[2][:, :], I128b[:], ULc[pc][:], start=True, stop=False)
                    last = None
                    for p in range(8):
                        for hh in range(2):
                            stop = (p == 7 and hh == 1)
                            last = t.matmul(pb[2][64 * hh:64 * hh + 64, 64 * p:64 * p + 64],
                                            BBTn[pc][64 * hh:64 * hh + 64, 64 * p:64 * p + 64],
                                            Sbf[64 * hh:64 * hh + 64, 64 * p:64 * p + 64],
                                            start=False, stop=stop,
                                            tile_position=(64 * hh, 64 * hh))
                    return last
                tk.t(f_U, reads=[f"ULc{pc}", f"BBTn{pc}", "Sbf", "I128b", PB[2]], writes=[PB[2]])
                tk.v(lambda v: v.tensor_copy(Ub[:], pb[2][:, :]), reads=[PB[2], "Ub"], writes=["Ub"])
            def T3():
                def f_dS(t, pc=pc):
                    last = t.matmul(pb[7][:, :], lCI[:], Sbf[:], start=True, stop=False)
                    for p in range(8):
                        for hh in range(2):
                            stop = (p == 7 and hh == 1)
                            last = t.matmul(pb[7][64 * hh:64 * hh + 64, 64 * p:64 * p + 64],
                                            KTb[pc][64 * hh:64 * hh + 64, 64 * p:64 * p + 64],
                                            Ub[64 * hh:64 * hh + 64, 64 * p:64 * p + 64],
                                            start=False, stop=stop,
                                            tile_position=(64 * hh, 64 * hh))
                    return last
                tk.t(f_dS, reads=["lCI", "Sbf", f"KTb{pc}", "Ub", PB[7]], writes=[PB[7]])
                tk.s(lambda s: s.copy(Sbf[:], pb[7][:, :]), reads=[PB[7], "Sbf"], writes=["Sbf"])
            def T4():
                tk.t(lambda t, pc=pc, q=quad16, pr=pairs, o=ppsum(1):
                     q(t, pr(Mqb[pc][:]), pr(Ub[:]), o),
                     reads=[f"Mqb{pc}", "Ub", PB[1], "ypart"], writes=[PB[1]])
                tk.v(lambda v: v.tensor_tensor(yv[:], ypart[:], pb[1][:, :], OP.add),
                     reads=["ypart", PB[1], "yv"], writes=["yv"])
            def T5():
                tk.v(lambda v: vred(v, tc0[:], yv[:], OP.add),
                     reads=["yv", "tc0"], writes=["tc0"])
                tk.s(lambda s: s.activation(sqb[:], yv[:], AF.Square), reads=["yv", "sqb"], writes=["sqb"])
                tk.v(lambda v: vred(v, tc1[:], sqb[:], OP.add),
                     reads=["sqb", "tc1"], writes=["tc1"])
                def f_sel(t):
                    t.matmul(pb[7][:, 0:1], seldup[:], tc0[:], start=True, stop=True)
                    return t.matmul(pb[7][:, 4:5], seldup[:], tc1[:], start=True, stop=True)
                tk.t(f_sel, reads=["seldup", "tc0", "tc1", PB[7]], writes=[PB[7]])
                tk.v(lambda v: v.tensor_scalar(muc[:], pb[7][:, 0:1], 1.0 / D, None, OP.mult),
                     reads=[PB[7], "muc"], writes=["muc"])
                tk.v(lambda v: v.tensor_scalar(tc1[:], pb[7][:, 4:5], 1.0 / D, None, OP.mult),
                     reads=[PB[7], "tc1"], writes=["tc1"])
                tk.v(lambda v: v.tensor_tensor(tc0[:], muc[:], muc[:], OP.mult),
                     reads=["muc", "tc0"], writes=["tc0"])
                tk.v(lambda v: v.tensor_tensor(tc1[:], tc1[:], tc0[:], OP.subtract),
                     reads=["tc0", "tc1"], writes=["tc1"])
                tk.v(lambda v: v.tensor_scalar(tc1[:], tc1[:], EPS, None, OP.add),
                     reads=["tc1"], writes=["tc1"])
                tk.v(lambda v: vrecip(v, tc0[:], tc1[:]), reads=["tc0", "tc1"], writes=["tc0"])
                tk.s(lambda s: s.activation(invc[:], tc0[:], AF.Sqrt), reads=["tc0", "invc"], writes=["invc"])
                tk.v(lambda v: v.tensor_scalar(muc[:], muc[:], -1.0, None, OP.mult),
                     reads=["muc"], writes=["muc"])
                tk.v(lambda v: v.tensor_scalar(yv[:], yv[:], muc[:], invc[:], OP.add, OP.mult),
                     reads=["yv", "muc", "invc"], writes=["yv"])
                tk.v(lambda v: v.tensor_tensor(yv[:], yv[:], gdl[:], OP.mult),
                     reads=["yv", "gdl"], writes=["yv"])
                tk.v(lambda v: v.tensor_tensor(ao[:], yv[:], bdl[:], OP.add),
                     reads=["yv", "bdl"], writes=["yv"])
            def T6():
                tk.t(lambda t, i=i32f, q=quad16, pr=pairs, o=ppsum(7): q(t, pr(ao[:]), i, o),
                     reads=["yv", "I128f", PB[7]], writes=[PB[7]])
                xrv = xr3[:, :, cc]
                pbv = pb[7][:].rearrange("p (a b) -> p a b", b=64)
                xrall = ["xr%d" % _j for _j in range(8)]
                tk.v(lambda v, xrv=xrv, pbv=pbv: v.tensor_tensor(xrv, xrv, pbv, OP.add),
                     reads=[PB[7]] + xrall, writes=xrall)
            return [T1, T2, T3, T4, T5, T6]

        def xload(seq, t0):
            for h in range(2):
                def f(g, sem, seq=seq, t0=t0, h=h):
                    for j in range(4 * h, 4 * h + 4):
                        g.dma_start(xr3[:, j, :],
                                    dx[seq][128 * j:128 * j + 128, t0:t0 + T]
                                    ).then_inc(sem, 16)
                tk.dg(f, reads=[f"xr{j}" for j in range(4 * h, 4 * h + 4)],
                      writes=[f"xr{j}" for j in range(4 * h, 4 * h + 4)], ndma=4)

        def xstore(seq, t0):
            for h in range(2):
                def f(g, sem, seq=seq, t0=t0, h=h):
                    for j in range(4 * h, 4 * h + 4):
                        g.dma_start(dout[seq][128 * j:128 * j + 128, t0:t0 + T],
                                    xr3[:, j, :]).then_inc(sem, 16)
                tk.dg(f, reads=[f"xr{j}" for j in range(4 * h, 4 * h + 4)], ndma=4)

        xload(0, 0)
        if do_f1:
            ffn_front()
        prep_all()

        for seq in range(n_seq):
            tk.v(lambda v: v.memset(Sbf[:], 0.0), reads=["Sbf"], writes=["Sbf"])
            for ti in range(n_tiles):
                t0 = T * ti
                if not (seq == 0 and ti == 0):
                    xload(seq, t0)
                if do_f1:
                    ffn(skip_front=(seq == 0 and ti == 0))
                if do_attn:
                    attn()
                if do_f2:
                    ffn()
                xstore(seq, t0)

        TINY = frozenset(["muc", "invc", "tc0", "tc1", "cA", "cB", "cC", "cD",
                          "cE", "mxc", "mnc", "muc4", "ivc4", "acc_col",
                          "red_col", "swcol", "dqcol", "etacol", "zcol",
                          "betacol", "nlamcol", "lamcol", "lrevb", "lCcol",
                          "escal", "zscal", "loglam"])
        tk.finalize(tiny=TINY)
        tk.emit(nc, sems)
    return nc


_NC_CACHE = {}

def _consts():
    I128 = np.eye(128, dtype=np.float32)
    ONESROW = np.ones((1, 128), np.float32)
    ONE1 = np.ones((1, 1), np.float32)
    ONESCOL = np.ones((128, 1), np.float32)
    q = np.arange(128)
    SELDUP = ((q[:, None] % 64) == (q[None, :] % 64)).astype(np.float32)
    pm = np.arange(128) % 64
    fm = np.arange(512) % 64
    PH, FH = np.meshgrid(pm, fm, indexing="ij")
    DL = np.where(PH > FH, (PH - FH), 0).astype(np.float32)
    ML = (PH > FH).astype(np.float32)
    DU = np.where(FH > PH, (FH - PH), 0).astype(np.float32)
    MU = (FH > PH).astype(np.float32)
    DQ = np.where(FH >= PH, (FH - PH), 0).astype(np.float32)
    MQ = (FH >= PH).astype(np.float32)
    IOT = (np.arange(128) % 64 + 1).astype(np.float32).reshape(128, 1)
    REV = (63 - np.arange(128) % 64).astype(np.float32).reshape(128, 1)
    CC = np.full((128, 1), 64.0, np.float32)
    IDP = ((np.arange(128) % 64)[:, None] == (np.arange(512) % 64)[None, :]).astype(np.float32)
    return dict(I128=I128, ONESROW=ONESROW, ONE1=ONE1, ONESCOL=ONESCOL,
                SELDUP=SELDUP, DLm=DL, MLm=ML, DUm=DU, MUm=MU, DQm=DQ, MQm=MQ,
                IOT=IOT, REV=REV, CC=CC, IDP=IDP)


def make_inputs(L0_attn_x, L0_attn_W, L0_attn_ln_gamma, L0_attn_ln_beta,
                L0_attn_eta, L0_attn_lam_logit, L0_ffn_w_gate, L0_ffn_w_up,
                L0_ffn_w_down):
    x = np.ascontiguousarray(np.asarray(L0_attn_x, np.float32).transpose(0, 2, 1))
    wgT = np.ascontiguousarray(np.asarray(L0_ffn_w_gate, np.float32).T)
    wuT = np.ascontiguousarray(np.asarray(L0_ffn_w_up, np.float32).T)
    wdT = np.ascontiguousarray(np.asarray(L0_ffn_w_down, np.float32).T)
    WT = np.ascontiguousarray(np.asarray(L0_attn_W, np.float32).T)
    gamma = np.asarray(L0_attn_ln_gamma, np.float32)
    betal = np.asarray(L0_attn_ln_beta, np.float32)
    g3 = gamma.reshape(8, 2, 64)
    b3 = betal.reshape(8, 2, 64)
    gdl = np.concatenate([
        np.broadcast_to(g3[:, 0, :].reshape(1, 512), (64, 512)),
        np.broadcast_to(g3[:, 1, :].reshape(1, 512), (64, 512))], 0).astype(np.float32)
    bdl = np.concatenate([
        np.broadcast_to(b3[:, 0, :].reshape(1, 512), (64, 512)),
        np.broadcast_to(b3[:, 1, :].reshape(1, 512), (64, 512))], 0).astype(np.float32)
    eta1 = np.asarray(L0_attn_eta, np.float32).reshape(1, 1)
    lamz = np.asarray(L0_attn_lam_logit, np.float32).reshape(1, 1)
    base = dict(wgT=wgT, wuT=wuT, wdT=wdT, WT=WT, gdl=gdl, bdl=bdl,
                eta1=eta1, lamz=lamz, **_consts())
    return x, base


def kernel(**inputs):
    x, base = make_inputs(**inputs)
    if "nc" not in _NC_CACHE:
        _NC_CACHE["nc"] = build_nc()
    nc = _NC_CACHE["nc"]
    in_maps = [dict(base, x=np.ascontiguousarray(x[2 * c:2 * c + 2]))
               for c in range(NCORES)]
    res = run_bass_kernel_spmd(nc, in_maps, core_ids=list(range(NCORES)))
    out = np.concatenate([r["out"] for r in res.results], 0)
    return np.ascontiguousarray(out.transpose(0, 2, 1))


# version 23
# speedup vs baseline: 1.0223x; 1.0018x over previous
import numpy as np
import ml_dtypes
from contextlib import ExitStack

import concourse.bass as bass
import concourse.mybir as mybir
from concourse.bass_utils import run_bass_kernel_spmd
from concourse.alu_op_type import AluOpType as OP

F32 = mybir.dt.float32
BF16 = mybir.dt.bfloat16
AF = mybir.ActivationFunctionType
AX = mybir.AxisListType.X
MAGIC = float(np.float32(1.5 * 2 ** 23))

NCORES = 8
L = 2048
D = 1024
T = 512
NTILE = L // T
C = 64
NCH = T // C
INNER = 4096
EPS = 1e-5

class Step:
    __slots__ = ("eng", "fn", "reads", "writes", "dma", "idx", "inc_val",
                 "waits", "needed", "ndma", "drain_after", "gate_id",
                 "lag", "parity", "gate_for")
    def __init__(self, eng, fn, reads, writes, dma, idx, ndma, lag=False):
        self.eng = eng; self.fn = fn; self.reads = tuple(reads)
        self.writes = tuple(writes); self.dma = dma; self.idx = idx
        self.inc_val = None; self.waits = []; self.needed = False
        self.ndma = ndma; self.drain_after = False; self.gate_id = None
        self.lag = lag; self.parity = 0; self.gate_for = None

class TK:
    def __init__(self):
        self.steps = []

    def step(self, eng, fn, reads=(), writes=(), dma=False, ndma=1, lag=False):
        s = Step(eng, fn, reads, writes, dma, len(self.steps), ndma, lag)
        self.steps.append(s)
        return s

    def t(self, fn, reads=(), writes=()):  return self.step("tensor", fn, reads, writes)
    def v(self, fn, reads=(), writes=()):  return self.step("vector", fn, reads, writes)
    def s(self, fn, reads=(), writes=()):  return self.step("scalar", fn, reads, writes)
    def g(self, fn, reads=(), writes=()):  return self.step("gpsimd", fn, reads, writes)
    def dsync(self, fn, reads=(), writes=(), ndma=1, lag=False):
        s = self.step("sync", fn, reads, writes, dma=True, ndma=ndma, lag=lag)
        g = self.step("gpsimd", None)
        g.gate_for = s
        return s
    def dg(self, fn, reads=(), writes=(), ndma=1):
        return self.step("gpsimd", fn, reads, writes, dma=True, ndma=ndma)

    def finalize(self, tiny=frozenset()):
        last_writer = {}
        readers = {}
        deps_of = []
        for s in self.steps:
            deps = set()
            for r in s.reads:
                lw = last_writer.get(r)
                if lw is not None:
                    deps.add(lw)
            for w in s.writes:
                lw = last_writer.get(w)
                if lw is not None:
                    deps.add(lw)
                deps.update(readers.get(w, ()))
            for r in s.reads:
                if r.startswith("pb"):
                    for pr_ in readers.get(r, ()):
                        if self.steps[pr_].eng != s.eng:
                            deps.add(pr_)
            pruned = set()
            for d in deps:
                ds = self.steps[d]
                if not ds.dma and any(w in tiny and w in s.reads for w in ds.writes):
                    ds.drain_after = True
                if ds.eng == s.eng and not ds.dma and not s.dma:
                    continue
                pruned.add(d)
            deps_of.append(pruned)
            for d in pruned:
                self.steps[d].needed = True
            for r in s.reads:
                readers.setdefault(r, set()).add(s.idx)
            for w in s.writes:
                last_writer[w] = s.idx
                readers[w] = {s.idx}
        cnt = {}
        qord = {}
        for s in self.steps:
            if s.dma:
                s.parity = qord.get(s.eng, 0) % 4
                qord[s.eng] = qord.get(s.eng, 0) + 1
                key = f"q_{s.eng}{s.parity}"
                cnt[key] = cnt.get(key, 0) + 16 * s.ndma
                s.inc_val = cnt[key]
                s.needed = True
                gk = "g_gate" if s.eng == "gpsimd" else "s_gate"
                cnt[gk] = cnt.get(gk, 0) + 1
                s.gate_id = cnt[gk]
            elif s.needed:
                key = s.eng
                cnt[key] = cnt.get(key, 0) + 1
                s.inc_val = cnt[key]
        waited = {}
        for s, deps in zip(self.steps, deps_of):
            for d in sorted(deps):
                ds = self.steps[d]
                if ds.inc_val is None:
                    continue
                if ds.gate_id is not None:
                    key = "g_gate" if ds.eng == "gpsimd" else "s_gate"
                    val = ds.gate_id
                else:
                    key, val = (("q_" + ds.eng) if ds.dma else ds.eng), ds.inc_val
                wkey = (s.eng, key)
                if waited.get(wkey, -1) >= val:
                    continue
                waited[wkey] = val
                s.waits.append((key, val))
        self.final_counts = cnt

    def emit(self, nc, sems):
        by_eng = {}
        for s in self.steps:
            by_eng.setdefault(s.eng, []).append(s)
        with nc.Block() as block:
            def section(eng_name):
                def body(eng):
                    inflight = []
                    def flush_gate(ps):
                        gk = "g_gate" if ps.eng == "gpsimd" else "s_gate"
                        eng.wait_ge(sems[f"q_{ps.eng}{ps.parity}"], ps.inc_val)
                        eng.nop().then_inc(sems[gk], 1)
                    for s in by_eng.get(eng_name, ()):
                        if s.gate_for is not None:
                            flush_gate(s.gate_for)
                            continue
                        for key, val in s.waits:
                            eng.wait_ge(sems[key], val)
                        if s.dma:
                            if s.eng == "sync":
                                while len(inflight) >= 4:
                                    eng.wait_ge(sems["s_gate"],
                                                inflight.pop(0).gate_id)
                                s.fn(eng, sems[f"q_{s.eng}{s.parity}"])
                                inflight.append(s)
                            else:
                                s.fn(eng, sems[f"q_{s.eng}{s.parity}"])
                                flush_gate(s)
                        else:
                            inst = s.fn(eng)
                            if s.drain_after:
                                eng.drain()
                                if s.inc_val is not None:
                                    eng.nop().then_inc(sems[s.eng], 1)
                            elif s.inc_val is not None:
                                inst.then_inc(sems[s.eng], 1)
                return body
            block.sync(section("sync"))
            block.tensor(section("tensor"))
            block.vector(section("vector"))
            block.scalar(section("scalar"))
            block.gpsimd(section("gpsimd"))
            @block.sync
            def _(sync):
                for q in ("q_sync0", "q_sync1", "q_sync2", "q_sync3",
                          "q_gpsimd0", "q_gpsimd1", "q_gpsimd2", "q_gpsimd3"):
                    if self.final_counts.get(q):
                        sync.wait_ge(sems[q], self.final_counts[q])


def build_nc(n_seq=2, n_tiles=NTILE, do_f1=True, do_attn=True, do_f2=True, dbg=False, delta_mode="all"):
    nc = bass.Bass("TRN2", target_bir_lowering=False)
    tk = TK()

    dx = nc.dram_tensor("x", [n_seq, D, L], F32, kind="ExternalInput").ap()
    dwg = nc.dram_tensor("wgT", [D, INNER], F32, kind="ExternalInput").ap()
    dwu = nc.dram_tensor("wuT", [D, INNER], F32, kind="ExternalInput").ap()
    dwd = nc.dram_tensor("wdT", [INNER, D], F32, kind="ExternalInput").ap()
    dW = nc.dram_tensor("WT", [D, D], F32, kind="ExternalInput").ap()
    dgdl = nc.dram_tensor("gdl", [128, 512], F32, kind="ExternalInput").ap()
    dbdl = nc.dram_tensor("bdl", [128, 512], F32, kind="ExternalInput").ap()
    deta = nc.dram_tensor("eta1", [1, 1], F32, kind="ExternalInput").ap()
    dlamz = nc.dram_tensor("lamz", [1, 1], F32, kind="ExternalInput").ap()
    dI128 = nc.dram_tensor("I128", [128, 128], F32, kind="ExternalInput").ap()
    dONESROW = nc.dram_tensor("ONESROW", [1, 128], F32, kind="ExternalInput").ap()
    dONE1 = nc.dram_tensor("ONE1", [1, 1], F32, kind="ExternalInput").ap()
    dONESCOL = nc.dram_tensor("ONESCOL", [128, 1], F32, kind="ExternalInput").ap()
    dSELDUP = nc.dram_tensor("SELDUP", [128, 128], F32, kind="ExternalInput").ap()
    dDL = nc.dram_tensor("DLm", [128, 512], F32, kind="ExternalInput").ap()
    dDU = nc.dram_tensor("DUm", [128, 512], F32, kind="ExternalInput").ap()
    dDQ = nc.dram_tensor("DQm", [128, 512], F32, kind="ExternalInput").ap()
    dML = nc.dram_tensor("MLm", [128, 512], F32, kind="ExternalInput").ap()
    dMU = nc.dram_tensor("MUm", [128, 512], F32, kind="ExternalInput").ap()
    dMQ = nc.dram_tensor("MQm", [128, 512], F32, kind="ExternalInput").ap()
    dIOT = nc.dram_tensor("IOT", [128, 1], F32, kind="ExternalInput").ap()
    dREV = nc.dram_tensor("REV", [128, 1], F32, kind="ExternalInput").ap()
    dCC = nc.dram_tensor("CC", [128, 1], F32, kind="ExternalInput").ap()
    dout = nc.dram_tensor("out", [n_seq, D, L], F32, kind="ExternalOutput").ap()
    sg = nc.dram_tensor("scr_g", [8, 128, 4096], BF16, kind="Internal").ap()
    su = nc.dram_tensor("scr_u", [8, 128, 4096], BF16, kind="Internal").ap()
    sd = nc.dram_tensor("scr_d", [8, 128, 4096], BF16, kind="Internal").ap()
    sW = nc.dram_tensor("scr_W", [8, 128, 1024], BF16, kind="Internal").ap()
    dIDP = nc.dram_tensor("IDP", [128, 512], F32, kind="ExternalInput").ap()
    ddbg = nc.dram_tensor("dbg", [40, 128, 512], F32, kind="ExternalOutput").ap() if dbg else None

    st = ExitStack()
    e = st.enter_context
    with st:
        xr = e(nc.sbuf_tensor("xr", [128, 4096], F32))
        gu = e(nc.sbuf_tensor("gu", [128, 16384], F32))
        qx = e(nc.sbuf_tensor("qxa", [128, 4096], BF16))
        xa = qx
        stat = e(nc.sbuf_tensor("stat", [128, 4096], F32))
        wg0 = e(nc.sbuf_tensor("wg0", [128, 4096], BF16))
        wg1 = e(nc.sbuf_tensor("wg1", [128, 4096], BF16))
        wu0 = e(nc.sbuf_tensor("wu0", [128, 4096], BF16))
        wu1 = e(nc.sbuf_tensor("wu1", [128, 4096], BF16))
        gsil = e(nc.sbuf_tensor("gsil", [128, 512], F32))
        qgu = [e(nc.sbuf_tensor(f"qgu{i}", [128, 512], BF16)) for i in range(2)]
        m2acc = e(nc.sbuf_tensor("m2acc", [128, 512], F32))
        sqb = e(nc.sbuf_tensor("sqb", [128, 512], F32))
        tA = e(nc.sbuf_tensor("tA", [128, 512], F32))
        tB = e(nc.sbuf_tensor("tB", [128, 512], F32))
        mxacc = tA
        mnacc = tB
        s2bc = e(nc.sbuf_tensor("s2bc", [128, 512], F32))
        is2bc = e(nc.sbuf_tensor("is2bc", [128, 512], F32))
        P_LOW = e(nc.sbuf_tensor("P_LOW", [128, 512], BF16))
        P_UP = e(nc.sbuf_tensor("P_UP", [128, 512], BF16))
        P_UQI = e(nc.sbuf_tensor("P_UQI", [128, 512], BF16))
        Nb = [e(nc.sbuf_tensor(f"Nb{i}", [128, 512], BF16)) for i in range(2)]
        NTw = [e(nc.sbuf_tensor(f"NTw{i}", [128, 512], BF16)) for i in range(2)]
        NT1 = e(nc.sbuf_tensor("NT1", [128, 512], BF16))
        FTS = [e(nc.sbuf_tensor(f"FTS{i}", [128, 512], BF16)) for i in range(5)]
        FT1n = e(nc.sbuf_tensor("FT1n", [128, 512], BF16))
        identp = e(nc.sbuf_tensor("identp", [128, 512], BF16))
        Mqb = [e(nc.sbuf_tensor(f"Mqb{i}", [128, 512], BF16)) for i in range(2)]
        X0 = [e(nc.sbuf_tensor(f"X0_{i}", [128, 1024], BF16)) for i in range(2)]
        Xw = [e(nc.sbuf_tensor(f"Xw{i}", [128, 1024], BF16)) for i in range(2)]
        Bcb = [e(nc.sbuf_tensor(f"Bcb{i}", [128, 512], BF16)) for i in range(2)]
        ULc = [e(nc.sbuf_tensor(f"ULc{i}", [128, 512], BF16)) for i in range(2)]
        BBTn = [e(nc.sbuf_tensor(f"BBTn{i}", [128, 512], BF16)) for i in range(2)]
        KTb = [e(nc.sbuf_tensor(f"KTb{i}", [128, 512], BF16)) for i in range(2)]
        Ub = e(nc.sbuf_tensor("Ub", [128, 512], BF16))
        Sbf = e(nc.sbuf_tensor("Sbf", [128, 512], BF16))
        ypart = e(nc.sbuf_tensor("ypart", [128, 512], F32))
        yv = e(nc.sbuf_tensor("yv", [128, 512], F32))
        ao = yv
        I128f = e(nc.sbuf_tensor("I128f", [128, 128], F32))
        I128b = e(nc.sbuf_tensor("I128b", [128, 128], BF16))
        lCI = e(nc.sbuf_tensor("lCI", [128, 128], BF16))
        onesrow = e(nc.sbuf_tensor("onesrow", [1, 128], F32))
        one1 = e(nc.sbuf_tensor("one1", [1, 1], F32))
        onescol = e(nc.sbuf_tensor("onescol", [128, 1], F32))
        seldup = e(nc.sbuf_tensor("seldup", [128, 128], F32))
        gdl = e(nc.sbuf_tensor("gdl_s", [128, 512], BF16))
        bdl = e(nc.sbuf_tensor("bdl_s", [128, 512], BF16))
        def col(name):
            return e(nc.sbuf_tensor(name, [128, 1], F32))
        zcol = col("zcol"); etacol = col("etacol"); loglam = col("loglam")
        betacol = col("betacol")
        lamcol = col("lamcol"); nlamcol = col("nlamcol"); lrevb = col("lrevb")
        lCcol = col("lCcol"); tc0 = col("tc0"); tc1 = col("tc1")
        swcol = col("swcol"); dqcol = col("dqcol")
        muc = col("muc"); invc = col("invc"); magcol = col("magcol")
        mxc = e(nc.sbuf_tensor("mxc", [128, 4], F32))
        mnc = e(nc.sbuf_tensor("mnc", [128, 4], F32))
        cA = e(nc.sbuf_tensor("cA", [128, 4], F32))
        cB = e(nc.sbuf_tensor("cB", [128, 4], F32))
        cC = e(nc.sbuf_tensor("cC", [128, 4], F32))
        cD = e(nc.sbuf_tensor("cD", [128, 4], F32))
        cE = e(nc.sbuf_tensor("cE", [128, 4], F32))
        muc4 = e(nc.sbuf_tensor("muc4", [128, 4], F32))
        ivc4 = e(nc.sbuf_tensor("ivc4", [128, 4], F32))
        def row(name):
            return e(nc.sbuf_tensor(name, [1, 512], F32))
        murow = row("murow")
        invrow = row("invrow"); r0 = row("r0"); r1 = row("r1"); r2 = row("r2")
        varrow = r1
        msrow = r2
        escal = e(nc.sbuf_tensor("escal", [1, 1], F32))
        zscal = e(nc.sbuf_tensor("zscal", [1, 1], F32))
        acc_col = col("acc_col"); red_col = col("red_col")
        pb = [e(nc.psum_tensor(f"pb{i}", [128, 512], F32)) for i in range(8)]
        PB = [f"pb{i}" for i in range(8)]

        sems = {}
        for k in ("tensor", "vector", "scalar", "gpsimd", "sync",
                  "q_sync0", "q_sync1", "q_sync2", "q_sync3",
                  "q_gpsimd0", "q_gpsimd1", "q_gpsimd2", "q_gpsimd3",
                  "g_gate", "s_gate"):
            sems[k] = e(nc.semaphore(k + "_sem"))

        class _WS:
            def __getitem__(self, idx):
                if idx == slice(None):
                    return gu[:, 0:4096]
                return gu.__getitem__(idx)
        wstage = _WS()
        wq_stage = qx
        xr3 = xr[:].rearrange("p (a b) -> p a b", b=512)
        gu3 = gu[:].rearrange("p (a b) -> p a b", b=512)
        qx3 = qx[:].rearrange("p (a b) -> p a b", b=512)
        xa3 = xa[:].rearrange("p (a b) -> p a b", b=512)
        st3 = stat[:].rearrange("p (a b) -> p a b", b=512)

        def vrecip(v, out, in_):
            v.drain()
            i = v.reciprocal(out, in_)
            v.drain()
            return i

        def vred(v, out, in_, op, absv=None):
            i = v.tensor_reduce(out, in_, AX, op, apply_absolute_value=absv)
            v.drain()
            return i

        dbg_i = [0]
        def dump(ap, res):
            if not dbg or dbg_i[0] >= 40:
                return
            slot = dbg_i[0]; dbg_i[0] += 1
            sh = ap.shape if hasattr(ap, "shape") else None
            tk.dg(lambda g, sem, ap=ap, slot=slot: g.dma_start(
                ddbg[slot][0:ap.shape[0], 0:ap.shape[1]], ap,
                allow_slow_non_contiguous=True).then_inc(sem, 16),
                reads=[res])

        def ld(dst, src, rs=(), ws=()):
            tk.dsync(lambda g, sem, d=dst, s=src: g.dma_start(d, s).then_inc(sem, 16),
                     reads=rs, writes=ws)

        def ldg(dst, src, rs=(), ws=()):
            tk.dg(lambda g, sem, d=dst, s=src: g.dma_start(d, s).then_inc(sem, 16),
                  reads=rs, writes=ws)

        ld(I128f[:], dI128, ws=["I128f"])
        ld(onesrow[:], dONESROW, ws=["onesrow"])
        ld(one1[:], dONE1, ws=["one1"])
        ld(onescol[:], dONESCOL, ws=["onescol"])
        ld(seldup[:], dSELDUP, ws=["seldup"])
        ldg(gdl[:], dgdl, ws=["gdl"])
        ldg(bdl[:], dbdl, ws=["bdl"])
        ld(escal[:], deta, ws=["escal"])
        ld(zscal[:], dlamz, ws=["zscal"])
        tk.v(lambda v: v.tensor_copy(I128b[:], I128f[:]), reads=["I128f"], writes=["I128b"])
        tk.v(lambda v: v.memset(magcol[:], MAGIC), reads=["magcol"], writes=["magcol"])

        tk.t(lambda t: t.matmul(pb[0][:, 0:1], onesrow[:], escal[:], start=True, stop=True),
             reads=["onesrow", "escal"], writes=[PB[0]])
        tk.v(lambda v: v.tensor_copy(etacol[:], pb[0][:, 0:1]), reads=[PB[0]], writes=["etacol"])
        tk.t(lambda t: t.matmul(pb[0][:, 0:1], onesrow[:], zscal[:], start=True, stop=True),
             reads=["onesrow", "zscal", "etacol"], writes=[PB[0]])
        tk.v(lambda v: v.tensor_copy(zcol[:], pb[0][:, 0:1]), reads=[PB[0]], writes=["zcol"])
        tk.s(lambda s: s.activation(tc0[:], zcol[:], AF.Sigmoid), reads=["zcol"], writes=["tc0"])
        tk.s(lambda s: s.activation(loglam[:], tc0[:], AF.Ln), reads=["tc0"], writes=["loglam"])
        tk.v(lambda v: v.tensor_scalar(tc1[:], tc0[:], -1.0, 1.0, OP.mult, OP.add),
             reads=["tc0"], writes=["tc1"])
        tk.v(lambda v: v.tensor_tensor(tc1[:], tc1[:], etacol[:], OP.mult),
             reads=["tc1", "etacol"], writes=["tc1"])
        tk.v(lambda v: v.tensor_scalar(betacol[:], tc1[:], 1.0 / 64, None, OP.mult),
             reads=["tc1"], writes=["betacol"])
        ld(tc0[:], dIOT, rs=["tc0"], ws=["tc0"])
        tk.v(lambda v: v.tensor_tensor(tc0[:], tc0[:], loglam[:], OP.mult),
             reads=["tc0", "loglam"], writes=["tc0"])
        tk.s(lambda s: s.activation(lamcol[:], tc0[:], AF.Exp), reads=["tc0"], writes=["lamcol"])
        tk.v(lambda v: v.tensor_scalar(nlamcol[:], lamcol[:], -1.0, None, OP.mult),
             reads=["lamcol"], writes=["nlamcol"])
        ld(tc0[:], dREV, rs=["tc0"], ws=["tc0"])
        tk.v(lambda v: v.tensor_tensor(tc0[:], tc0[:], loglam[:], OP.mult),
             reads=["tc0", "loglam"], writes=["tc0"])
        tk.s(lambda s: s.activation(tc1[:], tc0[:], AF.Exp), reads=["tc0"], writes=["tc1"])
        tk.v(lambda v: v.tensor_tensor(lrevb[:], tc1[:], betacol[:], OP.mult),
             reads=["tc1", "betacol"], writes=["lrevb"])
        ld(tc0[:], dCC, rs=["tc0"], ws=["tc0"])
        tk.v(lambda v: v.tensor_tensor(tc0[:], tc0[:], loglam[:], OP.mult),
             reads=["tc0", "loglam"], writes=["tc0"])
        tk.s(lambda s: s.activation(lCcol[:], tc0[:], AF.Exp), reads=["tc0"], writes=["lCcol"])
        tk.v(lambda v: v.tensor_scalar(lCI[:], I128f[:], lCcol[:], None, OP.mult),
             reads=["I128f", "lCcol"], writes=["lCI"])
        for Dm, Mm, dst, dstn in [(dDL, dML, P_LOW, "P_LOW"), (dDU, dMU, P_UP, "P_UP"),
                                  (dDQ, dMQ, P_UQI, "P_UQI")]:
            ld(tA[:], Dm, rs=["tA"], ws=["tA"])
            ld(tB[:], Mm, rs=["tB"], ws=["tB"])
            tk.v(lambda v: v.tensor_scalar(tA[:], tA[:], loglam[:], None, OP.mult),
                 reads=["tA", "loglam"], writes=["tA"])
            tk.s(lambda s: s.activation(tA[:], tA[:], AF.Exp), reads=["tA"], writes=["tA"])
            tk.v(lambda v, d=dst: v.scalar_tensor_tensor(d[:], tA[:], betacol[:], tB[:],
                                                         OP.mult, OP.mult),
                 reads=["tA", "betacol", "tB", dstn], writes=[dstn])

        ldg(identp[:], dIDP, ws=["identp"])
        for kd in range(8):
            tk.dg(lambda g, sem, kd=kd: g.dma_start(
                wg0[:, 0:1024], dW[128 * kd:128 * kd + 128, :]).then_inc(sem, 16),
                  reads=["wg0"], writes=["wg0"])
            tk.dg(lambda g, sem, kd=kd: g.dma_start(
                sW[kd], wg0[:, 0:1024]).then_inc(sem, 16),
                  reads=["wg0"], writes=["scr_W"])

        def prep_scale(extra):
            tk.v(lambda v: v.tensor_scalar(escal[:], acc_col[0:1, 0:1], 1.0 / (D * INNER), 1e-5,
                                           OP.mult, OP.max), reads=["acc_col", "escal"], writes=["escal"])
            tk.v(lambda v: vrecip(v, zscal[:], escal[:]), reads=["zscal", "escal"], writes=["zscal"])
            tk.t(lambda t: t.matmul(pb[0][:, 0:1], onesrow[:], zscal[:], start=True, stop=True),
                 reads=["onesrow", "zscal"], writes=[PB[0]])
            tk.v(lambda v: v.tensor_copy(swcol[:], pb[0][:, 0:1]), reads=[PB[0]], writes=["swcol"])
            tk.t(lambda t: t.matmul(pb[0][:, 0:1], onesrow[:], escal[:], start=True, stop=True),
                 reads=["onesrow", "escal", "swcol"], writes=[PB[0]])
            tk.v(lambda v, x=extra: v.tensor_scalar(dqcol[:], pb[0][:, 0:1], x, None, OP.mult),
                 reads=[PB[0]], writes=["dqcol"])

        wsl = [gu[:, 4096 * i:4096 * i + 4096] for i in range(4)]
        wsn = ["guA", "guB", "guC", "guD"]
        wctr = [0]

        def prep_reduce(dsrc, rowblock):
            for pg in range(8):
                pp = wctr[0] % 4; wctr[0] += 1
                if rowblock:
                    tk.dsync(lambda g, sem, pg=pg, pp=pp: g.dma_start(
                        wsl[pp].rearrange("p (a b) -> p a b", b=1024),
                        dsrc[512 * pg:512 * pg + 512, :].rearrange("(a p) b -> p a b", p=128)
                        ).then_inc(sem, 16),
                          reads=[wsn[pp]], writes=[wsn[pp]])
                else:
                    tk.dsync(lambda g, sem, pg=pg, pp=pp: g.dma_start(
                        wsl[pp], dsrc[128 * pg:128 * pg + 128, :]).then_inc(sem, 16),
                          reads=[wsn[pp]], writes=[wsn[pp]])
                tk.s(lambda s, pp=pp: s.activation(wsl[pp], wsl[pp], AF.Abs),
                     reads=[wsn[pp]], writes=[wsn[pp]])
                def f(t, pg=pg, pp=pp):
                    last = None
                    w3 = wsl[pp].rearrange("p (a b) -> p a b", b=512)
                    for q8 in range(8):
                        last = t.matmul(pb[1][0:1, :], onescol[:], w3[:, q8, :],
                                        start=(pg == 0 and q8 == 0),
                                        stop=(pg == 7 and q8 == 7))
                    return last
                tk.t(f, reads=[wsn[pp], "onescol"] + ([] if pg == 0 else [PB[1]]),
                     writes=[PB[1]])
            tk.v(lambda v: vred(v, acc_col[0:1, :], pb[1][0:1, :], OP.add),
                 reads=[PB[1], "acc_col"], writes=["acc_col"])

        def prep_quant(dsrc, dscr, scrn, rowblock):
            for pg in range(8):
                pp = wctr[0] % 4; wctr[0] += 1
                if rowblock:
                    tk.dsync(lambda g, sem, pg=pg, pp=pp: g.dma_start(
                        wsl[pp].rearrange("p (a b) -> p a b", b=1024),
                        dsrc[512 * pg:512 * pg + 512, :].rearrange("(a p) b -> p a b", p=128)
                        ).then_inc(sem, 16),
                          reads=[wsn[pp]], writes=[wsn[pp]])
                else:
                    tk.dsync(lambda g, sem, pg=pg, pp=pp: g.dma_start(
                        wsl[pp], dsrc[128 * pg:128 * pg + 128, :]).then_inc(sem, 16),
                          reads=[wsn[pp]], writes=[wsn[pp]])
                tk.s(lambda s, pp=pp: s.activation(wsl[pp], wsl[pp], AF.Identity,
                                                   scale=swcol[:], bias=magcol[:]),
                     reads=[wsn[pp], "swcol", "magcol"], writes=[wsn[pp]])
                tk.v(lambda v, pp=pp: v.tensor_scalar(wsl[pp], wsl[pp], MAGIC, 1.0,
                                               OP.subtract, OP.min),
                     reads=[wsn[pp]], writes=[wsn[pp]])
                tk.v(lambda v, pp=pp: v.tensor_scalar(wsl[pp], wsl[pp], -1.0, dqcol[:],
                                               OP.max, OP.mult),
                     reads=[wsn[pp], "dqcol"], writes=[wsn[pp]])
                tk.dg(lambda g, sem, pg=pg, pp=pp: g.dma_start(
                    dscr[pg], wsl[pp]).then_inc(sem, 16),
                    reads=[wsn[pp]], writes=[scrn])

        def prep_gu(dsrc, dscr, scrn, dbg_this=False):
            prep_reduce(dsrc, False)
            prep_scale(1.0)
            prep_quant(dsrc, dscr, scrn, False)

        def prep_all():
            prep_gu(dwg, sg, "scr_g")
            prep_gu(dwu, su, "scr_u")
            prep_reduce(dwd, True)
            prep_scale(0.5)
            prep_quant(dwd, sd, "scr_d", True)
            tk.v(lambda v: v.tensor_copy(gu[:, 0:1], gu[:, 0:1]),
                 reads=["guA", "guB", "guC", "guD", "gu"],
                 writes=["guA", "guB", "guC", "guD", "gu"])

        def ln_stats():
            for j in range(8):
                tk.t(lambda t, j=j: t.matmul(pb[6][0:1, :], onescol[:], xr3[:, j, :],
                                             start=(j == 0), stop=(j == 7)),
                     reads=[f"xr{j}", "onescol"] + ([] if j else [PB[6]]), writes=[PB[6]])
            for j in range(8):
                tk.s(lambda s, j=j: s.activation(sqb[:], xr3[:, j, :], AF.Square),
                     reads=[f"xr{j}", "sqb"], writes=["sqb"])
                tk.t(lambda t, j=j: t.matmul(pb[7][0:1, :], onescol[:], sqb[:],
                                             start=(j == 0), stop=(j == 7)),
                     reads=["sqb", "onescol"], writes=[PB[7]])
            tk.v(lambda v: v.tensor_scalar(murow[:], pb[6][0:1, :], 1.0 / D, None, OP.mult),
                 reads=[PB[6]], writes=["murow"])
            tk.v(lambda v: v.tensor_scalar(msrow[:], pb[7][0:1, :], 1.0 / D, None, OP.mult),
                 reads=[PB[7], "r2"], writes=["r2"])
            tk.v(lambda v: v.tensor_tensor(varrow[:], murow[:], murow[:], OP.mult),
                 reads=["murow", "r1"], writes=["r1"])
            tk.v(lambda v: v.tensor_tensor(varrow[:], msrow[:], varrow[:], OP.subtract),
                 reads=["r2", "r1"], writes=["r1"])
            tk.v(lambda v: v.tensor_scalar(varrow[:], varrow[:], EPS, None, OP.add),
                 reads=["r1"], writes=["r1"])
            tk.v(lambda v: vrecip(v, r0[:], varrow[:]), reads=["r1", "r0"], writes=["r0"])
            tk.s(lambda s: s.activation(invrow[:], r0[:], AF.Sqrt), reads=["r0"], writes=["invrow"])

        def bcast_row(src_row, srcn, bank):
            def f(t, src_row=src_row, bank=bank):
                last = None
                for bq in range(4):
                    last = t.matmul(pb[bank][:, 128 * bq:128 * bq + 128], onesrow[:],
                                    src_row[0:1, 128 * bq:128 * bq + 128],
                                    start=True, stop=True)
                return last
            tk.t(f, reads=[srcn, "onesrow"], writes=[PB[bank]])

        def quant_rows():
            tk.v(lambda v: v.tensor_tensor(mxacc[:], xr3[:, 0, :], xr3[:, 1, :], OP.max),
                 reads=["xr0", "xr1", "tA"], writes=["tA"])
            tk.v(lambda v: v.tensor_tensor(mnacc[:], xr3[:, 0, :], xr3[:, 1, :], OP.min),
                 reads=["xr0", "xr1", "tB"], writes=["tB"])
            tk.v(lambda v: v.tensor_tensor(sqb[:], xr3[:, 2, :], xr3[:, 3, :], OP.max),
                 reads=["xr2", "xr3", "sqb"], writes=["sqb"])
            tk.v(lambda v: v.tensor_tensor(mxacc[:], mxacc[:], sqb[:], OP.max),
                 reads=["tA", "sqb"], writes=["tA"])
            tk.v(lambda v: v.tensor_tensor(sqb[:], xr3[:, 2, :], xr3[:, 3, :], OP.min),
                 reads=["xr2", "xr3", "sqb"], writes=["sqb"])
            tk.v(lambda v: v.tensor_tensor(mnacc[:], mnacc[:], sqb[:], OP.min),
                 reads=["tB", "sqb"], writes=["tB"])
            tk.v(lambda v: v.tensor_tensor(sqb[:], xr3[:, 4, :], xr3[:, 5, :], OP.max),
                 reads=["xr4", "xr5", "sqb"], writes=["sqb"])
            tk.v(lambda v: v.tensor_tensor(mxacc[:], mxacc[:], sqb[:], OP.max),
                 reads=["tA", "sqb"], writes=["tA"])
            tk.v(lambda v: v.tensor_tensor(sqb[:], xr3[:, 4, :], xr3[:, 5, :], OP.min),
                 reads=["xr4", "xr5", "sqb"], writes=["sqb"])
            tk.v(lambda v: v.tensor_tensor(mnacc[:], mnacc[:], sqb[:], OP.min),
                 reads=["tB", "sqb"], writes=["tB"])
            tk.v(lambda v: v.tensor_tensor(sqb[:], xr3[:, 6, :], xr3[:, 7, :], OP.max),
                 reads=["xr6", "xr7", "sqb"], writes=["sqb"])
            tk.v(lambda v: v.tensor_tensor(mxacc[:], mxacc[:], sqb[:], OP.max),
                 reads=["tA", "sqb"], writes=["tA"])
            tk.v(lambda v: v.tensor_tensor(sqb[:], xr3[:, 6, :], xr3[:, 7, :], OP.min),
                 reads=["xr6", "xr7", "sqb"], writes=["sqb"])
            tk.v(lambda v: v.tensor_tensor(mnacc[:], mnacc[:], sqb[:], OP.min),
                 reads=["tB", "sqb"], writes=["tB"])
            def trx(t):
                last = None
                for bq in range(4):
                    last = t.matmul(pb[4][:, 128 * bq:128 * bq + 128],
                                    mxacc[:, 128 * bq:128 * bq + 128], I128f[:],
                                    start=True, stop=True)
                for bq in range(4):
                    last = t.matmul(pb[5][:, 128 * bq:128 * bq + 128],
                                    mnacc[:, 128 * bq:128 * bq + 128], I128f[:],
                                    start=True, stop=True)
                return last
            tk.t(trx, reads=["tA", "tB", "I128f"], writes=[PB[4], PB[5]])
            for bq in range(4):
                tk.v(lambda v, bq=bq: vred(v, mxc[:, bq:bq + 1],
                                                      pb[4][:, 128 * bq:128 * bq + 128], OP.max),
                     reads=[PB[4], "mxc"], writes=["mxc"])
                tk.v(lambda v, bq=bq: vred(v, mnc[:, bq:bq + 1],
                                                      pb[5][:, 128 * bq:128 * bq + 128], OP.min),
                     reads=[PB[5], "mnc"], writes=["mnc"])
            def trmu(t):
                last = None
                for bq in range(4):
                    last = t.matmul(pb[6][:, 4 * bq:4 * bq + 1],
                                    murow[0:1, 128 * bq:128 * bq + 128],
                                    one1[:], start=True, stop=True)
                for bq in range(4):
                    last = t.matmul(pb[7][:, 4 * bq:4 * bq + 1],
                                    invrow[0:1, 128 * bq:128 * bq + 128],
                                    one1[:], start=True, stop=True)
                return last
            tk.t(trmu, reads=["murow", "invrow", "one1"], writes=[PB[6], PB[7]])
            if dbg:
                tk.v(lambda v: v.tensor_copy(yv[:, 0:16], pb[6][:, 0:16]),
                     reads=[PB[6], "yv"], writes=["yv"])
                dump(yv[:, 0:16], "yv")
                tk.v(lambda v: v.tensor_copy(yv[:, 0:16], pb[7][:, 0:16]),
                     reads=[PB[7], "yv"], writes=["yv"])
                dump(yv[:, 0:16], "yv")
            for bq in range(4):
                tk.v(lambda v, bq=bq: v.tensor_copy(muc4[:, bq:bq + 1], pb[6][:, 4 * bq:4 * bq + 1]),
                     reads=[PB[6], "muc4"], writes=["muc4"])
                tk.v(lambda v, bq=bq: v.tensor_copy(ivc4[:, bq:bq + 1], pb[7][:, 4 * bq:4 * bq + 1]),
                     reads=[PB[7], "ivc4"], writes=["ivc4"])
            mu4 = muc4[:]
            iv4 = ivc4[:]
            tk.v(lambda v: v.tensor_tensor(cA[:], mxc[:], mu4, OP.subtract),
                 reads=["mxc", "muc4", "cA"], writes=["cA"])
            tk.v(lambda v: v.tensor_tensor(cB[:], mu4, mnc[:], OP.subtract),
                 reads=["mnc", "muc4", "cB"], writes=["cB"])
            tk.v(lambda v: v.tensor_tensor(cA[:], cA[:], cB[:], OP.max),
                 reads=["cA", "cB"], writes=["cA"])
            tk.v(lambda v: v.tensor_tensor(cA[:], cA[:], iv4, OP.mult),
                 reads=["cA", "ivc4"], writes=["cA"])
            tk.v(lambda v: v.tensor_scalar(cA[:], cA[:], 1e-5, None, OP.max),
                 reads=["cA"], writes=["cA"])
            tk.v(lambda v: vrecip(v, cB[:], cA[:]), reads=["cA", "cB"], writes=["cB"])
            tk.v(lambda v: v.tensor_scalar(cB[:], cB[:], 127.0, None, OP.mult),
                 reads=["cB"], writes=["cB"])
            tk.v(lambda v: v.tensor_scalar(cC[:], cA[:], 1.0 / 127, None, OP.mult),
                 reads=["cA", "cC"], writes=["cC"])
            tk.v(lambda v: v.tensor_tensor(cD[:], iv4, cB[:], OP.mult),
                 reads=["ivc4", "cB", "cD"], writes=["cD"])
            tk.v(lambda v: v.tensor_tensor(cE[:], mu4, cD[:], OP.mult),
                 reads=["muc4", "cD", "cE"], writes=["cE"])
            def c2r(t):
                last = None
                for bq in range(4):
                    last = t.matmul(pb[4][0:1, 128 * bq:128 * bq + 128], cD[:, bq:bq + 1],
                                    I128f[:], start=True, stop=True)
                for bq in range(4):
                    last = t.matmul(pb[5][0:1, 128 * bq:128 * bq + 128], cE[:, bq:bq + 1],
                                    I128f[:], start=True, stop=True)
                for bq in range(4):
                    last = t.matmul(pb[6][0:1, 128 * bq:128 * bq + 128], cC[:, bq:bq + 1],
                                    I128f[:], start=True, stop=True)
                return last
            tk.t(c2r, reads=["cC", "cD", "cE", "I128f"], writes=[PB[4], PB[5], PB[6]])
            tk.v(lambda v: v.tensor_copy(r0[:], pb[4][0:1, :]), reads=[PB[4], "r0"], writes=["r0"])
            tk.v(lambda v: v.tensor_copy(r1[:], pb[5][0:1, :]), reads=[PB[5], "r1"], writes=["r1"])
            tk.v(lambda v: v.tensor_copy(r2[:], pb[6][0:1, :]), reads=[PB[6], "r2"], writes=["r2"])
            dump(mxc[:], "mxc"); dump(mnc[:], "mnc")
            dump(cA[:], "cA"); dump(cB[:], "cB"); dump(cC[:], "cC")
            dump(cD[:], "cD"); dump(cE[:], "cE")

        def wdma(buf, bufn, scr, grp):
            scrn = "scr_g" if scr is sg else "scr_u"
            def f(g, sem, buf=buf, scr=scr, grp=grp):
                b3 = buf[:].rearrange("p (a b) -> p a b", b=512)
                for kf in range(8):
                    g.dma_start(b3[:, kf, :], scr[kf][:, 512 * grp:512 * grp + 512]).then_inc(sem, 16)
            tk.dsync(f, reads=[bufn, scrn], writes=[bufn], ndma=8, lag=True)

        def ddma(buf, bufn, jg):
            def f(g, sem, buf=buf, jg=jg):
                g.dma_start(buf[:], sd[jg]).then_inc(sem, 16)
            tk.dsync(f, reads=[bufn, "scr_d"], writes=[bufn], ndma=1, lag=True)

        def ffn_front():
            ln_stats()
            quant_rows()
            bcast_row(r0, "r0", 2)
            bcast_row(r1, "r1", 3)
            bcast_row(r2, "r2", 4)
            for j in range(8):
                tk.v(lambda v, j=j: v.tensor_tensor(tA[:], xr3[:, j, :], pb[2][:, :], OP.mult),
                     reads=[f"xr{j}", PB[2], "tA"], writes=["tA"])
                tk.v(lambda v: v.tensor_tensor(tA[:], tA[:], pb[3][:, :], OP.subtract),
                     reads=["tA", PB[3]], writes=["tA"])
                tk.v(lambda v: v.tensor_scalar(tA[:], tA[:], MAGIC, MAGIC, OP.add, OP.subtract),
                     reads=["tA"], writes=["tA"])
                tk.v(lambda v, j=j: v.tensor_tensor(qx3[:, j, :], tA[:], pb[4][:, :], OP.mult),
                     reads=["tA", PB[4], f"qx{j}"], writes=[f"qx{j}"])

        def ffn(skip_front=False):
            if not skip_front:
                ffn_front()
            wgb = [wg0, wg1]; wub = [wu0, wu1]
            wgn = ["wg0", "wg1"]; wun = ["wu0", "wu1"]
            wdma(wgb[0], wgn[0], sg, 0)
            wdma(wub[0], wun[0], su, 0)
            for grp in range(8):
                pa = grp % 2
                if grp + 1 < 8:
                    wdma(wgb[1 - pa], wgn[1 - pa], sg, grp + 1)
                    wdma(wub[1 - pa], wun[1 - pa], su, grp + 1)
                if grp == 0:
                    dump(wgb[0][:, 0:512], wgn[0])
                qall = ["qx%d" % _q for _q in range(8)]
                for ob in range(4):
                    gb = (ob % 2) * 2
                    ub = (ob % 2) * 2 + 1
                    if grp == 0 and ob == 0:
                        for kf in range(8):
                            tk.t(lambda t, pa=pa, kf=kf, gb=gb: t.matmul(
                                pb[gb][:, :],
                                wgb[pa][:].rearrange("p (a b) -> p a b", b=512)[:, kf, 0:128],
                                qx3[:, kf, :], start=(kf == 0), stop=(kf == 7)),
                                reads=[wgn[pa], f"qx{kf}"] + ([] if kf == 0 else [PB[gb]]),
                                writes=[PB[gb]])
                        for kf in range(8):
                            tk.t(lambda t, pa=pa, kf=kf, ub=ub: t.matmul(
                                pb[ub][:, :],
                                wub[pa][:].rearrange("p (a b) -> p a b", b=512)[:, kf, 0:128],
                                qx3[:, kf, :], start=(kf == 0), stop=(kf == 7)),
                                reads=[wun[pa], f"qx{kf}"] + ([] if kf == 0 else [PB[ub]]),
                                writes=[PB[ub]])
                        idx = 0
                        tk.s(lambda s, gb=gb: s.activation(gsil[:], pb[gb][:, :], AF.Silu),
                             reads=[PB[gb], "gsil"], writes=["gsil"])
                        tk.v(lambda v, ub=ub: v.tensor_tensor(gu3[:, 0, :], gsil[:],
                                                              pb[ub][:, :], OP.mult),
                             reads=["gsil", PB[ub], "gu"], writes=["gu"])
                        ab = tB
                        tk.s(lambda s: s.activation(tB[:], gu3[:, 0, :], AF.Abs),
                             reads=["gu", "tB"], writes=["tB"])
                        tk.v(lambda v: v.tensor_copy(m2acc[:], tB[:]),
                             reads=["tB", "m2acc"], writes=["m2acc"])
                        continue
                    def mmg(t, pa=pa, ob=ob, gb=gb):
                        w3 = wgb[pa][:].rearrange("p (a b) -> p a b", b=512)
                        last = None
                        for kf in range(8):
                            last = t.matmul(pb[gb][:, :], w3[:, kf, 128 * ob:128 * ob + 128],
                                            qx3[:, kf, :], start=(kf == 0), stop=(kf == 7))
                        return last
                    tk.t(mmg, reads=[wgn[pa]] + qall, writes=[PB[gb]])
                    def mmu(t, pa=pa, ob=ob, ub=ub):
                        w3 = wub[pa][:].rearrange("p (a b) -> p a b", b=512)
                        last = None
                        for kf in range(8):
                            last = t.matmul(pb[ub][:, :], w3[:, kf, 128 * ob:128 * ob + 128],
                                            qx3[:, kf, :], start=(kf == 0), stop=(kf == 7))
                        return last
                    tk.t(mmu, reads=[wun[pa]] + qall, writes=[PB[ub]])
                    idx = grp * 4 + ob
                    tk.s(lambda s, gb=gb: s.activation(gsil[:], pb[gb][:, :], AF.Silu),
                         reads=[PB[gb], "gsil"], writes=["gsil"])
                    tk.v(lambda v, idx=idx, ub=ub: v.tensor_tensor(gu3[:, idx, :], gsil[:],
                                                                   pb[ub][:, :], OP.mult),
                         reads=["gsil", PB[ub], "gu"], writes=["gu"])
                    ab = sqb if idx % 2 else tB
                    abn = "sqb" if idx % 2 else "tB"
                    tk.s(lambda s, idx=idx, ab=ab: s.activation(ab[:], gu3[:, idx, :], AF.Abs),
                         reads=["gu", abn], writes=[abn])
                    if idx == 0:
                        tk.v(lambda v, ab=ab: v.tensor_copy(m2acc[:], ab[:]),
                             reads=[abn, "m2acc"], writes=["m2acc"])
                    else:
                        tk.v(lambda v, ab=ab: v.tensor_tensor(m2acc[:], m2acc[:], ab[:], OP.max),
                             reads=["m2acc", abn], writes=["m2acc"])
            dump(gu3[:, 0, :], "gu"); dump(gu3[:, 31, :], "gu"); dump(m2acc[:], "m2acc")
            def trm2(t):
                last = None
                for bq in range(4):
                    last = t.matmul(pb[4][:, 128 * bq:128 * bq + 128],
                                    m2acc[:, 128 * bq:128 * bq + 128], I128f[:],
                                    start=True, stop=True)
                return last
            tk.t(trm2, reads=["m2acc", "I128f"], writes=[PB[4]])
            for bq in range(4):
                tk.v(lambda v, bq=bq: vred(v, mxc[:, bq:bq + 1],
                                                      pb[4][:, 128 * bq:128 * bq + 128], OP.max),
                     reads=[PB[4], "mxc"], writes=["mxc"])
            tk.v(lambda v: v.tensor_scalar(cA[:], mxc[:], 1e-5, None, OP.max),
                 reads=["mxc", "cA"], writes=["cA"])
            tk.v(lambda v: vrecip(v, cB[:], cA[:]), reads=["cA", "cB"], writes=["cB"])
            tk.v(lambda v: v.tensor_scalar(cB[:], cB[:], 127.0, None, OP.mult),
                 reads=["cB"], writes=["cB"])
            tk.v(lambda v: v.tensor_scalar(cC[:], cA[:], 1.0 / 127, None, OP.mult),
                 reads=["cA", "cC"], writes=["cC"])
            def c2r2(t):
                last = None
                for bq in range(4):
                    last = t.matmul(pb[4][0:1, 128 * bq:128 * bq + 128], cB[:, bq:bq + 1],
                                    I128f[:], start=True, stop=True)
                for bq in range(4):
                    last = t.matmul(pb[5][0:1, 128 * bq:128 * bq + 128], cC[:, bq:bq + 1],
                                    I128f[:], start=True, stop=True)
                return last
            tk.t(c2r2, reads=["cB", "cC", "I128f"], writes=[PB[4], PB[5]])
            tk.v(lambda v: v.tensor_copy(r0[:], pb[4][0:1, :]), reads=[PB[4], "r0"], writes=["r0"])
            tk.v(lambda v: v.tensor_copy(r1[:], pb[5][0:1, :]), reads=[PB[5], "r1"], writes=["r1"])
            bcast_row(r0, "r0", 4)
            bcast_row(r1, "r1", 5)
            tk.v(lambda v: v.tensor_copy(s2bc[:], pb[4][:, :]), reads=[PB[4], "s2bc"], writes=["s2bc"])
            tk.v(lambda v: v.tensor_copy(is2bc[:], pb[5][:, :]), reads=[PB[5], "is2bc"], writes=["is2bc"])
            dump(s2bc[:], "s2bc"); dump(is2bc[:], "is2bc")
            ddma(wg0, "wg0", 0)
            for jg in range(8):
                pa = jg % 2
                if jg + 1 < 8:
                    ddma(wgb[1 - pa], wgn[1 - pa], jg + 1)
                for jj in range(4):
                    j = 4 * jg + jj
                    qb = j % 2
                    tk.v(lambda v, j=j: v.tensor_tensor(tB[:], gu3[:, j, :], s2bc[:], OP.mult),
                         reads=["gu", "s2bc", "tB"], writes=["tB"])
                    tk.v(lambda v, qb=qb: v.tensor_scalar(qgu[qb][:], tB[:], MAGIC, MAGIC,
                                                          OP.add, OP.subtract),
                         reads=["tB", f"qgu{qb}"], writes=[f"qgu{qb}"])
                    def mmd(t, pa=pa, jj=jj, j=j, qb=qb):
                        w3 = wgb[pa][:].rearrange("p (a b) -> p a b", b=1024)
                        last = None
                        for o in range(8):
                            last = t.matmul(pb[o][:, :], w3[:, jj, 128 * o:128 * o + 128],
                                            qgu[qb][:], start=(j == 0), stop=(j == 31))
                        return last
                    tk.t(mmd, reads=[wgn[pa], f"qgu{qb}"], writes=list(PB))
            for o in range(8):
                tk.v(lambda v, o=o: v.tensor_tensor(tA[:], pb[o][:, :], is2bc[:], OP.mult),
                     reads=[PB[o], "is2bc", "tA"], writes=["tA"])
                tk.v(lambda v, o=o: v.tensor_tensor(xr3[:, o, :], xr3[:, o, :], tA[:], OP.add),
                     reads=["tA", f"xr{o}"], writes=[f"xr{o}"])

        def wWdma(buf, bufn, kd):
            def f(g, sem, buf=buf, kd=kd):
                g.dma_start(buf[:, 0:1024], sW[kd]).then_inc(sem, 16)
            tk.dsync(f, reads=[bufn, "scr_W"], writes=[bufn], ndma=1, lag=True)

        def attn():
            ln_stats()
            tk.v(lambda v: v.tensor_tensor(r0[:], murow[:], invrow[:], OP.mult),
                 reads=["murow", "invrow", "r0"], writes=["r0"])
            bcast_row(invrow, "invrow", 2)
            bcast_row(r0, "r0", 3)
            for j in range(8):
                tk.v(lambda v, j=j: v.tensor_tensor(tA[:], xr3[:, j, :], pb[2][:, :], OP.mult),
                     reads=[f"xr{j}", PB[2], "tA"], writes=["tA"])
                tk.v(lambda v: v.tensor_tensor(tA[:], tA[:], pb[3][:, :], OP.subtract),
                     reads=["tA", PB[3]], writes=["tA"])
                tk.s(lambda s, j=j: s.activation(xa3[:, j, :], tA[:], AF.Tanh),
                     reads=["tA", f"qx{j}"], writes=[f"qx{j}"])
            wgb = [wg0, wg1]; wgn = ["wg0", "wg1"]
            wWdma(wgb[0], wgn[0], 0)
            for kd in range(8):
                pa = kd % 2
                if kd + 1 < 8:
                    wWdma(wgb[1 - pa], wgn[1 - pa], kd + 1)
                def mstk(t, pa=pa, kd=kd):
                    last = None
                    for eb in range(8):
                        last = t.matmul(pb[eb][:, :],
                                        wgb[pa][:, 128 * eb:128 * eb + 128],
                                        xa3[:, kd, :], start=(kd == 0), stop=(kd == 7))
                    return last
                tk.t(mstk, reads=[wgn[pa], f"qx{kd}"], writes=list(PB))
            for eb in range(8):
                tk.s(lambda s, eb=eb: s.copy(st3[:, eb, :], pb[eb][:, :]),
                     reads=[PB[eb], "stat"], writes=["stat"])
            if delta_mode == "none":
                return
            for f in d_front_pieces(0):
                f()
            for c in range(NCH):
                ss = d_solve_stages(c)
                tp = d_tail_pieces(c - 1) if c >= 1 else [None] * 6
                fp = d_front_pieces(c + 1) if c + 1 < NCH else [None] * 3
                order = [ss[0], tp[0], ss[1], tp[1], ss[2], tp[2], ss[3],
                         tp[3], ss[4], fp[0], tp[4], ss[5], fp[1], tp[5],
                         ss[6], fp[2]]
                for f in order:
                    if f is not None:
                        f()
            for f in d_tail_pieces(NCH - 1):
                f()

        def _acc(c):
            cc = slice(64 * c, 64 * c + 64)
            def quad16(t, lhs_of, rhs_of, outp, start=True, stop=True):
                last = None
                for p in range(8):
                    for hh in range(2):
                        last = t.matmul(outp(p, hh), lhs_of(p, hh), rhs_of(p, hh),
                                        start=start, stop=stop,
                                        tile_position=(64 * hh, 64 * hh))
                return last
            def pairs(buf, w=64):
                return lambda p, hh, buf=buf, w=w: buf[64 * hh:64 * hh + 64, w * p:w * p + w]
            def ppsum(bank, w=64):
                return lambda p, hh, bank=bank, w=w: pb[bank][64 * hh:64 * hh + 64, w * p:w * p + w]
            kslice = lambda p, hh, cc=cc: xa3[:, p, cc][64 * hh:64 * hh + 64, :]
            sslice = lambda p, hh, cc=cc: st3[:, p, cc][64 * hh:64 * hh + 64, :]
            i32f = lambda p, hh: I128f[64 * hh:64 * hh + 64, 64 * hh:64 * hh + 64]
            i32b = lambda p, hh: I128b[64 * hh:64 * hh + 64, 64 * hh:64 * hh + 64]
            return cc, quad16, pairs, ppsum, kslice, sslice, i32f, i32b

        def d_front_pieces(c):
            pc = c % 2
            cc, quad16, pairs, ppsum, kslice, sslice, i32f, i32b = _acc(c)
            def F1():
                _front_f1(c, pc, quad16, ppsum, kslice)
            def F2():
                _front_f2(c, pc, quad16, ppsum, kslice, i32b)
            def F3():
                _front_f3(c, pc, quad16, ppsum, sslice, i32f)
            return [F1, F2, F3]

        def _front_f1(c, pc, quad16, ppsum, kslice):
            tk.t(lambda t, k=kslice, q=quad16, o=ppsum(0): q(t, k, k, o),
                 reads=["qx%d" % _q for _q in range(8)], writes=[PB[0]])
            tk.v(lambda v: v.tensor_tensor(Nb[0][:], pb[0][:, :], P_LOW[:], OP.mult),
                 reads=[PB[0], "P_LOW", "Nb0"], writes=["Nb0"])
            tk.v(lambda v: v.tensor_tensor(NT1[:], pb[0][:, :], P_UP[:], OP.mult),
                 reads=[PB[0], "P_UP", "NT1"], writes=["NT1"])
            tk.v(lambda v, pc=pc: v.tensor_tensor(Mqb[pc][:], pb[0][:, :], P_UQI[:], OP.mult),
                 reads=[PB[0], "P_UQI", f"Mqb{pc}"], writes=[f"Mqb{pc}"])
            tk.v(lambda v: v.tensor_tensor(FT1n[:], identp[:], NT1[:], OP.subtract),
                 reads=["identp", "NT1", "FT1n"], writes=["FT1n"])

        def _front_f2(c, pc, quad16, ppsum, kslice, i32b):
            tk.t(lambda t, k=kslice, i=i32b, q=quad16, o=ppsum(0): q(t, k, i, o),
                 reads=["qx%d" % _q for _q in range(8)] + ["I128b", PB[0]], writes=[PB[0]])
            tk.v(lambda v, pc=pc: v.tensor_scalar(KTb[pc][:], pb[0][:, :], lrevb[:], None, OP.mult),
                 reads=[PB[0], "lrevb", f"KTb{pc}"], writes=[f"KTb{pc}"])
            x0k = X0[pc][:].rearrange("p (a b) -> p a b", b=128)[:, :, 0:64]
            tk.v(lambda v, pc=pc, x0k=x0k: v.tensor_scalar(x0k, pb[0][:, :], lamcol[:], None, OP.mult),
                 reads=[PB[0], "lamcol", f"X0_{pc}"], writes=[f"X0_{pc}"])

        def _front_f3(c, pc, quad16, ppsum, sslice, i32f):
            tk.t(lambda t, s=sslice, i=i32f, q=quad16, o=ppsum(0): q(t, s, i, o),
                 reads=["stat", "I128f", PB[0]], writes=[PB[0]])
            x0v = X0[pc][:].rearrange("p (a b) -> p a b", b=128)[:, :, 64:128]
            tk.s(lambda s, pc=pc, x0v=x0v: s.copy(x0v, pb[0][:, :]),
                 reads=[PB[0], f"X0_{pc}"], writes=[f"X0_{pc}"])

        def d_solve_stages(c):
            pc = c % 2
            cc, quad16, pairs, ppsum, kslice, sslice, i32f, i32b = _acc(c)
            stages = []
            xseq = [(X0[pc], f"X0_{pc}"), (Xw[0], "Xw0"), (Xw[1], "Xw1"),
                    (Xw[0], "Xw0"), (Xw[1], "Xw1"), (Xw[0], "Xw0")]
            def app(t, ft, ftn, xi):
                xcur, xcn = xseq[xi]
                q = quad16; pr = pairs
                return q(t, pr(ft[:]), pr(xcur[:], 128),
                         lambda p, hh: pb[5 + p // 4][64 * hh:64 * hh + 64,
                                                      128 * (p % 4):128 * (p % 4) + 128])
            def stage_k(k):
                if k < 5:
                    ntp = NT1 if k == 0 else NTw[(k - 1) % 2]
                    ntpn = "NT1" if k == 0 else f"NTw{(k - 1) % 2}"
                    ncur = k % 2
                    tk.t(lambda t, ntp=ntp, ncur=ncur, q=quad16, pr=pairs, o=ppsum(3):
                         q(t, pr(ntp[:]), pr(Nb[ncur][:]), o),
                         reads=[ntpn, f"Nb{ncur}", PB[3]], writes=[PB[3]])
                    tk.t(lambda t, ntp=ntp, ncur=ncur, q=quad16, pr=pairs, o=ppsum(4):
                         q(t, pr(Nb[ncur][:]), pr(ntp[:]), o),
                         reads=[ntpn, f"Nb{ncur}", PB[4]], writes=[PB[4]])
                ft, ftn = (FT1n, "FT1n") if k == 0 else (FTS[k - 1], f"FTS{k - 1}")
                xcur, xcn = xseq[k]
                tk.t(lambda t, ft=ft, ftn=ftn, k=k: app(t, ft, ftn, k),
                     reads=[ftn, xcn, PB[5], PB[6]], writes=[PB[5], PB[6]])
                if k < 5:
                    if k < 4:
                        tk.v(lambda v, ncur=ncur: v.tensor_copy(Nb[1 - ncur][:], pb[3][:, :]),
                             reads=[PB[3], f"Nb{1 - ncur}"], writes=[f"Nb{1 - ncur}"])
                        tk.s(lambda s, k=k: s.copy(NTw[k % 2][:], pb[4][:, :]),
                             reads=[PB[4], f"NTw{k % 2}"], writes=[f"NTw{k % 2}"])
                    tk.v(lambda v, k=k: v.tensor_tensor(FTS[k][:], pb[4][:, :], identp[:], OP.add),
                         reads=[PB[4], "identp", f"FTS{k}"], writes=[f"FTS{k}"])
                    xnext, xnn = xseq[k + 1]
                    tk.v(lambda v, xnext=xnext: v.tensor_copy(xnext[:, 0:512], pb[5][:, :]),
                         reads=[PB[5], xnn], writes=[xnn])
                    tk.s(lambda s, xnext=xnext: s.copy(xnext[:, 512:1024], pb[6][:, :]),
                         reads=[PB[6], xnn], writes=[xnn])
                else:
                    for bi in range(2):
                        kview = pb[5 + bi][:].rearrange("p (a b) -> p a b", b=128)[:, :, 0:64]
                        vview = pb[5 + bi][:].rearrange("p (a b) -> p a b", b=128)[:, :, 64:128]
                        tk.v(lambda v, pc=pc, kview=kview, bi=bi: v.tensor_copy(
                            Bcb[pc][:, 256 * bi:256 * bi + 256], kview),
                             reads=[PB[5 + bi], f"Bcb{pc}"], writes=[f"Bcb{pc}"])
                        tk.s(lambda s, pc=pc, vview=vview, bi=bi: s.copy(
                            ULc[pc][:, 256 * bi:256 * bi + 256], vview),
                             reads=[PB[5 + bi], f"ULc{pc}"], writes=[f"ULc{pc}"])
            def stage_bbt():
                tk.t(lambda t, pc=pc, q=quad16, pr=pairs, i=i32b, o=ppsum(3):
                     q(t, pr(Bcb[pc][:]), i, o),
                     reads=[f"Bcb{pc}", "I128b", PB[3]], writes=[PB[3]])
                tk.v(lambda v, pc=pc: v.tensor_scalar(BBTn[pc][:], pb[3][:, :], -1.0, None, OP.mult),
                     reads=[PB[3], f"BBTn{pc}"], writes=[f"BBTn{pc}"])
            for k in range(6):
                stages.append(lambda k=k: stage_k(k))
            stages.append(stage_bbt)
            return stages

        def d_tail_pieces(c):
            pc = c % 2
            cc, quad16, pairs, ppsum, kslice, sslice, i32f, i32b = _acc(c)
            def T1():
                tk.t(lambda t, k=kslice, q=quad16, pr=pairs, o=ppsum(1):
                     q(t, k, pr(Sbf[:]), o),
                     reads=["qx%d" % _q for _q in range(8)] + ["Sbf", PB[1]], writes=[PB[1]])
                x0v = X0[pc][:].rearrange("p (a b) -> p a b", b=128)[:, :, 64:128]
                tk.v(lambda v, x0v=x0v: v.scalar_tensor_tensor(ypart[:], pb[1][:, :], lamcol[:],
                                                               x0v, OP.mult, OP.add),
                     reads=[PB[1], "lamcol", f"X0_{pc}", "ypart"], writes=["ypart"])
            def T2():
                def f_U(t, pc=pc):
                    t.matmul(pb[2][:, :], I128b[:], ULc[pc][:], start=True, stop=False)
                    last = None
                    for p in range(8):
                        for hh in range(2):
                            stop = (p == 7 and hh == 1)
                            last = t.matmul(pb[2][64 * hh:64 * hh + 64, 64 * p:64 * p + 64],
                                            BBTn[pc][64 * hh:64 * hh + 64, 64 * p:64 * p + 64],
                                            Sbf[64 * hh:64 * hh + 64, 64 * p:64 * p + 64],
                                            start=False, stop=stop,
                                            tile_position=(64 * hh, 64 * hh))
                    return last
                tk.t(f_U, reads=[f"ULc{pc}", f"BBTn{pc}", "Sbf", "I128b", PB[2]], writes=[PB[2]])
                tk.v(lambda v: v.tensor_copy(Ub[:], pb[2][:, :]), reads=[PB[2], "Ub"], writes=["Ub"])
            def T3():
                def f_dS(t, pc=pc):
                    last = t.matmul(pb[7][:, :], lCI[:], Sbf[:], start=True, stop=False)
                    for p in range(8):
                        for hh in range(2):
                            stop = (p == 7 and hh == 1)
                            last = t.matmul(pb[7][64 * hh:64 * hh + 64, 64 * p:64 * p + 64],
                                            KTb[pc][64 * hh:64 * hh + 64, 64 * p:64 * p + 64],
                                            Ub[64 * hh:64 * hh + 64, 64 * p:64 * p + 64],
                                            start=False, stop=stop,
                                            tile_position=(64 * hh, 64 * hh))
                    return last
                tk.t(f_dS, reads=["lCI", "Sbf", f"KTb{pc}", "Ub", PB[7]], writes=[PB[7]])
                tk.s(lambda s: s.copy(Sbf[:], pb[7][:, :]), reads=[PB[7], "Sbf"], writes=["Sbf"])
            def T4():
                tk.t(lambda t, pc=pc, q=quad16, pr=pairs, o=ppsum(1):
                     q(t, pr(Mqb[pc][:]), pr(Ub[:]), o),
                     reads=[f"Mqb{pc}", "Ub", PB[1], "ypart"], writes=[PB[1]])
                tk.v(lambda v: v.tensor_tensor(yv[:], ypart[:], pb[1][:, :], OP.add),
                     reads=["ypart", PB[1], "yv"], writes=["yv"])
            def T5():
                tk.v(lambda v: vred(v, tc0[:], yv[:], OP.add),
                     reads=["yv", "tc0"], writes=["tc0"])
                tk.s(lambda s: s.activation(sqb[:], yv[:], AF.Square), reads=["yv", "sqb"], writes=["sqb"])
                tk.v(lambda v: vred(v, tc1[:], sqb[:], OP.add),
                     reads=["sqb", "tc1"], writes=["tc1"])
                def f_sel(t):
                    t.matmul(pb[7][:, 0:1], seldup[:], tc0[:], start=True, stop=True)
                    return t.matmul(pb[7][:, 4:5], seldup[:], tc1[:], start=True, stop=True)
                tk.t(f_sel, reads=["seldup", "tc0", "tc1", PB[7]], writes=[PB[7]])
                tk.v(lambda v: v.tensor_scalar(muc[:], pb[7][:, 0:1], 1.0 / D, None, OP.mult),
                     reads=[PB[7], "muc"], writes=["muc"])
                tk.v(lambda v: v.tensor_scalar(tc1[:], pb[7][:, 4:5], 1.0 / D, None, OP.mult),
                     reads=[PB[7], "tc1"], writes=["tc1"])
                tk.v(lambda v: v.tensor_tensor(tc0[:], muc[:], muc[:], OP.mult),
                     reads=["muc", "tc0"], writes=["tc0"])
                tk.v(lambda v: v.tensor_tensor(tc1[:], tc1[:], tc0[:], OP.subtract),
                     reads=["tc0", "tc1"], writes=["tc1"])
                tk.v(lambda v: v.tensor_scalar(tc1[:], tc1[:], EPS, None, OP.add),
                     reads=["tc1"], writes=["tc1"])
                tk.v(lambda v: vrecip(v, tc0[:], tc1[:]), reads=["tc0", "tc1"], writes=["tc0"])
                tk.s(lambda s: s.activation(invc[:], tc0[:], AF.Sqrt), reads=["tc0", "invc"], writes=["invc"])
                tk.v(lambda v: v.tensor_scalar(muc[:], muc[:], -1.0, None, OP.mult),
                     reads=["muc"], writes=["muc"])
                tk.v(lambda v: v.tensor_scalar(yv[:], yv[:], muc[:], invc[:], OP.add, OP.mult),
                     reads=["yv", "muc", "invc"], writes=["yv"])
                tk.v(lambda v: v.tensor_tensor(yv[:], yv[:], gdl[:], OP.mult),
                     reads=["yv", "gdl"], writes=["yv"])
                tk.v(lambda v: v.tensor_tensor(ao[:], yv[:], bdl[:], OP.add),
                     reads=["yv", "bdl"], writes=["yv"])
            def T6():
                tk.t(lambda t, i=i32f, q=quad16, pr=pairs, o=ppsum(7): q(t, pr(ao[:]), i, o),
                     reads=["yv", "I128f", PB[7]], writes=[PB[7]])
                xrv = xr3[:, :, cc]
                pbv = pb[7][:].rearrange("p (a b) -> p a b", b=64)
                xrall = ["xr%d" % _j for _j in range(8)]
                tk.v(lambda v, xrv=xrv, pbv=pbv: v.tensor_tensor(xrv, xrv, pbv, OP.add),
                     reads=[PB[7]] + xrall, writes=xrall)
            return [T1, T2, T3, T4, T5, T6]

        def xload(seq, t0):
            for h in range(2):
                def f(g, sem, seq=seq, t0=t0, h=h):
                    for j in range(4 * h, 4 * h + 4):
                        g.dma_start(xr3[:, j, :],
                                    dx[seq][128 * j:128 * j + 128, t0:t0 + T]
                                    ).then_inc(sem, 16)
                tk.dg(f, reads=[f"xr{j}" for j in range(4 * h, 4 * h + 4)],
                      writes=[f"xr{j}" for j in range(4 * h, 4 * h + 4)], ndma=4)

        def xstore(seq, t0):
            for h in range(2):
                def f(g, sem, seq=seq, t0=t0, h=h):
                    for j in range(4 * h, 4 * h + 4):
                        g.dma_start(dout[seq][128 * j:128 * j + 128, t0:t0 + T],
                                    xr3[:, j, :]).then_inc(sem, 16)
                tk.dg(f, reads=[f"xr{j}" for j in range(4 * h, 4 * h + 4)], ndma=4)

        xload(0, 0)
        if do_f1:
            ffn_front()
        prep_all()

        for seq in range(n_seq):
            tk.v(lambda v: v.memset(Sbf[:], 0.0), reads=["Sbf"], writes=["Sbf"])
            for ti in range(n_tiles):
                t0 = T * ti
                if not (seq == 0 and ti == 0):
                    xload(seq, t0)
                if do_f1:
                    ffn(skip_front=(seq == 0 and ti == 0))
                if do_attn:
                    attn()
                if do_f2:
                    ffn()
                xstore(seq, t0)

        TINY = frozenset(["muc", "invc", "tc0", "tc1", "cA", "cB", "cC", "cD",
                          "cE", "mxc", "mnc", "muc4", "ivc4", "acc_col",
                          "red_col", "swcol", "dqcol", "etacol", "zcol",
                          "betacol", "nlamcol", "lamcol", "lrevb", "lCcol",
                          "escal", "zscal", "loglam"])
        tk.finalize(tiny=TINY)
        tk.emit(nc, sems)
    return nc


_NC_CACHE = {}

def _consts():
    I128 = np.eye(128, dtype=np.float32)
    ONESROW = np.ones((1, 128), np.float32)
    ONE1 = np.ones((1, 1), np.float32)
    ONESCOL = np.ones((128, 1), np.float32)
    q = np.arange(128)
    SELDUP = ((q[:, None] % 64) == (q[None, :] % 64)).astype(np.float32)
    pm = np.arange(128) % 64
    fm = np.arange(512) % 64
    PH, FH = np.meshgrid(pm, fm, indexing="ij")
    DL = np.where(PH > FH, (PH - FH), 0).astype(np.float32)
    ML = (PH > FH).astype(np.float32)
    DU = np.where(FH > PH, (FH - PH), 0).astype(np.float32)
    MU = (FH > PH).astype(np.float32)
    DQ = np.where(FH >= PH, (FH - PH), 0).astype(np.float32)
    MQ = (FH >= PH).astype(np.float32)
    IOT = (np.arange(128) % 64 + 1).astype(np.float32).reshape(128, 1)
    REV = (63 - np.arange(128) % 64).astype(np.float32).reshape(128, 1)
    CC = np.full((128, 1), 64.0, np.float32)
    IDP = ((np.arange(128) % 64)[:, None] == (np.arange(512) % 64)[None, :]).astype(np.float32)
    return dict(I128=I128, ONESROW=ONESROW, ONE1=ONE1, ONESCOL=ONESCOL,
                SELDUP=SELDUP, DLm=DL, MLm=ML, DUm=DU, MUm=MU, DQm=DQ, MQm=MQ,
                IOT=IOT, REV=REV, CC=CC, IDP=IDP)


def make_inputs(L0_attn_x, L0_attn_W, L0_attn_ln_gamma, L0_attn_ln_beta,
                L0_attn_eta, L0_attn_lam_logit, L0_ffn_w_gate, L0_ffn_w_up,
                L0_ffn_w_down):
    x = np.ascontiguousarray(np.asarray(L0_attn_x, np.float32).transpose(0, 2, 1))
    wgT = np.ascontiguousarray(np.asarray(L0_ffn_w_gate, np.float32).T)
    wuT = np.ascontiguousarray(np.asarray(L0_ffn_w_up, np.float32).T)
    wdT = np.ascontiguousarray(np.asarray(L0_ffn_w_down, np.float32).T)
    WT = np.ascontiguousarray(np.asarray(L0_attn_W, np.float32).T)
    gamma = np.asarray(L0_attn_ln_gamma, np.float32)
    betal = np.asarray(L0_attn_ln_beta, np.float32)
    g3 = gamma.reshape(8, 2, 64)
    b3 = betal.reshape(8, 2, 64)
    gdl = np.concatenate([
        np.broadcast_to(g3[:, 0, :].reshape(1, 512), (64, 512)),
        np.broadcast_to(g3[:, 1, :].reshape(1, 512), (64, 512))], 0).astype(np.float32)
    bdl = np.concatenate([
        np.broadcast_to(b3[:, 0, :].reshape(1, 512), (64, 512)),
        np.broadcast_to(b3[:, 1, :].reshape(1, 512), (64, 512))], 0).astype(np.float32)
    eta1 = np.asarray(L0_attn_eta, np.float32).reshape(1, 1)
    lamz = np.asarray(L0_attn_lam_logit, np.float32).reshape(1, 1)
    base = dict(wgT=wgT, wuT=wuT, wdT=wdT, WT=WT, gdl=gdl, bdl=bdl,
                eta1=eta1, lamz=lamz, **_consts())
    return x, base


def kernel(**inputs):
    x, base = make_inputs(**inputs)
    if "nc" not in _NC_CACHE:
        _NC_CACHE["nc"] = build_nc()
    nc = _NC_CACHE["nc"]
    in_maps = [dict(base, x=np.ascontiguousarray(x[2 * c:2 * c + 2]))
               for c in range(NCORES)]
    res = run_bass_kernel_spmd(nc, in_maps, core_ids=list(range(NCORES)))
    out = np.concatenate([r["out"] for r in res.results], 0)
    return np.ascontiguousarray(out.transpose(0, 2, 1))
